# version 15
# baseline (speedup 1.0000x reference)
import numpy as np
import concourse.bass as bass
import concourse.mybir as mybir
from concourse.bass_utils import run_bass_kernel_spmd
from contextlib import ExitStack

F32 = mybir.dt.float32
BF16 = mybir.dt.bfloat16
AF = mybir.ActivationFunctionType
ALU = mybir.AluOpType

D = 1024
DCN = 8
E = 8
NEXP = 9
H = 2048
NHC = 16
NST = 16
T = 2048
TC = 512
NCH = 4
NTT = 4
EPS = 1e-5

ENGINES = ("sync", "scalar", "vector", "tensor", "gpsimd")
DMA_Q = {"sync": "qs", "scalar": "qa", "gpsimd": "qg", "vector": "qv"}
NLANES = 8
EOALL = ("eoA", "eoX")


class Sched:

    def __init__(self):
        self.items = []
        self.writers = {}
        self.readers = {}

    def op(self, engine, fn, reads=(), writes=(), dma=False, tag=""):
        idx = len(self.items)
        deps = set()
        for r in reads:
            deps.update(self.writers.get(r, ()))
        for r in writes:
            deps.update(self.writers.get(r, ()))
            deps.update(self.readers.get(r, ()))
        self.items.append(dict(engine=engine, fn=fn, deps=deps, dma=dma,
                               tag=tag, needs_inc=False))
        for r in reads:
            self.readers.setdefault(r, []).append(idx)
        for r in writes:
            self.writers[r] = [idx]
            self.readers[r] = []
        return idx

    @staticmethod
    def _queue(it):
        return DMA_Q[it["engine"]] if it["dma"] else None

    @staticmethod
    def _chan(it):
        if it["dma"]:
            return DMA_Q[it["engine"]] + str(it.get("lane", 0))
        return it["engine"]

    MARGIN = 24
    SYNC_CHANS = ("vector", "scalar", "gpsimd")

    def prepare(self):
        qseq = {}
        lane_prev = {}
        for i, it in enumerate(self.items):
            q = self._queue(it)
            if q is not None:
                k = qseq.get(q, 0)
                qseq[q] = k + 1
                it["lane"] = k % NLANES
                key = (q, it["lane"])
                if key in lane_prev:
                    it["deps"].add(lane_prev[key])
                lane_prev[key] = i
        seq = {}
        for it in self.items:
            c = self._chan(it)
            it["cseq"] = seq.get(c, 0)
            seq[c] = it["cseq"] + 1
        for it in self.items:
            for d in it["deps"]:
                dit = self.items[d]
                c = self._chan(dit)
                if c != self._chan(it):
                    dit["needs_inc"] = True
                elif (c in self.SYNC_CHANS and not dit["dma"] and not it["dma"]
                      and it["cseq"] - dit["cseq"] <= self.MARGIN):
                    dit["needs_inc"] = True
                    it.setdefault("close_deps", []).append(d)
        for it in self.items:
            if it["dma"]:
                it["needs_inc"] = True
        counters = {}
        for it in self.items:
            c = self._chan(it)
            if it["needs_inc"]:
                counters[c] = counters.get(c, 0) + 1
                it["inc_val"] = counters[c] * (16 if it["dma"] else 1)
            else:
                it["inc_val"] = None

    def emit_engine(self, ename, eng, sems):
        hiwater = {}
        nwaits = 0
        for it in self.items:
            if it["engine"] != ename:
                continue
            mychan = self._chan(it)
            want = {}
            close = set(it.get("close_deps", ()))
            for d in it["deps"]:
                dit = self.items[d]
                c = self._chan(dit)
                if (c == mychan and not it["dma"] and not dit["dma"]
                        and d not in close):
                    continue
                if dit["inc_val"] is None:
                    raise RuntimeError(f"dep without inc: {dit['tag']}")
                want[c] = max(want.get(c, 0), dit["inc_val"])
            for c, v in sorted(want.items()):
                if hiwater.get(c, 0) >= v:
                    continue
                eng.wait_ge(sems[c], v)
                hiwater[c] = v
                nwaits += 1
            inst = it["fn"](eng)
            if it["inc_val"] is not None:
                if inst is None:
                    raise RuntimeError(f"needs inc, no inst: {it['tag']}")
                inst.then_inc(sems[mychan], 16 if it["dma"] else 1)
        return nwaits


def build_program():
    nc = bass.Bass("TRN2", target_bir_lowering=False)

    def din(name, shape, dt=F32):
        return nc.dram_tensor(name, shape, dt, kind="ExternalInput").ap()

    x_main = din("x_main", [T, D])
    x_halo = din("x_halo", [3, D])
    w_inproj = din("w_inproj", [2 * D, D])
    w_convw = din("w_convw", [D, 4])
    w_convb = din("w_convb", [D])
    w_xproj = din("w_xproj", [33, D])
    w_alog = din("w_alog", [D, NST])
    w_dpar = din("w_dpar", [D])
    w_outproj = din("w_outproj", [D, D])
    w_router = din("w_router", [E, D])
    w_ebias = din("w_ebias", [E])
    w_w1 = din("w_w1", [E, D, H])
    w_w3 = din("w_w3", [E, D, H])
    w_w2 = din("w_w2", [E, H, D])
    w_sw1 = din("w_sw1", [1, D, H])
    w_sw3 = din("w_sw3", [1, D, H])
    w_sw2 = din("w_sw2", [1, H, D])
    eye_in = din("eye_in", [128, 128])
    masks_in = din("masks_in", [128, 2])
    out_d = nc.dram_tensor("out", [T, D], F32, kind="ExternalOutput").ap()

    w1bf_d = nc.dram_tensor("w1bf_d", [NEXP, 128, DCN, H], BF16).ap()
    w3bf_d = nc.dram_tensor("w3bf_d", [NEXP, 128, DCN, H], BF16).ap()
    w2bf_d = nc.dram_tensor("w2bf_d", [NEXP, 8, 128, NHC, 128], BF16).ap()
    iprojT_d = nc.dram_tensor("iprojT_d", [128, DCN, 2 * D], BF16).ap()
    oprojT_d = nc.dram_tensor("oprojT_d", [128, DCN, D], BF16).ap()
    xres_d = nc.dram_tensor("xres_d", [NCH, 128, DCN, TC], F32).ap()
    szbf_d = nc.dram_tensor("szbf_d", [NCH, 128, DCN, TC], BF16).ap()
    rows_d = nc.dram_tensor("rows_d", [NCH, 34, TC], F32).ap()
    lnrows_d = nc.dram_tensor("lnrows_d", [2, TC], F32).ap()
    cwT_d = nc.dram_tensor("cwT_d", [E, T], F32).ap()
    cc_in_d = nc.dram_tensor("cc_in_d", [128, 128], F32).ap()
    cc_out_d = nc.dram_tensor("cc_out_d", [128, 128], F32).ap()
    hrows_d = nc.dram_tensor("hrows_d", [2, TC], F32).ap()
    cw3_d = nc.dram_tensor("cw3_d", [E, 4], F32).ap()

    sched = Sched()

    with ExitStack() as ctx:
        def sb(name, shape, dt=F32):
            return ctx.enter_context(nc.sbuf_tensor(name, shape, dt))

        xres = sb("xres", [128, 2, DCN, TC])
        xssm = sb("xssm", [128, DCN, 515])
        xconv = sb("xconv", [128, DCN, TC])
        yacc = sb("yacc", [128, DCN, TC])
        xnorm = sb("xnorm", [128, DCN, TC], BF16)
        xnm = sb("xnm", [128, DCN, TC], BF16)
        eo = sb("eo", [128, NHC, TC], BF16)
        szbf_w = sb("szbf_w", [128, DCN, TC], BF16)
        sg = sb("sg", [128, 2, TC])
        sge = sb("sge", [128, TC])
        w1b = sb("w1b", [128, 2, DCN, 256], BF16)
        w3b = sb("w3b", [128, 2, DCN, 256], BF16)
        w2b = sb("w2b", [128, 2, NHC, 128], BF16)
        ipb = sb("ipb", [128, 2, DCN, 128], BF16)
        opb = sb("opb", [128, 2, DCN, 128], BF16)
        dAt = sb("dAt", [128, 2, TC])
        ut = sb("ut", [128, TC])
        ut2 = sb("ut2", [128, TC])
        hst = sb("hst", [128, TC])
        bc = sb("bc", [128, 3, TC])
        cwb = sb("cwb", [128, 2, TC])
        scb = sb("scb", [128, 2, TC])
        scanrows = sb("scanrows", [48, TC])
        m_row = sb("m_row", [1, TC])
        q_row = sb("q_row", [1, TC])
        mr_row = sb("mr_row", [1, TC])
        drow = sb("drow", [1, TC])
        srow = q_row
        scar = sb("scar", [1, 2])
        ones_c = sb("ones_c", [128, 1])
        ones_bf = sb("ones_bf", [1, 128], BF16)
        hstate = sb("hstate", [128, DCN, NST])
        hin = sb("hin", [128, DCN, NST])
        ccs = sb("ccs", [128, 128])
        A_sb = sb("A_sb", [128, DCN, NST])
        convw = sb("convw", [128, DCN, 4])
        convb = sb("convb", [128, DCN])
        D_sb = sb("D_sb", [128, DCN])
        masks = sb("masks", [128, 2])
        cvals = sb("cvals", [128, 2])
        eye = sb("eye", [128, 128])
        eye_bf = sb("eye_bf", [128, 128], BF16)
        xprojT = sb("xprojT", [128, DCN, 80], BF16)
        routerT = sb("routerT", [128, DCN, E], BF16)
        ebias_r = sb("ebias_r", [1, E])
        ebias_bf = sb("ebias_bf", [1, E], BF16)
        cw_tm = sb("cw_tm", [128, 4, E])
        cwT_sb = sb("cwT_sb", [8, TC])
        tmp_r = sb("tmp_r", [128, 4, E])
        tmp_r2 = sb("tmp_r2", [128, 4, E])
        m1c = sb("m1c", [128, 4])
        m2c = sb("m2c", [128, 4])
        s1c = sb("s1c", [128, 4])
        s2c = sb("s2c", [128, 4])
        x3f = sb("x3f", [128, DCN, 4])
        x3n = sb("x3n", [128, DCN, 4], BF16)
        eo3 = sb("eo3", [128, NHC, 4], BF16)
        sg3 = sb("sg3", [128, 4])
        y3 = sb("y3", [128, DCN, 4])
        cw3_tm = sb("cw3_tm", [3, E])
        cw3T = sb("cw3T", [8, 4])
        cwb3 = sb("cwb3", [128, 4])
        tail3 = sb("tail3", [128, DCN, 3])

        eo_f = eo.ap().bitcast(F32)
        eo_flat = eo_f.rearrange("p a b -> p (a b)")
        eo_bflat = eo.ap().rearrange("p a b -> p (a b)")
        xs_flat = xssm.ap().rearrange("p a b -> p (a b)")
        xs_bflat = xssm.ap().bitcast(BF16).rearrange("p a b -> p (a b)")
        xc_flat = xconv.ap().rearrange("p a b -> p (a b)")
        xc_bflat = xconv.ap().bitcast(BF16).rearrange("p a b -> p (a b)")
        ya_flat = yacc.ap().rearrange("p a b -> p (a b)")
        ya_bflat = yacc.ap().bitcast(BF16).rearrange("p a b -> p (a b)")
        sz_bflat = szbf_w.ap().rearrange("p a b -> p (a b)")
        xn_bflat = xnorm.ap().rearrange("p a b -> p (a b)")
        band_f32 = [sz_bflat.bitcast(F32), xs_flat[:, 0:2048],
                    xc_flat[:, 0:2048], ya_flat[:, 0:2048]]
        band_bf = [xn_bflat[:, 0:2048], xs_bflat[:, 4096:6144],
                   xc_bflat[:, 4096:6144], ya_bflat[:, 4096:6144]]
        band_tag = [["szbf_w", "xnm"], ["xssm"], ["xconv"], ["yacc"]]
        xconv_bf = ya_bflat[:, 0:4096].rearrange("p (dc t) -> p dc t", dc=DCN)
        xtm = eo_flat[:, 3072:4096]
        xproj_sb = eo_flat[0:33, 0:1024]
        router_sb = eo_flat[0:8, 1024:2048]
        x3 = eo_flat[0:3, 0:1024]
        outstg = xs_flat[:, 0:2048].rearrange(
            "p (b x) -> p b x", b=2)

        pg = [ctx.enter_context(nc.psum_tensor(f"pg{i}", [128, TC], F32))
              for i in range(4)]
        pw = [ctx.enter_context(nc.psum_tensor(f"pw{i}", [128, TC], F32))
              for i in range(2)]
        pstat = ctx.enter_context(nc.psum_tensor("pstat", [128, TC], F32))
        pbf = ctx.enter_context(nc.psum_tensor("pbf", [128, 1024], BF16))

        semdict = {}
        for s in ("sync", "scalar", "vector", "tensor", "gpsimd"):
            semdict[s] = ctx.enter_context(nc.semaphore("sem_" + s))
        for q in ("qs", "qa", "qg", "qv"):
            for ln in range(NLANES):
                semdict[q + str(ln)] = ctx.enter_context(
                    nc.semaphore(f"sem_{q}{ln}"))

        block = ctx.enter_context(nc.Block())
        OP = sched.op

        def dma(qeng, out, in_, reads, writes, tag="", slow=False):
            if slow:
                OP(qeng, lambda e, o=out, i=in_: e.dma_start(
                    out=o, in_=i, allow_slow_non_contiguous=True),
                   reads=reads, writes=writes, dma=True, tag=tag)
            else:
                OP(qeng, lambda e, o=out, i=in_: e.dma_start(out=o, in_=i),
                   reads=reads, writes=writes, dma=True, tag=tag)

        dma("sync", eye[:], eye_in, [], ["eye"], "eye")
        dma("sync", masks[:], masks_in, [], ["masks"], "masks")
        dma("sync", A_sb[:], w_alog.rearrange("(dc dp) n -> dp dc n", dp=128),
            [], ["A"], "alog")
        dma("sync", convw[:], w_convw.rearrange("(dc dp) k -> dp dc k", dp=128),
            [], ["convw"], "convw")
        dma("sync", convb[:], w_convb.rearrange("(dc dp) -> dp dc", dp=128),
            [], ["convb"], "convb", slow=True)
        dma("sync", D_sb[:], w_dpar.rearrange("(dc dp) -> dp dc", dp=128),
            [], ["Dsb"], "dpar", slow=True)
        dma("sync", ebias_r[:], w_ebias.rearrange("(o e) -> o e", o=1), [], ["ebias"], "eb")
        OP("scalar", lambda e: e.activation(out=A_sb[:], in_=A_sb[:], func=AF.Exp),
           ["A"], ["A"], tag="a_exp")
        OP("scalar", lambda e: e.mul(out=A_sb[:], in_=A_sb[:], mul=-1.0),
           ["A"], ["A"], tag="a_neg")
        OP("scalar", lambda e: e.copy(out=ebias_bf[:], in_=ebias_r[:]),
           ["ebias"], ["ebias_bf"], tag="ebbf")
        OP("scalar", lambda e: e.copy(out=eye_bf[:], in_=eye[:]),
           ["eye"], ["eye_bf"], tag="eyebf")
        OP("vector", lambda e: e.memset(ones_c[:], 1.0), [], ["ones_c"], tag="o2")
        OP("vector", lambda e: e.memset(ones_bf[:], 1.0), [], ["ones_bf"], tag="o3")
        OP("vector", lambda e: e.memset(scar[:], 0.0), [], ["scar"], tag="o4")
        OP("vector", lambda e: e.memset(hstate[:], 0.0), [], ["hstate"], tag="o5")
        OP("vector", lambda e: e.memset(cvals[:, 0:1], EPS), [], ["cvals"], tag="o6")
        OP("vector", lambda e: e.memset(cvals[:, 1:2], 0.5), [], ["cvals"], tag="o7")

        OP("vector", lambda e: e.memset(xprojT[:], 0.0), [], ["xprojT"], tag="xp0")
        dma("sync", xproj_sb, w_xproj, ["eoA"], ["eoA"], "xpl")
        for dc in range(DCN):
            OP("tensor", lambda e, dc=dc: e.transpose(
                out=pstat[:, 0:33], in_=xproj_sb[:, dc * 128:(dc + 1) * 128],
                identity=eye[0:33, 0:33]),
               ["eoA", "eye", "pstat"], ["pstat"], tag=f"xpT{dc}")
            OP("vector", lambda e, dc=dc: e.tensor_copy(
                out=xprojT[:, dc, 0:16], in_=pstat[:, 0:16]),
               ["pstat", "xprojT"], ["xprojT"], tag=f"xpE1{dc}")
            OP("vector", lambda e, dc=dc: e.tensor_copy(
                out=xprojT[:, dc, 32:48], in_=pstat[:, 16:32]),
               ["pstat", "xprojT"], ["xprojT"], tag=f"xpE2{dc}")
            OP("vector", lambda e, dc=dc: e.tensor_copy(
                out=xprojT[:, dc, 64:65], in_=pstat[:, 32:33]),
               ["pstat", "xprojT"], ["xprojT"], tag=f"xpE3{dc}")
        dma("sync", router_sb, w_router, ["eoA"], ["eoA"], "rtl")
        for dc in range(DCN):
            OP("tensor", lambda e, dc=dc: e.transpose(
                out=pstat[:, 40:48], in_=router_sb[:, dc * 128:(dc + 1) * 128],
                identity=eye[0:8, 0:8]),
               ["eoA", "eye", "pstat"], ["pstat"], tag=f"rtT{dc}")
            OP("vector", lambda e, dc=dc: e.tensor_copy(
                out=routerT[:, dc, :], in_=pstat[:, 40:48]),
               ["pstat", "routerT"], ["routerT"], tag=f"rtE{dc}")

        for c in range(NCH):
            s = c % 2
            for tt in range(NTT):
                trow = c * TC + tt * 128
                dma("sync", xtm, x_main[trow:trow + 128, :],
                    ["eoX"], ["eoX"], f"xl{c}{tt}")
                for dc in range(DCN):
                    OP("tensor", lambda e, dc=dc: e.transpose(
                        out=pstat[:, 128:256], in_=xtm[:, dc * 128:(dc + 1) * 128],
                        identity=eye[:]),
                       ["eoX", "eye", "pstat"], ["pstat"], tag=f"xT{c}{tt}{dc}")
                    OP("vector", lambda e, s=s, tt=tt, dc=dc: e.tensor_copy(
                        out=xres[:, s, dc, tt * 128:(tt + 1) * 128],
                        in_=pstat[:, 128:256]),
                       ["pstat", f"xres{s}"], [f"xres{s}"], tag=f"xE{c}{tt}{dc}")
            dma("gpsimd", xres_d[c], xres[:, s], [f"xres{s}"],
                [f"xres{s}", f"xrd{c}"], f"xs{c}")

        band_i = [0]

        def band_unit(load_src, nrows, ncols_f32, store_fn, dres, ename):
            b = band_i[0] % 4
            band_i[0] += 1
            tags = band_tag[b]
            qload = "sync" if b % 2 == 0 else "gpsimd"
            fsrc = band_f32[b][0:nrows, 0:ncols_f32]
            bsrc = band_bf[b][0:nrows, 0:ncols_f32]
            dma(qload, fsrc, load_src, tags, tags, f"{ename}l")
            OP("scalar", lambda e, fsrc=fsrc, bsrc=bsrc: e.copy(
                out=bsrc, in_=fsrc), tags, tags, tag=f"{ename}c")
            store_fn(bsrc, tags)

        def prep_ffn(wsrc, dst, dres, ename):
            for dc in range(DCN):
                def store(bsrc, tags, dres=dres, dc=dc, ename=ename):
                    dma("scalar", dst[:, dc:dc + 1, :], bsrc.unsqueeze(1),
                        tags, list(tags) + [dres], f"{ename}s{dc}")
                band_unit(wsrc[dc * 128:(dc + 1) * 128, :], 128, H,
                          store, dres, f"{ename}{dc}")

        def prep_w2(eidx, wsrc, dres, ename):
            for hc in range(NHC):
                def store(bsrc, tags, dres=dres, hc=hc, ename=ename):
                    dma("scalar",
                        w2bf_d[:, :, :, hc, :][eidx].transpose([1, 0, 2]),
                        bsrc.rearrange("p (dp ds) -> p dp ds", dp=8),
                        tags, list(tags) + [dres], f"{ename}s{hc}")
                band_unit(wsrc[hc * 128:(hc + 1) * 128, :], 128, D,
                          store, dres, f"{ename}{hc}")

        def prep_expert(ei):
            if ei < E:
                prep_ffn(w_w1[ei], w1bf_d[ei], f"w1d{ei}", f"pw1e{ei}")
                prep_ffn(w_w3[ei], w3bf_d[ei], f"w3d{ei}", f"pw3e{ei}")
                prep_w2(ei, w_w2[ei], f"w2d{ei}", f"pw2e{ei}")
            else:
                prep_ffn(w_sw1[0], w1bf_d[E], f"w1d{E}", "psw1")
                prep_ffn(w_sw3[0], w3bf_d[E], f"w3d{E}", "psw3")
                prep_w2(E, w_sw2[0], f"w2d{E}", "psw2")

        prep_expert(0)

        def prep_transposed(wsrc, n_rows, dst, dres, ename):
            for ob in range(n_rows // 128):
                b = band_i[0] % 4
                band_i[0] += 1
                tags = band_tag[b]
                qload = "sync" if b % 2 == 0 else "gpsimd"
                fsrc = band_f32[b][:, 0:1024]
                bsrc = band_bf[b][:, 0:1024]
                btmp = band_bf[b][:, 1024:2048]
                dma(qload, fsrc, wsrc[ob * 128:(ob + 1) * 128, :],
                    tags, tags, f"{ename}l{ob}")
                OP("scalar", lambda e, fsrc=fsrc, bsrc=bsrc: e.copy(
                    out=bsrc, in_=fsrc), tags, tags, tag=f"{ename}c{ob}")
                for dc in range(DCN):
                    OP("tensor", lambda e, dc=dc, bsrc=bsrc: e.transpose(
                        out=pbf[:, (dc % 8) * 128:(dc % 8) * 128 + 128],
                        in_=bsrc[:, dc * 128:(dc + 1) * 128],
                        identity=eye_bf[:]),
                       list(tags) + ["eye_bf", "pbf"], ["pbf"],
                       tag=f"{ename}t{ob}{dc}")
                OP("vector", lambda e, btmp=btmp: e.tensor_copy(
                    out=btmp, in_=pbf[:, 0:1024]),
                   ["pbf"] + list(tags), tags, tag=f"{ename}e{ob}")
                dma("scalar", dst[:, :, ob * 128:(ob + 1) * 128],
                    btmp.rearrange("p (dc x) -> p dc x", dc=DCN),
                    tags, list(tags) + [dres], f"{ename}s{ob}")

        prep_transposed(w_inproj, 2 * D, iprojT_d, "ipd", "pip")
        prep_transposed(w_outproj, D, oprojT_d, "opd", "pop")
        for _pe in range(1, NEXP):
            prep_expert(_pe)

        def ln_chunk(c, nametag, outbuf=None, outres="xnorm"):
            if outbuf is None:
                outbuf = xnorm
            s = c % 2
            xr = f"xres{s}"
            for dc in range(DCN):
                OP("tensor", lambda e, dc=dc, s=s: e.matmul(
                    out=pstat[0:1, :], lhsT=ones_c[:], rhs=xres[:, s, dc, :],
                    start=(dc == 0), stop=(dc == DCN - 1)),
                   [xr, "ones_c", "pstat"], ["pstat"], tag=f"{nametag}sm{dc}")
            for dc in range(DCN):
                sb_i = dc % 2
                OP("vector", lambda e, dc=dc, s=s, sb_i=sb_i: e.tensor_tensor(
                    out=sg[:, sb_i, :], in0=xres[:, s, dc, :],
                    in1=xres[:, s, dc, :], op=ALU.mult),
                   [xr, f"sg{sb_i}"], [f"sg{sb_i}"], tag=f"{nametag}sq{dc}")
                OP("tensor", lambda e, dc=dc, sb_i=sb_i: e.matmul(
                    out=pw[0][0:1, :], lhsT=ones_c[:], rhs=sg[:, sb_i, :],
                    start=(dc == 0), stop=(dc == DCN - 1)),
                   [f"sg{sb_i}", "ones_c", "pw0"], ["pw0"], tag=f"{nametag}sk{dc}")
            OP("scalar", lambda e: e.activation(out=m_row[:], in_=pstat[0:1, :],
                                                func=AF.Copy, scale=1.0 / D),
               ["pstat", "m_row"], ["m_row"], tag=f"{nametag}m")
            OP("scalar", lambda e: e.activation(out=q_row[:], in_=pw[0][0:1, :],
                                                func=AF.Copy, scale=1.0 / D),
               ["pw0", "q_row"], ["q_row"], tag=f"{nametag}q")
            OP("vector", lambda e: e.tensor_tensor(
                out=mr_row[:], in0=m_row[:], in1=m_row[:], op=ALU.mult),
               ["m_row", "mr_row"], ["mr_row"], tag=f"{nametag}mm")
            OP("vector", lambda e: e.tensor_tensor(
                out=q_row[:], in0=q_row[:], in1=mr_row[:], op=ALU.subtract),
               ["q_row", "mr_row"], ["q_row"], tag=f"{nametag}v")
            OP("scalar", lambda e: e.activation(out=q_row[:], in_=q_row[:],
                                                func=AF.Ln, bias=cvals[0:1, 0:1]),
               ["q_row", "cvals"], ["q_row"], tag=f"{nametag}ln")
            OP("scalar", lambda e: e.activation(out=q_row[:], in_=q_row[:],
                                                func=AF.Exp, scale=-0.5),
               ["q_row"], ["q_row"], tag=f"{nametag}ex")
            OP("vector", lambda e: e.tensor_tensor(
                out=mr_row[:], in0=m_row[:], in1=q_row[:], op=ALU.mult),
               ["m_row", "q_row", "mr_row"], ["mr_row"], tag=f"{nametag}mr")
            dma("gpsimd", lnrows_d[0:1, :], q_row[:], ["q_row"], ["lnr_d"],
                f"{nametag}wr")
            dma("gpsimd", lnrows_d[1:2, :], mr_row[:], ["mr_row"], ["lnr_d"],
                f"{nametag}wm")
            dma("gpsimd", bc[:, 0, :], lnrows_d[0:1, :].partition_broadcast(128),
                ["lnr_d", "bc0"], ["bc0"], f"{nametag}br")
            dma("gpsimd", bc[:, 1, :], lnrows_d[1:2, :].partition_broadcast(128),
                ["lnr_d", "bc1"], ["bc1"], f"{nametag}bm")
            for dc in range(DCN):
                OP("vector", lambda e, dc=dc, s=s: e.tensor_tensor(
                    out=sge[:], in0=xres[:, s, dc, :], in1=bc[:, 0, :],
                    op=ALU.mult),
                   [xr, "bc0", "sge"], ["sge"], tag=f"{nametag}n1{dc}")
                OP("vector", lambda e, dc=dc, outbuf=outbuf: e.tensor_tensor(
                    out=outbuf[:, dc, :], in0=sge[:], in1=bc[:, 1, :],
                    op=ALU.subtract),
                   ["sge", "bc1", outres], [outres], tag=f"{nametag}n2{dc}")

        def router_chunk(c, nametag):
            for tt in range(NTT):
                for dc in range(DCN):
                    OP("tensor", lambda e, tt=tt, dc=dc: e.matmul(
                        out=pstat[:, tt * 8:tt * 8 + 8],
                        lhsT=xnorm[:, dc, tt * 128:(tt + 1) * 128],
                        rhs=routerT[:, dc, :], start=(dc == 0), stop=False),
                       ["xnorm", "routerT", "pstat"], ["pstat"],
                       tag=f"{nametag}lg{tt}{dc}")
                OP("tensor", lambda e, tt=tt: e.matmul(
                    out=pstat[:, tt * 8:tt * 8 + 8], lhsT=ones_bf[:],
                    rhs=ebias_bf[:], start=False, stop=True),
                   ["ones_bf", "ebias_bf", "pstat"], ["pstat"],
                   tag=f"{nametag}lb{tt}")
            OP("vector", lambda e: e.tensor_copy(
                out=cw_tm.ap().rearrange("p a e -> p (a e)"), in_=pstat[:, 0:32]),
               ["pstat", "cw_tm"], ["cw_tm"], tag=f"{nametag}cp")
            OP("vector", lambda e: e.tensor_reduce(
                out=m1c[:], in_=cw_tm[:], axis=mybir.AxisListType.X, op=ALU.max),
               ["cw_tm", "m1c"], ["m1c"], tag=f"{nametag}m1")
            for a in range(4):
                OP("vector", lambda e, a=a: e.tensor_scalar(
                    out=tmp_r[:, a, :], in0=cw_tm[:, a, :], scalar1=m1c[:, a:a + 1],
                    scalar2=None, op0=ALU.is_ge),
                   ["cw_tm", "m1c", "tmp_r"], ["tmp_r"], tag=f"{nametag}e1{a}")
            OP("vector", lambda e: e.scalar_tensor_tensor(
                out=tmp_r2.ap().rearrange("p a e -> p (a e)"),
                in0=tmp_r.ap().rearrange("p a e -> p (a e)"), scalar=-1e30,
                in1=cw_tm.ap().rearrange("p a e -> p (a e)"),
                op0=ALU.mult, op1=ALU.add),
               ["tmp_r", "cw_tm", "tmp_r2"], ["tmp_r2"], tag=f"{nametag}l2")
            OP("vector", lambda e: e.tensor_reduce(
                out=m2c[:], in_=tmp_r2[:], axis=mybir.AxisListType.X, op=ALU.max),
               ["tmp_r2", "m2c"], ["m2c"], tag=f"{nametag}m2")
            for a in range(4):
                OP("vector", lambda e, a=a: e.tensor_scalar(
                    out=tmp_r2[:, a, :], in0=tmp_r2[:, a, :],
                    scalar1=m2c[:, a:a + 1], scalar2=None, op0=ALU.is_ge),
                   ["tmp_r2", "m2c"], ["tmp_r2"], tag=f"{nametag}e2{a}")
            OP("vector", lambda e: e.tensor_tensor(
                out=s1c[:], in0=m1c[:], in1=m2c[:], op=ALU.subtract),
               ["m1c", "m2c", "s1c"], ["s1c"], tag=f"{nametag}d")
            OP("scalar", lambda e: e.activation(out=s1c[:], in_=s1c[:],
                                                func=AF.Sigmoid),
               ["s1c"], ["s1c"], tag=f"{nametag}sg")
            OP("scalar", lambda e: e.activation(out=s2c[:], in_=s1c[:],
                                                func=AF.Copy, scale=-0.5,
                                                bias=0.5),
               ["s1c", "s2c"], ["s2c"], tag=f"{nametag}s2")
            OP("scalar", lambda e: e.activation(out=s1c[:], in_=s1c[:],
                                                func=AF.Copy, scale=0.5),
               ["s1c"], ["s1c"], tag=f"{nametag}s1")
            for a in range(4):
                OP("vector", lambda e, a=a: e.tensor_scalar(
                    out=tmp_r[:, a, :], in0=tmp_r[:, a, :], scalar1=s1c[:, a:a + 1],
                    scalar2=None, op0=ALU.mult),
                   ["tmp_r", "s1c"], ["tmp_r"], tag=f"{nametag}h1{a}")
                OP("vector", lambda e, a=a: e.tensor_scalar(
                    out=tmp_r2[:, a, :], in0=tmp_r2[:, a, :],
                    scalar1=s2c[:, a:a + 1], scalar2=None, op0=ALU.mult),
                   ["tmp_r2", "s2c"], ["tmp_r2"], tag=f"{nametag}h2{a}")
            OP("vector", lambda e: e.tensor_tensor(
                out=cw_tm.ap().rearrange("p a e -> p (a e)"),
                in0=tmp_r.ap().rearrange("p a e -> p (a e)"),
                in1=tmp_r2.ap().rearrange("p a e -> p (a e)"), op=ALU.add),
               ["tmp_r", "tmp_r2", "cw_tm"], ["cw_tm"], tag=f"{nametag}cw")
            for tt in range(NTT):
                OP("tensor", lambda e, tt=tt: e.transpose(
                    out=pw[1][0:8, tt * 128:(tt + 1) * 128],
                    in_=cw_tm[:, tt, :], identity=eye[:]),
                   ["cw_tm", "eye", "pw1"], ["pw1"], tag=f"{nametag}T{tt}")
            OP("vector", lambda e: e.tensor_copy(out=cwT_sb[:],
                                                 in_=pw[1][0:8, :]),
               ["pw1", "cwT_sb"], ["cwT_sb"], tag=f"{nametag}Te")
            dma("gpsimd", cwT_d[:, c * TC:(c + 1) * TC], cwT_sb[:],
                ["cwT_sb"], ["cwT_d"], f"{nametag}Ts")

        def ffn_expert(c, nametag, ei, filler=None):
            def fill():
                if filler:
                    filler()
            s = c % 2
            xr = f"xres{s}"
            if True:
                for blk in range(8):
                    b = blk % 2
                    cols = slice(blk * 256, (blk + 1) * 256)
                    dma("sync", w1b[:, b], w1bf_d[ei][:, :, cols],
                        [f"w1b{b}", f"w1d{ei}"], [f"w1b{b}"], f"{nametag}a{ei}{blk}")
                    dma("sync", w3b[:, b], w3bf_d[ei][:, :, cols],
                        [f"w3b{b}", f"w3d{ei}"], [f"w3b{b}"], f"{nametag}b{ei}{blk}")
                    for sub in range(2):
                        hc = blk * 2 + sub
                        pgi = hc % 2
                        pui = 2 + hc % 2
                        for dc in range(DCN):
                            OP("tensor", lambda e, b=b, sub=sub, dc=dc, pgi=pgi:
                               e.matmul(
                                   out=pg[pgi][:],
                                   lhsT=w1b[:, b, dc, sub * 128:(sub + 1) * 128],
                                   rhs=xnorm[:, dc, :], start=(dc == 0),
                                   stop=(dc == DCN - 1)),
                               [f"w1b{b}", "xnorm", f"pg{pgi}"], [f"pg{pgi}"],
                               tag=f"{nametag}g{ei}{hc}{dc}")
                        for dc in range(DCN):
                            OP("tensor", lambda e, b=b, sub=sub, dc=dc, pui=pui:
                               e.matmul(
                                   out=pg[pui][:],
                                   lhsT=w3b[:, b, dc, sub * 128:(sub + 1) * 128],
                                   rhs=xnorm[:, dc, :], start=(dc == 0),
                                   stop=(dc == DCN - 1)),
                               [f"w3b{b}", "xnorm", f"pg{pui}"], [f"pg{pui}"],
                               tag=f"{nametag}w{ei}{hc}{dc}")
                        OP("scalar", lambda e, pgi=pgi, hc=hc: e.activation(
                            out=sg[:, hc % 2, :], in_=pg[pgi][:], func=AF.Silu),
                           [f"pg{pgi}", f"sg{hc % 2}"], [f"sg{hc % 2}"],
                           tag=f"{nametag}s{ei}{hc}")
                        OP("vector", lambda e, pui=pui, hc=hc: e.tensor_tensor(
                            out=eo[:, hc, :], in0=sg[:, hc % 2, :], in1=pg[pui][:],
                            op=ALU.mult),
                           [f"sg{hc % 2}", f"pg{pui}"] + list(EOALL), list(EOALL),
                           tag=f"{nametag}e{ei}{hc}")
                        fill()
                if ei < E:
                    dma("gpsimd", cwb[:, ei % 2, :],
                        cwT_d[ei:ei + 1, c * TC:(c + 1) * TC].partition_broadcast(128),
                        ["cwT_d", f"cwb{ei % 2}"], [f"cwb{ei % 2}"],
                        f"{nametag}c{ei}")
                for dp in range(8):
                    b = dp % 2
                    dma("sync", w2b[:, b], w2bf_d[ei, dp],
                        [f"w2b{b}", f"w2d{ei}"], [f"w2b{b}"], f"{nametag}d{ei}{dp}")
                    pwi = dp % 2
                    for hc in range(NHC):
                        OP("tensor", lambda e, b=b, hc=hc, pwi=pwi: e.matmul(
                            out=pw[pwi][:], lhsT=w2b[:, b, hc, :], rhs=eo[:, hc, :],
                            start=(hc == 0), stop=(hc == NHC - 1)),
                           [f"w2b{b}", f"pw{pwi}"] + list(EOALL), [f"pw{pwi}"],
                           tag=f"{nametag}2{ei}{dp}{hc}")
                    if ei < E:
                        OP("vector", lambda e, pwi=pwi, ei=ei: e.tensor_tensor(
                            out=sge[:], in0=pw[pwi][:], in1=cwb[:, ei % 2, :],
                            op=ALU.mult),
                           [f"pw{pwi}", f"cwb{ei % 2}", "sge"], ["sge"],
                           tag=f"{nametag}v{ei}{dp}")
                    else:
                        OP("vector", lambda e, pwi=pwi: e.tensor_scalar(
                            out=sge[:], in0=pw[pwi][:], scalar1=0.5, scalar2=None,
                            op0=ALU.mult),
                           [f"pw{pwi}", "sge"], ["sge"], tag=f"{nametag}v{ei}{dp}")
                    OP("vector", lambda e, dp=dp, s=s: e.tensor_tensor(
                        out=xres[:, s, dp, :], in0=xres[:, s, dp, :], in1=sge[:],
                        op=ALU.add),
                       ["sge", xr], [xr], tag=f"{nametag}x{ei}{dp}")
                    fill()

        def out_proj_chunk(c, nametag):
            s = c % 2
            xr = f"xres{s}"
            for ep in range(8):
                b = ep % 2
                dma("sync", opb[:, b], oprojT_d[:, :, ep * 128:(ep + 1) * 128],
                    [f"opb{b}", "opd"], [f"opb{b}"], f"{nametag}l{ep}")
                pwi = ep % 2
                for dc in range(DCN):
                    OP("tensor", lambda e, b=b, dc=dc, pwi=pwi: e.matmul(
                        out=pw[pwi][:], lhsT=opb[:, b, dc, :], rhs=xnm[:, dc, :],
                        start=(dc == 0), stop=(dc == DCN - 1)),
                       [f"opb{b}", "xnm", f"pw{pwi}"], [f"pw{pwi}"],
                       tag=f"{nametag}m{ep}{dc}")
                OP("vector", lambda e, ep=ep, pwi=pwi, s=s: e.tensor_tensor(
                    out=xres[:, s, ep, :], in0=xres[:, s, ep, :], in1=pw[pwi][:],
                    op=ALU.add),
                   [f"pw{pwi}", xr], [xr], tag=f"{nametag}e{ep}")

        def halo_ln(dst_is_x3n, nametag):
            for dc in range(DCN):
                OP("tensor", lambda e, dc=dc: e.matmul(
                    out=pstat[0:1, 8:11], lhsT=ones_c[:], rhs=x3f[:, dc, 0:3],
                    start=(dc == 0), stop=(dc == DCN - 1)),
                   ["x3f", "ones_c", "pstat"], ["pstat"], tag=f"{nametag}s{dc}")
            for dc in range(DCN):
                OP("vector", lambda e, dc=dc: e.tensor_tensor(
                    out=sg3[:, 0:3], in0=x3f[:, dc, 0:3], in1=x3f[:, dc, 0:3],
                    op=ALU.mult),
                   ["x3f", "sg3"], ["sg3"], tag=f"{nametag}q{dc}")
                OP("tensor", lambda e, dc=dc: e.matmul(
                    out=pstat[0:1, 12:15], lhsT=ones_c[:], rhs=sg3[:, 0:3],
                    start=(dc == 0), stop=(dc == DCN - 1)),
                   ["sg3", "ones_c", "pstat"], ["pstat"], tag=f"{nametag}k{dc}")
            OP("scalar", lambda e: e.activation(
                out=m_row[0:1, 0:3], in_=pstat[0:1, 8:11], func=AF.Copy,
                scale=1.0 / D), ["pstat", "m_row"], ["m_row"], tag=f"{nametag}m")
            OP("scalar", lambda e: e.activation(
                out=q_row[0:1, 0:3], in_=pstat[0:1, 12:15], func=AF.Copy,
                scale=1.0 / D), ["pstat", "q_row"], ["q_row"], tag=f"{nametag}qq")
            OP("vector", lambda e: e.tensor_tensor(
                out=mr_row[0:1, 0:3], in0=m_row[0:1, 0:3], in1=m_row[0:1, 0:3],
                op=ALU.mult), ["m_row", "mr_row"], ["mr_row"], tag=f"{nametag}mm")
            OP("vector", lambda e: e.tensor_tensor(
                out=q_row[0:1, 0:3], in0=q_row[0:1, 0:3], in1=mr_row[0:1, 0:3],
                op=ALU.subtract), ["q_row", "mr_row"], ["q_row"], tag=f"{nametag}v")
            OP("scalar", lambda e: e.activation(out=q_row[0:1, 0:3],
                                                in_=q_row[0:1, 0:3],
                                                func=AF.Ln, bias=cvals[0:1, 0:1]),
               ["q_row", "cvals"], ["q_row"], tag=f"{nametag}l")
            OP("scalar", lambda e: e.activation(out=q_row[0:1, 0:3],
                                                in_=q_row[0:1, 0:3],
                                                func=AF.Exp, scale=-0.5),
               ["q_row"], ["q_row"], tag=f"{nametag}x")
            OP("vector", lambda e: e.tensor_tensor(
                out=mr_row[0:1, 0:3], in0=m_row[0:1, 0:3], in1=q_row[0:1, 0:3],
                op=ALU.mult), ["m_row", "q_row", "mr_row"], ["mr_row"],
               tag=f"{nametag}mr")
            dma("gpsimd", hrows_d[0:1, 0:3], q_row[0:1, 0:3], ["q_row"],
                ["hrows_d"], f"{nametag}w1")
            dma("gpsimd", hrows_d[1:2, 0:3], mr_row[0:1, 0:3], ["mr_row"],
                ["hrows_d"], f"{nametag}w2")
            dma("gpsimd", bc[:, 0, 0:3], hrows_d[0:1, 0:3].partition_broadcast(128),
                ["hrows_d", "bc0"], ["bc0"], f"{nametag}b1")
            dma("gpsimd", bc[:, 1, 0:3], hrows_d[1:2, 0:3].partition_broadcast(128),
                ["hrows_d", "bc1"], ["bc1"], f"{nametag}b2")
            for dc in range(DCN):
                OP("vector", lambda e, dc=dc: e.tensor_tensor(
                    out=sg3[:, 0:3], in0=x3f[:, dc, 0:3], in1=bc[:, 0, 0:3],
                    op=ALU.mult), ["x3f", "bc0", "sg3"], ["sg3"],
                   tag=f"{nametag}a{dc}")
                OP("vector", lambda e, dc=dc: e.tensor_tensor(
                    out=x3n[:, dc, 0:3], in0=sg3[:, 0:3], in1=bc[:, 1, 0:3],
                    op=ALU.subtract), ["sg3", "bc1", "x3n"], ["x3n"],
                   tag=f"{nametag}b{dc}")

        def halo_pre():
            dma("sync", x3, x_halo, ["eoA"], ["eoA"], "hxl")
            OP("vector", lambda e: e.memset(x3f[:], 0.0), ["x3f"], ["x3f"],
               tag="hx0")
            for dc in range(DCN):
                OP("tensor", lambda e, dc=dc: e.transpose(
                    out=pstat[:, 0:3], in_=x3[:, dc * 128:(dc + 1) * 128],
                    identity=eye[0:3, 0:3]),
                   ["eoA", "eye", "pstat"], ["pstat"], tag=f"hT{dc}")
                OP("vector", lambda e, dc=dc: e.tensor_copy(
                    out=x3f[:, dc, 0:3], in_=pstat[:, 0:3]),
                   ["pstat", "x3f"], ["x3f"], tag=f"hE{dc}")
            halo_ln(True, "hl1")
            for dc in range(DCN):
                OP("tensor", lambda e, dc=dc: e.matmul(
                    out=pstat[0:3, 16:24], lhsT=x3n[:, dc, 0:3],
                    rhs=routerT[:, dc, :], start=(dc == 0), stop=False),
                   ["x3n", "routerT", "pstat"], ["pstat"], tag=f"hg{dc}")
            OP("tensor", lambda e: e.matmul(
                out=pstat[0:3, 16:24], lhsT=ones_bf[0:1, 0:3], rhs=ebias_bf[:],
                start=False, stop=True),
               ["ones_bf", "ebias_bf", "pstat"], ["pstat"], tag="hgb")
            OP("vector", lambda e: e.tensor_copy(out=cw3_tm[:],
                                                 in_=pstat[0:3, 16:24]),
               ["pstat", "cw3"], ["cw3"], tag="hcp")
            OP("vector", lambda e: e.tensor_reduce(
                out=m1c[0:3, 0:1], in_=cw3_tm[:], axis=mybir.AxisListType.X,
                op=ALU.max), ["cw3", "m1c"], ["m1c"], tag="hm1")
            OP("vector", lambda e: e.tensor_scalar(
                out=tmp_r[0:3, 0, :], in0=cw3_tm[:], scalar1=m1c[0:3, 0:1],
                scalar2=None, op0=ALU.is_ge),
               ["cw3", "m1c", "tmp_r"], ["tmp_r"], tag="he1")
            OP("vector", lambda e: e.scalar_tensor_tensor(
                out=tmp_r2[0:3, 0, :], in0=tmp_r[0:3, 0, :], scalar=-1e30,
                in1=cw3_tm[:], op0=ALU.mult, op1=ALU.add),
               ["tmp_r", "cw3", "tmp_r2"], ["tmp_r2"], tag="hl2")
            OP("vector", lambda e: e.tensor_reduce(
                out=m2c[0:3, 0:1], in_=tmp_r2[0:3, 0, :],
                axis=mybir.AxisListType.X, op=ALU.max),
               ["tmp_r2", "m2c"], ["m2c"], tag="hm2")
            OP("vector", lambda e: e.tensor_scalar(
                out=tmp_r2[0:3, 0, :], in0=tmp_r2[0:3, 0, :],
                scalar1=m2c[0:3, 0:1], scalar2=None, op0=ALU.is_ge),
               ["tmp_r2", "m2c"], ["tmp_r2"], tag="he2")
            OP("vector", lambda e: e.tensor_tensor(
                out=s1c[0:3, 0:1], in0=m1c[0:3, 0:1], in1=m2c[0:3, 0:1],
                op=ALU.subtract), ["m1c", "m2c", "s1c"], ["s1c"], tag="hd")
            OP("scalar", lambda e: e.activation(out=s1c[0:3, 0:1],
                                                in_=s1c[0:3, 0:1],
                                                func=AF.Sigmoid),
               ["s1c"], ["s1c"], tag="hsg")
            OP("scalar", lambda e: e.activation(out=s2c[0:3, 0:1],
                                                in_=s1c[0:3, 0:1], func=AF.Copy,
                                                scale=-0.5, bias=0.5),
               ["s1c", "s2c"], ["s2c"], tag="hs2")
            OP("scalar", lambda e: e.activation(out=s1c[0:3, 0:1],
                                                in_=s1c[0:3, 0:1], func=AF.Copy,
                                                scale=0.5),
               ["s1c"], ["s1c"], tag="hs1")
            OP("vector", lambda e: e.tensor_scalar(
                out=tmp_r[0:3, 0, :], in0=tmp_r[0:3, 0, :], scalar1=s1c[0:3, 0:1],
                scalar2=None, op0=ALU.mult), ["tmp_r", "s1c"], ["tmp_r"],
               tag="hh1")
            OP("vector", lambda e: e.tensor_scalar(
                out=tmp_r2[0:3, 0, :], in0=tmp_r2[0:3, 0, :],
                scalar1=s2c[0:3, 0:1], scalar2=None, op0=ALU.mult),
               ["tmp_r2", "s2c"], ["tmp_r2"], tag="hh2")
            OP("vector", lambda e: e.tensor_tensor(
                out=cw3_tm[:], in0=tmp_r[0:3, 0, :], in1=tmp_r2[0:3, 0, :],
                op=ALU.add), ["tmp_r", "tmp_r2", "cw3"], ["cw3"], tag="hcw")
            OP("tensor", lambda e: e.transpose(
                out=pstat[0:8, 32:35], in_=cw3_tm[:], identity=eye[0:3, 0:3]),
               ["cw3", "eye", "pstat"], ["pstat"], tag="hcT")
            OP("vector", lambda e: e.tensor_copy(out=cw3T[:, 0:3],
                                                 in_=pstat[0:8, 32:35]),
               ["pstat", "cw3T"], ["cw3T"], tag="hcTe")
            dma("gpsimd", cw3_d[:, 0:3], cw3T[:, 0:3], ["cw3T"], ["cw3_d"], "hcTs")
        def halo_expert(ei):
            for blk in range(8):
                b = blk % 2
                cols = slice(blk * 256, (blk + 1) * 256)
                dma("sync", w1b[:, b], w1bf_d[ei][:, :, cols],
                    [f"w1b{b}", f"w1d{ei}"], [f"w1b{b}"], f"ha{ei}{blk}")
                dma("sync", w3b[:, b], w3bf_d[ei][:, :, cols],
                    [f"w3b{b}", f"w3d{ei}"], [f"w3b{b}"], f"hb{ei}{blk}")
                for sub in range(2):
                    hc = blk * 2 + sub
                    for dc in range(DCN):
                        OP("tensor", lambda e, b=b, sub=sub, dc=dc: e.matmul(
                            out=pg[0][:, 0:3],
                            lhsT=w1b[:, b, dc, sub * 128:(sub + 1) * 128],
                            rhs=x3n[:, dc, 0:3], start=(dc == 0),
                            stop=(dc == DCN - 1)),
                           [f"w1b{b}", "x3n", "pg0"], ["pg0"],
                           tag=f"hg{ei}{hc}{dc}")
                    for dc in range(DCN):
                        OP("tensor", lambda e, b=b, sub=sub, dc=dc: e.matmul(
                            out=pg[2][:, 0:3],
                            lhsT=w3b[:, b, dc, sub * 128:(sub + 1) * 128],
                            rhs=x3n[:, dc, 0:3], start=(dc == 0),
                            stop=(dc == DCN - 1)),
                           [f"w3b{b}", "x3n", "pg2"], ["pg2"],
                           tag=f"hu{ei}{hc}{dc}")
                    OP("scalar", lambda e: e.activation(
                        out=sg3[:, 0:3], in_=pg[0][:, 0:3], func=AF.Silu),
                       ["pg0", "sg3"], ["sg3"], tag=f"hs{ei}{hc}")
                    OP("vector", lambda e, hc=hc: e.tensor_tensor(
                        out=eo3[:, hc, 0:3], in0=sg3[:, 0:3],
                        in1=pg[2][:, 0:3], op=ALU.mult),
                       ["sg3", "pg2", "eo3"], ["eo3"], tag=f"he{ei}{hc}")
            if ei < E:
                dma("gpsimd", cwb3[:, 0:3],
                    cw3_d[ei:ei + 1, 0:3].partition_broadcast(128),
                    ["cw3_d", "cwb3"], ["cwb3"], f"hc{ei}")
            for dp in range(8):
                b = dp % 2
                dma("sync", w2b[:, b], w2bf_d[ei, dp],
                    [f"w2b{b}", f"w2d{ei}"], [f"w2b{b}"], f"hd{ei}{dp}")
                for hc in range(NHC):
                    OP("tensor", lambda e, b=b, hc=hc: e.matmul(
                        out=pw[0][:, 0:3], lhsT=w2b[:, b, hc, :],
                        rhs=eo3[:, hc, 0:3], start=(hc == 0),
                        stop=(hc == NHC - 1)),
                       [f"w2b{b}", "eo3", "pw0"], ["pw0"],
                       tag=f"h2{ei}{dp}{hc}")
                if ei < E:
                    OP("vector", lambda e: e.tensor_tensor(
                        out=sg3[:, 0:3], in0=pw[0][:, 0:3], in1=cwb3[:, 0:3],
                        op=ALU.mult), ["pw0", "cwb3", "sg3"], ["sg3"],
                       tag=f"hv{ei}{dp}")
                else:
                    OP("vector", lambda e: e.tensor_scalar(
                        out=sg3[:, 0:3], in0=pw[0][:, 0:3], scalar1=0.5,
                        scalar2=None, op0=ALU.mult),
                       ["pw0", "sg3"], ["sg3"], tag=f"hv{ei}{dp}")
                OP("vector", lambda e, dp=dp: e.tensor_tensor(
                    out=x3f[:, dp, 0:3], in0=x3f[:, dp, 0:3],
                    in1=sg3[:, 0:3], op=ALU.add),
                   ["sg3", "x3f"], ["x3f"], tag=f"hx{ei}{dp}")

        def halo_post():
                halo_ln(True, "hl2")
                for et in range(8):
                    b = et % 2
                    dma("sync", ipb[:, b], iprojT_d[:, :, et * 128:(et + 1) * 128],
                        [f"ipb{b}", "ipd"], [f"ipb{b}"], f"hil{et}")
                    for dc in range(DCN):
                        OP("tensor", lambda e, b=b, dc=dc, et=et: e.matmul(
                            out=pg[et % 2][:, 0:3], lhsT=ipb[:, b, dc, :],
                            rhs=x3n[:, dc, 0:3], start=(dc == 0),
                            stop=(dc == DCN - 1)),
                           [f"ipb{b}", "x3n", f"pg{et % 2}"], [f"pg{et % 2}"],
                           tag=f"hi{et}{dc}")
                    OP("vector", lambda e, et=et: e.tensor_copy(
                        out=y3[:, et, 0:3], in_=pg[et % 2][:, 0:3]),
                       [f"pg{et % 2}", "y3"], ["y3"], tag=f"hie{et}")
                for dc in range(DCN):
                    OP("vector", lambda e, dc=dc: e.tensor_scalar(
                        out=tail3[:, dc, :], in0=y3[:, dc, 0:3],
                        scalar1=masks[:, 1:2], scalar2=None, op0=ALU.mult),
                       ["y3", "masks", "tail3"], ["tail3"], tag=f"hms{dc}")

        halo_pre()

        def scan_p1_pre(c, nametag):
            dma("gpsimd", bc[:, 2, :], rows_d[c, 32:33, :].partition_broadcast(128),
                [f"rows{c}", "bc2"], ["bc2"], f"{nametag}bd")

        def scan_p1_step(c, nametag, n, dg):
            bs = n % 2
            if dg == 0:
                dma("gpsimd", bc[:, bs, :],
                    rows_d[c, n:n + 1, :].partition_broadcast(128),
                    [f"rows{c}", f"bc{bs}"], [f"bc{bs}"], f"{nametag}bB{n}")
                dma("gpsimd", scb[:, bs, :],
                    rows_d[c, 16 + n:17 + n, :].partition_broadcast(128),
                    [f"rows{c}", f"scb{bs}"], [f"scb{bs}"], f"{nametag}bC{n}")
            if True:
                ds = (n * DCN + dg) % 2
                OP("scalar", lambda e, dg=dg, n=n, ds=ds: e.activation(
                    out=dAt[:, ds, :], in_=bc[:, 2, :], func=AF.Exp,
                    scale=A_sb[:, dg, n:n + 1]),
                   ["bc2", "A", f"dAt{ds}"], [f"dAt{ds}"],
                   tag=f"{nametag}A{n}{dg}")
                OP("vector", lambda e, dg=dg, bs=bs: e.tensor_tensor(
                    out=ut[:], in0=xconv[:, dg, :], in1=bc[:, bs, :],
                    op=ALU.mult),
                   ["xconv", f"bc{bs}", "ut"], ["ut"], tag=f"{nametag}u{n}{dg}")
                OP("vector", lambda e, dg=dg, n=n, ds=ds: e.tensor_tensor_scan(
                    out=hst[:], data0=dAt[:, ds, :], data1=ut[:],
                    initial=hstate[:, dg, n:n + 1], op0=ALU.mult, op1=ALU.add),
                   [f"dAt{ds}", "ut", "hstate", "hst"], ["hst"],
                   tag=f"{nametag}S{n}{dg}")
                OP("vector", lambda e, dg=dg, n=n: e.tensor_copy(
                    out=hstate[:, dg, n:n + 1], in_=hst[:, TC - 1:TC]),
                   ["hst", "hstate"], ["hstate"], tag=f"{nametag}h{n}{dg}")
                if n == 0:
                    OP("gpsimd", lambda e, dg=dg, bs=bs: e.tensor_tensor(
                        out=yacc[:, dg, :], in0=hst[:], in1=scb[:, bs, :],
                        op=ALU.mult),
                       ["hst", f"scb{bs}", "yacc"], ["yacc"],
                       tag=f"{nametag}y{n}{dg}")
                else:
                    OP("gpsimd", lambda e, dg=dg, bs=bs: e.tensor_tensor(
                        out=ut2[:], in0=hst[:], in1=scb[:, bs, :], op=ALU.mult),
                       ["hst", f"scb{bs}", "ut2"], ["ut2"],
                       tag=f"{nametag}p{n}{dg}")
                    OP("gpsimd", lambda e, dg=dg: e.tensor_tensor(
                        out=yacc[:, dg, :], in0=yacc[:, dg, :], in1=ut2[:],
                        op=ALU.add),
                       ["ut2", "yacc"], ["yacc"], tag=f"{nametag}a{n}{dg}")

        def scan_p1_fin(c, nametag):
            for dg in range(DCN):
                OP("vector", lambda e, dg=dg: e.scalar_tensor_tensor(
                    out=yacc[:, dg, :], in0=xconv[:, dg, :],
                    scalar=D_sb[:, dg:dg + 1], in1=yacc[:, dg, :],
                    op0=ALU.mult, op1=ALU.add),
                   ["xconv", "Dsb", "yacc"], ["yacc"], tag=f"{nametag}D{dg}")
                OP("vector", lambda e, dg=dg: e.tensor_tensor(
                    out=xnm[:, dg, :], in0=yacc[:, dg, :], in1=szbf_w[:, dg, :],
                    op=ALU.mult),
                   ["yacc", "szbf_w", "xnm"], ["xnm"], tag=f"{nametag}G{dg}")

        def front(c):
            s = c % 2
            xr = f"xres{s}"
            ln_chunk(c, f"1B{c}", outbuf=xnm, outres="xnm")
            for et in range(16):
                b = et % 2
                dma("sync", ipb[:, b], iprojT_d[:, :, et * 128:(et + 1) * 128],
                    [f"ipb{b}", "ipd"], [f"ipb{b}"], f"1il{c}{et}")
                pgi = et % 4
                for dc in range(DCN):
                    OP("tensor", lambda e, b=b, dc=dc, pgi=pgi: e.matmul(
                        out=pg[pgi][:], lhsT=ipb[:, b, dc, :], rhs=xnm[:, dc, :],
                        start=(dc == 0), stop=(dc == DCN - 1)),
                       [f"ipb{b}", "xnm", f"pg{pgi}"], [f"pg{pgi}"],
                       tag=f"1i{c}{et}{dc}")
                if et < 8:
                    OP("vector", lambda e, et=et, pgi=pgi: e.tensor_copy(
                        out=xssm[:, et, 3:515], in_=pg[pgi][:]),
                       [f"pg{pgi}", "xssm"], ["xssm"], tag=f"1ie{c}{et}")
                else:
                    OP("scalar", lambda e, et=et, pgi=pgi: e.activation(
                        out=szbf_w[:, et - 8, :], in_=pg[pgi][:], func=AF.Silu),
                       [f"pg{pgi}", "szbf_w"], ["szbf_w"], tag=f"1iz{c}{et}")
            dma("scalar", szbf_d[c], szbf_w[:], ["szbf_w"], [f"szd{c}"],
                f"1zs{c}")
            if c == 0:
                OP("vector", lambda e: e.tensor_copy(out=xssm[:, :, 0:3],
                                                     in_=tail3[:]),
                   ["tail3", "xssm"], ["xssm"], tag="1t0")
            for dg in range(DCN):
                OP("vector", lambda e, dg=dg: e.tensor_scalar(
                    out=xconv[:, dg, :], in0=xssm[:, dg, 0:TC],
                    scalar1=convw[:, dg, 0:1], scalar2=None, op0=ALU.mult),
                   ["xssm", "convw", "xconv"], ["xconv"], tag=f"1c0{c}{dg}")
                for k in range(1, 4):
                    OP("vector", lambda e, dg=dg, k=k: e.scalar_tensor_tensor(
                        out=xconv[:, dg, :], in0=xssm[:, dg, k:k + TC],
                        scalar=convw[:, dg, k:k + 1], in1=xconv[:, dg, :],
                        op0=ALU.mult, op1=ALU.add),
                       ["xssm", "convw", "xconv"], ["xconv"],
                       tag=f"1c{k}{c}{dg}")
                OP("scalar", lambda e, dg=dg: e.activation(
                    out=xconv[:, dg, :], in_=xconv[:, dg, :], func=AF.Silu,
                    bias=convb[:, dg:dg + 1]),
                   ["xconv", "convb"], ["xconv"], tag=f"1cs{c}{dg}")
                OP("scalar", lambda e, dg=dg: e.copy(
                    out=xconv_bf[:, dg, :], in_=xconv[:, dg, :]),
                   ["xconv", "yacc"], ["yacc"], tag=f"1cb{c}{dg}")
            OP("vector", lambda e: e.tensor_copy(
                out=xssm[:, :, 0:3], in_=xssm[:, :, 512:515]),
               ["xssm"], ["xssm"], tag=f"1tl{c}")
            for dc in range(DCN):
                OP("tensor", lambda e, dc=dc: e.matmul(
                    out=pstat[0:80, :], lhsT=xprojT[:, dc, :],
                    rhs=xconv_bf[:, dc, :], start=(dc == 0),
                    stop=(dc == DCN - 1)),
                   ["yacc", "xprojT", "pstat"], ["pstat"], tag=f"1xp{c}{dc}")
            OP("scalar", lambda e: e.activation(out=drow[:], in_=pstat[64:65, :],
                                                func=AF.Abs),
               ["pstat", "drow"], ["drow"], tag=f"1d1{c}")
            OP("scalar", lambda e: e.activation(out=drow[:], in_=drow[:],
                                                func=AF.Exp, scale=-1.0),
               ["drow"], ["drow"], tag=f"1d2{c}")
            OP("scalar", lambda e: e.activation(out=drow[:], in_=drow[:],
                                                func=AF.Ln, bias=1.0),
               ["drow"], ["drow"], tag=f"1d3{c}")
            OP("scalar", lambda e: e.activation(out=srow[:], in_=pstat[64:65, :],
                                                func=AF.Relu),
               ["pstat", "q_row"], ["q_row"], tag=f"1d4{c}")
            OP("vector", lambda e: e.tensor_tensor(out=drow[:], in0=drow[:],
                                                   in1=srow[:], op=ALU.add),
               ["drow", "q_row"], ["drow"], tag=f"1d5{c}")
            dma("gpsimd", rows_d[c, 32:33, :], drow[:], ["drow"], [f"rows{c}"],
                f"1dw{c}")
            dma("gpsimd", scanrows[0:16, :],
                rows_d[c, 32:33, :].partition_broadcast(16),
                [f"rows{c}", "scanrows"], ["scanrows"], f"1db{c}")
            OP("vector", lambda e: e.tensor_tensor(
                out=scanrows[0:16, :], in0=pstat[0:16, :], in1=scanrows[0:16, :],
                op=ALU.mult),
               ["pstat", "scanrows"], ["scanrows"], tag=f"1dB{c}")
            OP("vector", lambda e: e.tensor_copy(out=scanrows[32:48, :],
                                                 in_=pstat[32:48, :]),
               ["pstat", "scanrows"], ["scanrows"], tag=f"1Ce{c}")
            dma("gpsimd", rows_d[c, 0:16, :], scanrows[0:16, :],
                ["scanrows"], [f"rows{c}"], f"1dBs{c}")
            dma("gpsimd", rows_d[c, 16:32, :], scanrows[32:48, :],
                ["scanrows"], [f"rows{c}"], f"1Cs{c}")

        def load_x(c):
            s = c % 2
            xr = f"xres{s}"
            dma("sync", xres[:, s], xres_d[c], [xr, f"xrd{c}"], [xr], f"1xl{c}")

        def chunk_head(c, with_load=True):
            if with_load:
                load_x(c)
            ln_chunk(c, f"1A{c}")
            router_chunk(c, f"1R{c}")

        def p1_tail(c):
            scan_p1_fin(c, f"1s{c}")
            out_proj_chunk(c, f"1o{c}")
            s = c % 2
            dma("sync", xres_d[c], xres[:, s], [f"xres{s}"],
                [f"xres{s}", f"xrd{c}"], f"1xs{c}")

        def make_filler(steps):
            def filler():
                if steps:
                    steps.pop(0)()
            def drain():
                while steps:
                    steps.pop(0)()
            return filler, drain

        def p3loads(c):
            s = c % 2
            xr = f"xres{s}"
            dma("sync", xres[:, s], xres_d[c], [xr, f"xrd{c}"], [xr], f"3xl{c}")
            dma("scalar", szbf_w[:], szbf_d[c], [f"szd{c}", "szbf_w"],
                ["szbf_w"], f"3zl{c}")

        def pass2_pre(c):
            dma("gpsimd", bc[:, 2, :], rows_d[c, 33:34, :].partition_broadcast(128),
                [f"rows{c}", "bc2"], ["bc2"], f"3bS{c}")

        def pass2_step(c, n, dg):
            bs = n % 2
            if dg == 0:
                dma("gpsimd", scb[:, bs, :],
                    rows_d[c, 16 + n:17 + n, :].partition_broadcast(128),
                    [f"rows{c}", f"scb{bs}"], [f"scb{bs}"], f"3bC{c}{n}")
            if True:
                ds = (n * DCN + dg) % 2
                OP("scalar", lambda e, dg=dg, n=n, ds=ds: e.activation(
                    out=dAt[:, ds, :], in_=bc[:, 2, :], func=AF.Exp,
                    scale=A_sb[:, dg, n:n + 1]),
                   ["bc2", "A", f"dAt{ds}"], [f"dAt{ds}"],
                   tag=f"3E{c}{n}{dg}")
                OP("vector", lambda e, dg=dg, n=n, ds=ds, bs=bs:
                   e.scalar_tensor_tensor(
                       out=ut[:], in0=dAt[:, ds, :],
                       scalar=hin[:, dg, n:n + 1], in1=scb[:, bs, :],
                       op0=ALU.mult, op1=ALU.mult),
                   [f"dAt{ds}", "hin", f"scb{bs}", "ut"], ["ut"],
                   tag=f"3t{c}{n}{dg}")
                if n == 0:
                    OP("vector", lambda e, dg=dg: e.tensor_copy(
                        out=yacc[:, dg, :], in_=ut[:]),
                       ["ut", "yacc"], ["yacc"], tag=f"3y{c}{n}{dg}")
                else:
                    OP("vector", lambda e, dg=dg: e.tensor_tensor(
                        out=yacc[:, dg, :], in0=yacc[:, dg, :], in1=ut[:],
                        op=ALU.add),
                       ["ut", "yacc"], ["yacc"], tag=f"3a{c}{n}{dg}")

        def pass2_fin(c):
            for dg in range(DCN):
                OP("vector", lambda e, dg=dg: e.tensor_tensor(
                    out=xnm[:, dg, :], in0=yacc[:, dg, :], in1=szbf_w[:, dg, :],
                    op=ALU.mult),
                   ["yacc", "szbf_w", "xnm"], ["xnm"], tag=f"3g{c}{dg}")
            out_proj_chunk(c, f"3o{c}")
            ln_chunk(c, f"3A{c}")
            router_chunk(c, f"3R{c}")

        def output_chunk(c):
            s = c % 2
            xr = f"xres{s}"
            for tt in range(NTT):
                ob = tt % 2
                for dc in range(DCN):
                    OP("tensor", lambda e, tt=tt, dc=dc, s=s: e.transpose(
                        out=pstat[:, 128:256],
                        in_=xres[:, s, dc, tt * 128:(tt + 1) * 128],
                        identity=eye[:]),
                       [xr, "eye", "pstat"], ["pstat"], tag=f"3T{c}{tt}{dc}")
                    OP("vector", lambda e, ob=ob, dc=dc: e.tensor_copy(
                        out=outstg[:, ob, dc * 128:(dc + 1) * 128],
                        in_=pstat[:, 128:256]),
                       ["pstat", f"os{ob}"], [f"os{ob}", "xssm"],
                       tag=f"3O{c}{tt}{dc}")
                trow = c * TC + tt * 128
                dma("sync", out_d[trow:trow + 128, :], outstg[:, ob],
                    [f"os{ob}"], [f"os{ob}done"], f"3os{c}{tt}")


        chunk_head(0)
        for c in range(NCH):
            if c >= 1:
                steps = [
                    (lambda cc=c - 1, n=n, dg=dg:
                     scan_p1_step(cc, f"1s{cc}", n, dg))
                    for n in range(NST) for dg in range(DCN)]
                filler, drain = make_filler(steps)
            else:
                filler, drain = None, lambda: None
            for ei in range(NEXP):
                ffn_expert(c, f"1F{c}", ei, filler=(filler if ei >= 2 else None))
                if c == 0:
                    halo_expert(ei)
                if c >= 1 and ei == 0:
                    front(c - 1)
                    scan_p1_pre(c - 1, f"1s{c - 1}")
                if c == 0 and ei == 1:
                    load_x(1)
            drain()
            if c == 0:
                halo_post()
                chunk_head(1, with_load=False)
            if c >= 1:
                p1_tail(c - 1)
                if c + 1 < NCH:
                    chunk_head(c + 1)
        front(3)
        sview = eo_flat[0:1, 0:2 * T]
        ones_r2k = yacc.ap().rearrange("p a b -> p (a b)")[0:1, 0:TC]
        OP("vector", lambda e: e.memset(ones_r2k[0:1, :], 1.0),
           ["yacc"], ["ones_r2k", "yacc"], tag="Sones")
        dma("gpsimd", sview[0:1, 0:T].rearrange("r (c t) -> r c t", c=NCH),
            rows_d[:, 32:33, :].transpose([1, 0, 2]),
            list(EOALL) + [f"rows{c}" for c in range(NCH)], list(EOALL), "Sload")
        for sc in range(NCH):
            OP("vector", lambda e, sc=sc: e.tensor_tensor_scan(
                out=sview[0:1, T + sc * TC:T + (sc + 1) * TC],
                data0=ones_r2k[0:1, :],
                data1=sview[0:1, sc * TC:(sc + 1) * TC],
                initial=0.0, op0=ALU.mult, op1=ALU.add),
               list(EOALL) + ["ones_r2k"], list(EOALL), tag=f"Sscan{sc}")
        for sc in range(1, NCH):
            OP("vector", lambda e, sc=sc: e.tensor_scalar(
                out=sview[0:1, T + sc * TC:T + (sc + 1) * TC],
                in0=sview[0:1, T + sc * TC:T + (sc + 1) * TC],
                scalar1=sview[0:1, T + sc * TC - 1:T + sc * TC],
                scalar2=None, op0=ALU.add),
               list(EOALL), list(EOALL), tag=f"Soff{sc}")
        dma("gpsimd", rows_d[:, 33:34, :].transpose([1, 0, 2]),
            sview[0:1, T:2 * T].rearrange("r (c t) -> r c t", c=NCH),
            list(EOALL),
            [f"rows{c}" for c in range(NCH)], "Sstore")
        scan_p1_pre(3, "1s3")
        for n in range(NST):
            for dg in range(DCN):
                scan_p1_step(3, "1s3", n, dg)
        OP("vector", lambda e: e.tensor_scalar(
            out=ccs[:], in0=hstate.ap().rearrange("p dg n -> p (dg n)"),
            scalar1=masks[:, 0:1], scalar2=None, op0=ALU.mult),
           ["hstate", "masks", "ccs"], ["ccs"], tag="ccm")
        dma("gpsimd", cc_in_d[:, :], ccs[:], ["ccs"], ["cc_in"], "ccw")
        OP("gpsimd", lambda e: e.collective_compute(
            "AllReduce", ALU.add,
            replica_groups=[[0, 1], [2, 3], [4, 5], [6, 7]],
            ins=[cc_in_d[:, :]], outs=[cc_out_d[:, :]]),
           ["cc_in", "cc_out"], ["cc_out"], tag="cc")
        dma("gpsimd", ccs[:], cc_out_d[:, :], ["cc_out", "ccs"], ["ccs"], "ccr")
        OP("vector", lambda e: e.tensor_scalar(
            out=hin.ap().rearrange("p dg n -> p (dg n)"), in0=ccs[:],
            scalar1=masks[:, 1:2], scalar2=None, op0=ALU.mult),
           ["ccs", "masks", "hin"], ["hin"], tag="cci")
        p1_tail(3)
        p3loads(0)
        pass2_pre(0)
        for n in range(NST):
            for dg in range(DCN):
                pass2_step(0, n, dg)
        pass2_fin(0)
        p3loads(1)
        for c in range(NCH):
            if c + 1 < NCH:
                pass2_pre(c + 1)
                steps = [
                    (lambda cc=c + 1, n=n, dg=dg: pass2_step(cc, n, dg))
                    for n in range(NST) for dg in range(DCN)]
                filler, drain = make_filler(steps)
            else:
                filler, drain = None, lambda: None
            for ei in range(NEXP):
                ffn_expert(c, f"3F{c}", ei, filler=filler)
            drain()
            if c + 1 < NCH:
                pass2_fin(c + 1)
            output_chunk(c)
            if c + 2 < NCH:
                p3loads(c + 2)

        OP("vector", lambda e: e.engine_nop(),
           reads=[], writes=["os0done", "os1done"], tag="final")

        sched.prepare()
        nw = [0]

        @block.sync
        def _(eng):
            nw[0] += sched.emit_engine("sync", eng, semdict)

        @block.scalar
        def _(eng):
            nw[0] += sched.emit_engine("scalar", eng, semdict)

        @block.vector
        def _(eng):
            nw[0] += sched.emit_engine("vector", eng, semdict)

        @block.tensor
        def _(eng):
            nw[0] += sched.emit_engine("tensor", eng, semdict)

        @block.gpsimd
        def _(eng):
            nw[0] += sched.emit_engine("gpsimd", eng, semdict)

        import os
        if os.environ.get("KB_DEBUG"):
            print(f"[kernel] items={len(sched.items)} waits={nw[0]} "
                  f"ninst={nc.n_instructions()}")

    return nc


_PROG_CACHE = {}


def kernel(**inputs):
    x = np.asarray(inputs["L0_attn_x"])
    if "prog" not in _PROG_CACHE:
        _PROG_CACHE["prog"] = build_program()
    nc = _PROG_CACHE["prog"]

    eye = np.eye(128, dtype=np.float32)
    base = {
        "w_inproj": np.ascontiguousarray(inputs["L0_attn_in_proj_w"], np.float32),
        "w_convw": np.ascontiguousarray(inputs["L0_attn_conv_w"], np.float32),
        "w_convb": np.ascontiguousarray(inputs["L0_attn_conv_b"], np.float32),
        "w_xproj": np.ascontiguousarray(inputs["L0_attn_x_proj_w"], np.float32),
        "w_alog": np.ascontiguousarray(inputs["L0_attn_A_log"], np.float32),
        "w_dpar": np.ascontiguousarray(inputs["L0_attn_D_param"], np.float32),
        "w_outproj": np.ascontiguousarray(inputs["L0_attn_out_proj_w"], np.float32),
        "w_router": np.ascontiguousarray(inputs["L0_ffn_router_w"], np.float32),
        "w_ebias": np.ascontiguousarray(inputs["L0_ffn_experts_bias"], np.float32),
        "w_w1": np.ascontiguousarray(inputs["L0_ffn_w1"], np.float32),
        "w_w3": np.ascontiguousarray(inputs["L0_ffn_w3"], np.float32),
        "w_w2": np.ascontiguousarray(inputs["L0_ffn_w2"], np.float32),
        "w_sw1": np.ascontiguousarray(inputs["L0_ffn_sw1"], np.float32),
        "w_sw3": np.ascontiguousarray(inputs["L0_ffn_sw3"], np.float32),
        "w_sw2": np.ascontiguousarray(inputs["L0_ffn_sw2"], np.float32),
        "eye_in": eye,
    }
    in_maps = []
    for core in range(8):
        b, half = core // 2, core % 2
        xm = np.ascontiguousarray(x[b, half * T:(half + 1) * T, :], np.float32)
        if half == 0:
            xh = np.zeros((3, D), np.float32)
        else:
            xh = np.ascontiguousarray(x[b, half * T - 3:half * T, :], np.float32)
        masks = np.zeros((128, 2), np.float32)
        masks[:, 0] = 1.0 - half
        masks[:, 1] = float(half)
        m = dict(base)
        m.update({"x_main": xm, "x_halo": xh, "masks_in": masks})
        in_maps.append(m)
    res = run_bass_kernel_spmd(nc, in_maps, core_ids=list(range(8)))
    out = np.empty((4, 4096, D), np.float32)
    for core in range(8):
        b, half = core // 2, core % 2
        out[b, half * T:(half + 1) * T, :] = res.results[core]["out"]
    return out


# version 16
# speedup vs baseline: 1.1099x; 1.1099x over previous
import numpy as np
import concourse.bass as bass
import concourse.mybir as mybir
from concourse.bass_utils import run_bass_kernel_spmd
from contextlib import ExitStack

F32 = mybir.dt.float32
BF16 = mybir.dt.bfloat16
AF = mybir.ActivationFunctionType
ALU = mybir.AluOpType

D = 1024
DCN = 8
E = 8
NEXP = 9
H = 2048
NHC = 16
NST = 16
T = 2048
TC = 512
NCH = 4
NTT = 4
EPS = 1e-5

ENGINES = ("sync", "scalar", "vector", "tensor", "gpsimd")
DMA_Q = {"sync": "qs", "scalar": "qa", "gpsimd": "qg", "vector": "qv"}
NLANES = 8
EOALL = ("eoA", "eoX")


class Sched:

    def __init__(self):
        self.items = []
        self.writers = {}
        self.readers = {}

    def op(self, engine, fn, reads=(), writes=(), dma=False, tag=""):
        idx = len(self.items)
        deps = set()
        for r in reads:
            deps.update(self.writers.get(r, ()))
        for r in writes:
            deps.update(self.writers.get(r, ()))
            deps.update(self.readers.get(r, ()))
        self.items.append(dict(engine=engine, fn=fn, deps=deps, dma=dma,
                               tag=tag, needs_inc=False))
        for r in reads:
            self.readers.setdefault(r, []).append(idx)
        for r in writes:
            self.writers[r] = [idx]
            self.readers[r] = []
        return idx

    @staticmethod
    def _queue(it):
        return DMA_Q[it["engine"]] if it["dma"] else None

    @staticmethod
    def _chan(it):
        if it["dma"]:
            return DMA_Q[it["engine"]] + str(it.get("lane", 0))
        return it["engine"]

    MARGIN = 24
    SYNC_CHANS = ("vector", "scalar", "gpsimd")

    def prepare(self):
        qseq = {}
        lane_prev = {}
        for i, it in enumerate(self.items):
            q = self._queue(it)
            if q is not None:
                k = qseq.get(q, 0)
                qseq[q] = k + 1
                it["lane"] = k % NLANES
                key = (q, it["lane"])
                if key in lane_prev:
                    it["deps"].add(lane_prev[key])
                lane_prev[key] = i
        seq = {}
        for it in self.items:
            c = self._chan(it)
            it["cseq"] = seq.get(c, 0)
            seq[c] = it["cseq"] + 1
        for it in self.items:
            for d in it["deps"]:
                dit = self.items[d]
                c = self._chan(dit)
                if c != self._chan(it):
                    dit["needs_inc"] = True
                elif (c in self.SYNC_CHANS and not dit["dma"] and not it["dma"]
                      and it["cseq"] - dit["cseq"] <= self.MARGIN):
                    dit["needs_inc"] = True
                    it.setdefault("close_deps", []).append(d)
        for it in self.items:
            if it["dma"]:
                it["needs_inc"] = True
        counters = {}
        for it in self.items:
            c = self._chan(it)
            if it["needs_inc"]:
                counters[c] = counters.get(c, 0) + 1
                it["inc_val"] = counters[c] * (16 if it["dma"] else 1)
            else:
                it["inc_val"] = None

    def emit_engine(self, ename, eng, sems):
        hiwater = {}
        nwaits = 0
        for it in self.items:
            if it["engine"] != ename:
                continue
            mychan = self._chan(it)
            want = {}
            close = set(it.get("close_deps", ()))
            for d in it["deps"]:
                dit = self.items[d]
                c = self._chan(dit)
                if (c == mychan and not it["dma"] and not dit["dma"]
                        and d not in close):
                    continue
                if dit["inc_val"] is None:
                    raise RuntimeError(f"dep without inc: {dit['tag']}")
                want[c] = max(want.get(c, 0), dit["inc_val"])
            for c, v in sorted(want.items()):
                if hiwater.get(c, 0) >= v:
                    continue
                eng.wait_ge(sems[c], v)
                hiwater[c] = v
                nwaits += 1
            inst = it["fn"](eng)
            if it["inc_val"] is not None:
                if inst is None:
                    raise RuntimeError(f"needs inc, no inst: {it['tag']}")
                inst.then_inc(sems[mychan], 16 if it["dma"] else 1)
        return nwaits


def build_program():
    nc = bass.Bass("TRN2", target_bir_lowering=False)

    def din(name, shape, dt=F32):
        return nc.dram_tensor(name, shape, dt, kind="ExternalInput").ap()

    x_main = din("x_main", [T, D])
    x_halo = din("x_halo", [3, D])
    w_inproj = din("w_inproj", [2 * D, D])
    w_convw = din("w_convw", [D, 4])
    w_convb = din("w_convb", [D])
    w_xproj = din("w_xproj", [33, D])
    w_alog = din("w_alog", [D, NST])
    w_dpar = din("w_dpar", [D])
    w_outproj = din("w_outproj", [D, D])
    w_router = din("w_router", [E, D])
    w_ebias = din("w_ebias", [E])
    w_w1 = din("w_w1", [E, D, H])
    w_w3 = din("w_w3", [E, D, H])
    w_w2 = din("w_w2", [E, H, D])
    w_sw1 = din("w_sw1", [1, D, H])
    w_sw3 = din("w_sw3", [1, D, H])
    w_sw2 = din("w_sw2", [1, H, D])
    eye_in = din("eye_in", [128, 128])
    masks_in = din("masks_in", [128, 2])
    out_d = nc.dram_tensor("out", [T, D], F32, kind="ExternalOutput").ap()

    w1bf_d = nc.dram_tensor("w1bf_d", [NEXP, 128, DCN, H], BF16).ap()
    w3bf_d = nc.dram_tensor("w3bf_d", [NEXP, 128, DCN, H], BF16).ap()
    w2bf_d = nc.dram_tensor("w2bf_d", [NEXP, 8, 128, NHC, 128], BF16).ap()
    iprojT_d = nc.dram_tensor("iprojT_d", [128, DCN, 2 * D], BF16).ap()
    oprojT_d = nc.dram_tensor("oprojT_d", [128, DCN, D], BF16).ap()
    xres_d = nc.dram_tensor("xres_d", [NCH, 128, DCN, TC], F32).ap()
    szbf_d = nc.dram_tensor("szbf_d", [NCH, 128, DCN, TC], BF16).ap()
    rows_d = nc.dram_tensor("rows_d", [NCH, 34, TC], F32).ap()
    lnrows_d = nc.dram_tensor("lnrows_d", [2, TC], F32).ap()
    cwT_d = nc.dram_tensor("cwT_d", [E, T], F32).ap()
    cc_in_d = nc.dram_tensor("cc_in_d", [128, 128], F32).ap()
    cc_out_d = nc.dram_tensor("cc_out_d", [128, 128], F32).ap()
    hrows_d = nc.dram_tensor("hrows_d", [2, TC], F32).ap()
    cw3_d = nc.dram_tensor("cw3_d", [E, 4], F32).ap()

    sched = Sched()

    with ExitStack() as ctx:
        def sb(name, shape, dt=F32):
            return ctx.enter_context(nc.sbuf_tensor(name, shape, dt))

        xres = sb("xres", [128, 2, DCN, TC])
        xssm = sb("xssm", [128, DCN, 515])
        xconv = sb("xconv", [128, DCN, TC])
        yacc = sb("yacc", [128, DCN, TC])
        xnorm = sb("xnorm", [128, DCN, TC], BF16)
        xnm = sb("xnm", [128, DCN, TC], BF16)
        eo = sb("eo", [128, NHC, TC], BF16)
        szbf_w = sb("szbf_w", [128, DCN, TC], BF16)
        sg = sb("sg", [128, 2, TC])
        sge = sb("sge", [128, TC])
        w1b = sb("w1b", [128, 2, DCN, 256], BF16)
        w3b = sb("w3b", [128, 2, DCN, 256], BF16)
        w2b = sb("w2b", [128, 2, NHC, 128], BF16)
        ipb = sb("ipb", [128, 2, DCN, 128], BF16)
        opb = sb("opb", [128, 2, DCN, 128], BF16)
        dAt = sb("dAt", [128, 2, TC])
        ut = sb("ut", [128, TC])
        ut2 = sb("ut2", [128, TC])
        hst = sb("hst", [128, TC])
        bc = sb("bc", [128, 3, TC])
        cwb = sb("cwb", [128, 2, TC])
        scb = sb("scb", [128, 2, TC])
        scanrows = sb("scanrows", [48, TC])
        m_row = sb("m_row", [1, TC])
        q_row = sb("q_row", [1, TC])
        mr_row = sb("mr_row", [1, TC])
        drow = sb("drow", [1, TC])
        srow = q_row
        scar = sb("scar", [1, 2])
        ones_c = sb("ones_c", [128, 1])
        ones_bf = sb("ones_bf", [1, 128], BF16)
        hstate = sb("hstate", [128, DCN, NST])
        hin = sb("hin", [128, DCN, NST])
        ccs = sb("ccs", [128, 128])
        A_sb = sb("A_sb", [128, DCN, NST])
        convw = sb("convw", [128, DCN, 4])
        convb = sb("convb", [128, DCN])
        D_sb = sb("D_sb", [128, DCN])
        masks = sb("masks", [128, 2])
        cvals = sb("cvals", [128, 2])
        eye = sb("eye", [128, 128])
        eye_bf = sb("eye_bf", [128, 128], BF16)
        xprojT = sb("xprojT", [128, DCN, 80], BF16)
        routerT = sb("routerT", [128, DCN, E], BF16)
        ebias_r = sb("ebias_r", [1, E])
        ebias_bf = sb("ebias_bf", [1, E], BF16)
        cw_tm = sb("cw_tm", [128, 4, E])
        cwT_sb = sb("cwT_sb", [8, TC])
        tmp_r = sb("tmp_r", [128, 4, E])
        tmp_r2 = sb("tmp_r2", [128, 4, E])
        m1c = sb("m1c", [128, 4])
        m2c = sb("m2c", [128, 4])
        s1c = sb("s1c", [128, 4])
        s2c = sb("s2c", [128, 4])
        x3f = sb("x3f", [128, DCN, 4])
        x3n = sb("x3n", [128, DCN, 4], BF16)
        eo3 = sb("eo3", [128, NHC, 4], BF16)
        sg3 = sb("sg3", [128, 4])
        y3 = sb("y3", [128, DCN, 4])
        cw3_tm = sb("cw3_tm", [3, E])
        cw3T = sb("cw3T", [8, 4])
        cwb3 = sb("cwb3", [128, 4])
        tail3 = sb("tail3", [128, DCN, 3])

        eo_f = eo.ap().bitcast(F32)
        eo_flat = eo_f.rearrange("p a b -> p (a b)")
        eo_bflat = eo.ap().rearrange("p a b -> p (a b)")
        xs_flat = xssm.ap().rearrange("p a b -> p (a b)")
        xs_bflat = xssm.ap().bitcast(BF16).rearrange("p a b -> p (a b)")
        xc_flat = xconv.ap().rearrange("p a b -> p (a b)")
        xc_bflat = xconv.ap().bitcast(BF16).rearrange("p a b -> p (a b)")
        ya_flat = yacc.ap().rearrange("p a b -> p (a b)")
        ya_bflat = yacc.ap().bitcast(BF16).rearrange("p a b -> p (a b)")
        sz_bflat = szbf_w.ap().rearrange("p a b -> p (a b)")
        xn_bflat = xnorm.ap().rearrange("p a b -> p (a b)")
        band_f32 = [sz_bflat.bitcast(F32), xs_flat[:, 0:2048],
                    xc_flat[:, 0:2048], ya_flat[:, 0:2048]]
        band_bf = [xn_bflat[:, 0:2048], xs_bflat[:, 4096:6144],
                   xc_bflat[:, 4096:6144], ya_bflat[:, 4096:6144]]
        band_tag = [["szbf_w", "xnm"], ["xssm"], ["xconv"], ["yacc"]]
        xconv_bf = ya_bflat[:, 0:4096].rearrange("p (dc t) -> p dc t", dc=DCN)
        xtm = eo_flat[:, 3072:4096]
        xproj_sb = eo_flat[0:33, 0:1024]
        router_sb = eo_flat[0:8, 1024:2048]
        x3 = eo_flat[0:3, 0:1024]
        outstg = xs_flat[:, 0:2048].rearrange(
            "p (b x) -> p b x", b=2)

        pg = [ctx.enter_context(nc.psum_tensor(f"pg{i}", [128, TC], F32))
              for i in range(4)]
        pw = [ctx.enter_context(nc.psum_tensor(f"pw{i}", [128, TC], F32))
              for i in range(2)]
        pstat = ctx.enter_context(nc.psum_tensor("pstat", [128, TC], F32))
        pbf = ctx.enter_context(nc.psum_tensor("pbf", [128, 1024], BF16))

        semdict = {}
        for s in ("sync", "scalar", "vector", "tensor", "gpsimd"):
            semdict[s] = ctx.enter_context(nc.semaphore("sem_" + s))
        for q in ("qs", "qa", "qg", "qv"):
            for ln in range(NLANES):
                semdict[q + str(ln)] = ctx.enter_context(
                    nc.semaphore(f"sem_{q}{ln}"))

        block = ctx.enter_context(nc.Block())
        OP = sched.op

        def dma(qeng, out, in_, reads, writes, tag="", slow=False):
            if slow:
                OP(qeng, lambda e, o=out, i=in_: e.dma_start(
                    out=o, in_=i, allow_slow_non_contiguous=True),
                   reads=reads, writes=writes, dma=True, tag=tag)
            else:
                OP(qeng, lambda e, o=out, i=in_: e.dma_start(out=o, in_=i),
                   reads=reads, writes=writes, dma=True, tag=tag)

        dma("sync", eye[:], eye_in, [], ["eye"], "eye")
        dma("sync", masks[:], masks_in, [], ["masks"], "masks")
        dma("sync", A_sb[:], w_alog.rearrange("(dc dp) n -> dp dc n", dp=128),
            [], ["A"], "alog")
        dma("sync", convw[:], w_convw.rearrange("(dc dp) k -> dp dc k", dp=128),
            [], ["convw"], "convw")
        dma("sync", convb[:], w_convb.rearrange("(dc dp) -> dp dc", dp=128),
            [], ["convb"], "convb", slow=True)
        dma("sync", D_sb[:], w_dpar.rearrange("(dc dp) -> dp dc", dp=128),
            [], ["Dsb"], "dpar", slow=True)
        dma("sync", ebias_r[:], w_ebias.rearrange("(o e) -> o e", o=1), [], ["ebias"], "eb")
        OP("scalar", lambda e: e.activation(out=A_sb[:], in_=A_sb[:], func=AF.Exp),
           ["A"], ["A"], tag="a_exp")
        OP("scalar", lambda e: e.mul(out=A_sb[:], in_=A_sb[:], mul=-1.0),
           ["A"], ["A"], tag="a_neg")
        OP("scalar", lambda e: e.copy(out=ebias_bf[:], in_=ebias_r[:]),
           ["ebias"], ["ebias_bf"], tag="ebbf")
        OP("scalar", lambda e: e.copy(out=eye_bf[:], in_=eye[:]),
           ["eye"], ["eye_bf"], tag="eyebf")
        OP("vector", lambda e: e.memset(ones_c[:], 1.0), [], ["ones_c"], tag="o2")
        OP("vector", lambda e: e.memset(ones_bf[:], 1.0), [], ["ones_bf"], tag="o3")
        OP("vector", lambda e: e.memset(scar[:], 0.0), [], ["scar"], tag="o4")
        OP("vector", lambda e: e.memset(hstate[:], 0.0), [], ["hstate"], tag="o5")
        OP("vector", lambda e: e.memset(cvals[:, 0:1], EPS), [], ["cvals"], tag="o6")
        OP("vector", lambda e: e.memset(cvals[:, 1:2], 0.5), [], ["cvals"], tag="o7")

        OP("vector", lambda e: e.memset(xprojT[:], 0.0), [], ["xprojT"], tag="xp0")
        dma("sync", xproj_sb, w_xproj, ["eoA"], ["eoA"], "xpl")
        for dc in range(DCN):
            OP("tensor", lambda e, dc=dc: e.transpose(
                out=pstat[:, 0:33], in_=xproj_sb[:, dc * 128:(dc + 1) * 128],
                identity=eye[0:33, 0:33]),
               ["eoA", "eye", "pstat"], ["pstat"], tag=f"xpT{dc}")
            OP("vector", lambda e, dc=dc: e.tensor_copy(
                out=xprojT[:, dc, 0:16], in_=pstat[:, 0:16]),
               ["pstat", "xprojT"], ["xprojT"], tag=f"xpE1{dc}")
            OP("vector", lambda e, dc=dc: e.tensor_copy(
                out=xprojT[:, dc, 32:48], in_=pstat[:, 16:32]),
               ["pstat", "xprojT"], ["xprojT"], tag=f"xpE2{dc}")
            OP("vector", lambda e, dc=dc: e.tensor_copy(
                out=xprojT[:, dc, 64:65], in_=pstat[:, 32:33]),
               ["pstat", "xprojT"], ["xprojT"], tag=f"xpE3{dc}")
        dma("sync", router_sb, w_router, ["eoA"], ["eoA"], "rtl")
        for dc in range(DCN):
            OP("tensor", lambda e, dc=dc: e.transpose(
                out=pstat[:, 40:48], in_=router_sb[:, dc * 128:(dc + 1) * 128],
                identity=eye[0:8, 0:8]),
               ["eoA", "eye", "pstat"], ["pstat"], tag=f"rtT{dc}")
            OP("vector", lambda e, dc=dc: e.tensor_copy(
                out=routerT[:, dc, :], in_=pstat[:, 40:48]),
               ["pstat", "routerT"], ["routerT"], tag=f"rtE{dc}")

        for c in range(NCH):
            s = c % 2
            for tt in range(NTT):
                trow = c * TC + tt * 128
                dma("sync", xtm, x_main[trow:trow + 128, :],
                    ["eoX"], ["eoX"], f"xl{c}{tt}")
                for dc in range(DCN):
                    OP("tensor", lambda e, dc=dc: e.transpose(
                        out=pstat[:, 128:256], in_=xtm[:, dc * 128:(dc + 1) * 128],
                        identity=eye[:]),
                       ["eoX", "eye", "pstat"], ["pstat"], tag=f"xT{c}{tt}{dc}")
                    OP("vector", lambda e, s=s, tt=tt, dc=dc: e.tensor_copy(
                        out=xres[:, s, dc, tt * 128:(tt + 1) * 128],
                        in_=pstat[:, 128:256]),
                       ["pstat", f"xres{s}"], [f"xres{s}"], tag=f"xE{c}{tt}{dc}")
            dma("gpsimd", xres_d[c], xres[:, s], [f"xres{s}"],
                [f"xres{s}", f"xrd{c}"], f"xs{c}")

        band_i = [0]

        def band_unit(load_src, nrows, ncols_f32, store_fn, dres, ename):
            b = band_i[0] % 4
            band_i[0] += 1
            tags = band_tag[b]
            qload = "sync" if b % 2 == 0 else "gpsimd"
            fsrc = band_f32[b][0:nrows, 0:ncols_f32]
            bsrc = band_bf[b][0:nrows, 0:ncols_f32]
            dma(qload, fsrc, load_src, tags, tags, f"{ename}l")
            OP("scalar", lambda e, fsrc=fsrc, bsrc=bsrc: e.copy(
                out=bsrc, in_=fsrc), tags, tags, tag=f"{ename}c")
            store_fn(bsrc, tags)

        def prep_ffn(wsrc, dst, dres, ename):
            for dc in range(DCN):
                def store(bsrc, tags, dres=dres, dc=dc, ename=ename):
                    dma("scalar", dst[:, dc:dc + 1, :], bsrc.unsqueeze(1),
                        tags, list(tags) + [dres], f"{ename}s{dc}")
                band_unit(wsrc[dc * 128:(dc + 1) * 128, :], 128, H,
                          store, dres, f"{ename}{dc}")

        def prep_w2(eidx, wsrc, dres, ename):
            for hc in range(NHC):
                def store(bsrc, tags, dres=dres, hc=hc, ename=ename):
                    dma("scalar",
                        w2bf_d[:, :, :, hc, :][eidx].transpose([1, 0, 2]),
                        bsrc.rearrange("p (dp ds) -> p dp ds", dp=8),
                        tags, list(tags) + [dres], f"{ename}s{hc}")
                band_unit(wsrc[hc * 128:(hc + 1) * 128, :], 128, D,
                          store, dres, f"{ename}{hc}")

        def prep_expert(ei):
            if ei < E:
                prep_ffn(w_w1[ei], w1bf_d[ei], f"w1d{ei}", f"pw1e{ei}")
                prep_ffn(w_w3[ei], w3bf_d[ei], f"w3d{ei}", f"pw3e{ei}")
                prep_w2(ei, w_w2[ei], f"w2d{ei}", f"pw2e{ei}")
            else:
                prep_ffn(w_sw1[0], w1bf_d[E], f"w1d{E}", "psw1")
                prep_ffn(w_sw3[0], w3bf_d[E], f"w3d{E}", "psw3")
                prep_w2(E, w_sw2[0], f"w2d{E}", "psw2")

        prep_expert(0)
        prep_expert(1)

        def prep_transposed(wsrc, n_rows, dst, dres, ename):
            for ob in range(n_rows // 128):
                b = band_i[0] % 4
                band_i[0] += 1
                tags = band_tag[b]
                qload = "sync" if b % 2 == 0 else "gpsimd"
                fsrc = band_f32[b][:, 0:1024]
                bsrc = band_bf[b][:, 0:1024]
                btmp = band_bf[b][:, 1024:2048]
                dma(qload, fsrc, wsrc[ob * 128:(ob + 1) * 128, :],
                    tags, tags, f"{ename}l{ob}")
                OP("scalar", lambda e, fsrc=fsrc, bsrc=bsrc: e.copy(
                    out=bsrc, in_=fsrc), tags, tags, tag=f"{ename}c{ob}")
                for dc in range(DCN):
                    OP("tensor", lambda e, dc=dc, bsrc=bsrc: e.transpose(
                        out=pbf[:, (dc % 8) * 128:(dc % 8) * 128 + 128],
                        in_=bsrc[:, dc * 128:(dc + 1) * 128],
                        identity=eye_bf[:]),
                       list(tags) + ["eye_bf", "pbf"], ["pbf"],
                       tag=f"{ename}t{ob}{dc}")
                OP("vector", lambda e, btmp=btmp: e.tensor_copy(
                    out=btmp, in_=pbf[:, 0:1024]),
                   ["pbf"] + list(tags), tags, tag=f"{ename}e{ob}")
                dma("scalar", dst[:, :, ob * 128:(ob + 1) * 128],
                    btmp.rearrange("p (dc x) -> p dc x", dc=DCN),
                    tags, list(tags) + [dres], f"{ename}s{ob}")

        prep_transposed(w_inproj, 2 * D, iprojT_d, "ipd", "pip")
        prep_expert(2)
        prep_expert(3)
        prep_transposed(w_outproj, D, oprojT_d, "opd", "pop")
        for _pe in range(4, NEXP):
            prep_expert(_pe)

        def ln_chunk(c, nametag, outbuf=None, outres="xnorm"):
            if outbuf is None:
                outbuf = xnorm
            s = c % 2
            xr = f"xres{s}"
            for dc in range(DCN):
                OP("tensor", lambda e, dc=dc, s=s: e.matmul(
                    out=pstat[0:1, :], lhsT=ones_c[:], rhs=xres[:, s, dc, :],
                    start=(dc == 0), stop=(dc == DCN - 1)),
                   [xr, "ones_c", "pstat"], ["pstat"], tag=f"{nametag}sm{dc}")
            for dc in range(DCN):
                sb_i = dc % 2
                OP("vector", lambda e, dc=dc, s=s, sb_i=sb_i: e.tensor_tensor(
                    out=sg[:, sb_i, :], in0=xres[:, s, dc, :],
                    in1=xres[:, s, dc, :], op=ALU.mult),
                   [xr, f"sg{sb_i}"], [f"sg{sb_i}"], tag=f"{nametag}sq{dc}")
                OP("tensor", lambda e, dc=dc, sb_i=sb_i: e.matmul(
                    out=pw[0][0:1, :], lhsT=ones_c[:], rhs=sg[:, sb_i, :],
                    start=(dc == 0), stop=(dc == DCN - 1)),
                   [f"sg{sb_i}", "ones_c", "pw0"], ["pw0"], tag=f"{nametag}sk{dc}")
            OP("scalar", lambda e: e.activation(out=m_row[:], in_=pstat[0:1, :],
                                                func=AF.Copy, scale=1.0 / D),
               ["pstat", "m_row"], ["m_row"], tag=f"{nametag}m")
            OP("scalar", lambda e: e.activation(out=q_row[:], in_=pw[0][0:1, :],
                                                func=AF.Copy, scale=1.0 / D),
               ["pw0", "q_row"], ["q_row"], tag=f"{nametag}q")
            OP("vector", lambda e: e.tensor_tensor(
                out=mr_row[:], in0=m_row[:], in1=m_row[:], op=ALU.mult),
               ["m_row", "mr_row"], ["mr_row"], tag=f"{nametag}mm")
            OP("vector", lambda e: e.tensor_tensor(
                out=q_row[:], in0=q_row[:], in1=mr_row[:], op=ALU.subtract),
               ["q_row", "mr_row"], ["q_row"], tag=f"{nametag}v")
            OP("scalar", lambda e: e.activation(out=q_row[:], in_=q_row[:],
                                                func=AF.Ln, bias=cvals[0:1, 0:1]),
               ["q_row", "cvals"], ["q_row"], tag=f"{nametag}ln")
            OP("scalar", lambda e: e.activation(out=q_row[:], in_=q_row[:],
                                                func=AF.Exp, scale=-0.5),
               ["q_row"], ["q_row"], tag=f"{nametag}ex")
            OP("vector", lambda e: e.tensor_tensor(
                out=mr_row[:], in0=m_row[:], in1=q_row[:], op=ALU.mult),
               ["m_row", "q_row", "mr_row"], ["mr_row"], tag=f"{nametag}mr")
            dma("gpsimd", lnrows_d[0:1, :], q_row[:], ["q_row"], ["lnr_d"],
                f"{nametag}wr")
            dma("gpsimd", lnrows_d[1:2, :], mr_row[:], ["mr_row"], ["lnr_d"],
                f"{nametag}wm")
            dma("gpsimd", bc[:, 0, :], lnrows_d[0:1, :].partition_broadcast(128),
                ["lnr_d", "bc0"], ["bc0"], f"{nametag}br")
            dma("gpsimd", bc[:, 1, :], lnrows_d[1:2, :].partition_broadcast(128),
                ["lnr_d", "bc1"], ["bc1"], f"{nametag}bm")
            for dc in range(DCN):
                OP("vector", lambda e, dc=dc, s=s: e.tensor_tensor(
                    out=sge[:], in0=xres[:, s, dc, :], in1=bc[:, 0, :],
                    op=ALU.mult),
                   [xr, "bc0", "sge"], ["sge"], tag=f"{nametag}n1{dc}")
                OP("vector", lambda e, dc=dc, outbuf=outbuf: e.tensor_tensor(
                    out=outbuf[:, dc, :], in0=sge[:], in1=bc[:, 1, :],
                    op=ALU.subtract),
                   ["sge", "bc1", outres], [outres], tag=f"{nametag}n2{dc}")

        def router_chunk(c, nametag):
            for tt in range(NTT):
                for dc in range(DCN):
                    OP("tensor", lambda e, tt=tt, dc=dc: e.matmul(
                        out=pstat[:, tt * 8:tt * 8 + 8],
                        lhsT=xnorm[:, dc, tt * 128:(tt + 1) * 128],
                        rhs=routerT[:, dc, :], start=(dc == 0), stop=False),
                       ["xnorm", "routerT", "pstat"], ["pstat"],
                       tag=f"{nametag}lg{tt}{dc}")
                OP("tensor", lambda e, tt=tt: e.matmul(
                    out=pstat[:, tt * 8:tt * 8 + 8], lhsT=ones_bf[:],
                    rhs=ebias_bf[:], start=False, stop=True),
                   ["ones_bf", "ebias_bf", "pstat"], ["pstat"],
                   tag=f"{nametag}lb{tt}")
            OP("vector", lambda e: e.tensor_copy(
                out=cw_tm.ap().rearrange("p a e -> p (a e)"), in_=pstat[:, 0:32]),
               ["pstat", "cw_tm"], ["cw_tm"], tag=f"{nametag}cp")
            OP("vector", lambda e: e.tensor_reduce(
                out=m1c[:], in_=cw_tm[:], axis=mybir.AxisListType.X, op=ALU.max),
               ["cw_tm", "m1c"], ["m1c"], tag=f"{nametag}m1")
            for a in range(4):
                OP("vector", lambda e, a=a: e.tensor_scalar(
                    out=tmp_r[:, a, :], in0=cw_tm[:, a, :], scalar1=m1c[:, a:a + 1],
                    scalar2=None, op0=ALU.is_ge),
                   ["cw_tm", "m1c", "tmp_r"], ["tmp_r"], tag=f"{nametag}e1{a}")
            OP("vector", lambda e: e.scalar_tensor_tensor(
                out=tmp_r2.ap().rearrange("p a e -> p (a e)"),
                in0=tmp_r.ap().rearrange("p a e -> p (a e)"), scalar=-1e30,
                in1=cw_tm.ap().rearrange("p a e -> p (a e)"),
                op0=ALU.mult, op1=ALU.add),
               ["tmp_r", "cw_tm", "tmp_r2"], ["tmp_r2"], tag=f"{nametag}l2")
            OP("vector", lambda e: e.tensor_reduce(
                out=m2c[:], in_=tmp_r2[:], axis=mybir.AxisListType.X, op=ALU.max),
               ["tmp_r2", "m2c"], ["m2c"], tag=f"{nametag}m2")
            for a in range(4):
                OP("vector", lambda e, a=a: e.tensor_scalar(
                    out=tmp_r2[:, a, :], in0=tmp_r2[:, a, :],
                    scalar1=m2c[:, a:a + 1], scalar2=None, op0=ALU.is_ge),
                   ["tmp_r2", "m2c"], ["tmp_r2"], tag=f"{nametag}e2{a}")
            OP("vector", lambda e: e.tensor_tensor(
                out=s1c[:], in0=m1c[:], in1=m2c[:], op=ALU.subtract),
               ["m1c", "m2c", "s1c"], ["s1c"], tag=f"{nametag}d")
            OP("scalar", lambda e: e.activation(out=s1c[:], in_=s1c[:],
                                                func=AF.Sigmoid),
               ["s1c"], ["s1c"], tag=f"{nametag}sg")
            OP("scalar", lambda e: e.activation(out=s2c[:], in_=s1c[:],
                                                func=AF.Copy, scale=-0.5,
                                                bias=0.5),
               ["s1c", "s2c"], ["s2c"], tag=f"{nametag}s2")
            OP("scalar", lambda e: e.activation(out=s1c[:], in_=s1c[:],
                                                func=AF.Copy, scale=0.5),
               ["s1c"], ["s1c"], tag=f"{nametag}s1")
            for a in range(4):
                OP("vector", lambda e, a=a: e.tensor_scalar(
                    out=tmp_r[:, a, :], in0=tmp_r[:, a, :], scalar1=s1c[:, a:a + 1],
                    scalar2=None, op0=ALU.mult),
                   ["tmp_r", "s1c"], ["tmp_r"], tag=f"{nametag}h1{a}")
                OP("vector", lambda e, a=a: e.tensor_scalar(
                    out=tmp_r2[:, a, :], in0=tmp_r2[:, a, :],
                    scalar1=s2c[:, a:a + 1], scalar2=None, op0=ALU.mult),
                   ["tmp_r2", "s2c"], ["tmp_r2"], tag=f"{nametag}h2{a}")
            OP("vector", lambda e: e.tensor_tensor(
                out=cw_tm.ap().rearrange("p a e -> p (a e)"),
                in0=tmp_r.ap().rearrange("p a e -> p (a e)"),
                in1=tmp_r2.ap().rearrange("p a e -> p (a e)"), op=ALU.add),
               ["tmp_r", "tmp_r2", "cw_tm"], ["cw_tm"], tag=f"{nametag}cw")
            for tt in range(NTT):
                OP("tensor", lambda e, tt=tt: e.transpose(
                    out=pw[1][0:8, tt * 128:(tt + 1) * 128],
                    in_=cw_tm[:, tt, :], identity=eye[:]),
                   ["cw_tm", "eye", "pw1"], ["pw1"], tag=f"{nametag}T{tt}")
            OP("vector", lambda e: e.tensor_copy(out=cwT_sb[:],
                                                 in_=pw[1][0:8, :]),
               ["pw1", "cwT_sb"], ["cwT_sb"], tag=f"{nametag}Te")
            dma("gpsimd", cwT_d[:, c * TC:(c + 1) * TC], cwT_sb[:],
                ["cwT_sb"], ["cwT_d"], f"{nametag}Ts")

        def ffn_expert(c, nametag, ei, filler=None):
            def fill():
                if filler:
                    filler()
            s = c % 2
            xr = f"xres{s}"
            if True:
                for blk in range(8):
                    b = blk % 2
                    cols = slice(blk * 256, (blk + 1) * 256)
                    dma("sync", w1b[:, b], w1bf_d[ei][:, :, cols],
                        [f"w1b{b}", f"w1d{ei}"], [f"w1b{b}"], f"{nametag}a{ei}{blk}")
                    dma("sync", w3b[:, b], w3bf_d[ei][:, :, cols],
                        [f"w3b{b}", f"w3d{ei}"], [f"w3b{b}"], f"{nametag}b{ei}{blk}")
                    for sub in range(2):
                        hc = blk * 2 + sub
                        pgi = hc % 2
                        pui = 2 + hc % 2
                        for dc in range(DCN):
                            OP("tensor", lambda e, b=b, sub=sub, dc=dc, pgi=pgi:
                               e.matmul(
                                   out=pg[pgi][:],
                                   lhsT=w1b[:, b, dc, sub * 128:(sub + 1) * 128],
                                   rhs=xnorm[:, dc, :], start=(dc == 0),
                                   stop=(dc == DCN - 1)),
                               [f"w1b{b}", "xnorm", f"pg{pgi}"], [f"pg{pgi}"],
                               tag=f"{nametag}g{ei}{hc}{dc}")
                        for dc in range(DCN):
                            OP("tensor", lambda e, b=b, sub=sub, dc=dc, pui=pui:
                               e.matmul(
                                   out=pg[pui][:],
                                   lhsT=w3b[:, b, dc, sub * 128:(sub + 1) * 128],
                                   rhs=xnorm[:, dc, :], start=(dc == 0),
                                   stop=(dc == DCN - 1)),
                               [f"w3b{b}", "xnorm", f"pg{pui}"], [f"pg{pui}"],
                               tag=f"{nametag}w{ei}{hc}{dc}")
                        OP("scalar", lambda e, pgi=pgi, hc=hc: e.activation(
                            out=sg[:, hc % 2, :], in_=pg[pgi][:], func=AF.Silu),
                           [f"pg{pgi}", f"sg{hc % 2}"], [f"sg{hc % 2}"],
                           tag=f"{nametag}s{ei}{hc}")
                        OP("vector", lambda e, pui=pui, hc=hc: e.tensor_tensor(
                            out=eo[:, hc, :], in0=sg[:, hc % 2, :], in1=pg[pui][:],
                            op=ALU.mult),
                           [f"sg{hc % 2}", f"pg{pui}"] + list(EOALL), list(EOALL),
                           tag=f"{nametag}e{ei}{hc}")
                        fill()
                if ei < E:
                    dma("gpsimd", cwb[:, ei % 2, :],
                        cwT_d[ei:ei + 1, c * TC:(c + 1) * TC].partition_broadcast(128),
                        ["cwT_d", f"cwb{ei % 2}"], [f"cwb{ei % 2}"],
                        f"{nametag}c{ei}")
                for dp in range(8):
                    b = dp % 2
                    dma("sync", w2b[:, b], w2bf_d[ei, dp],
                        [f"w2b{b}", f"w2d{ei}"], [f"w2b{b}"], f"{nametag}d{ei}{dp}")
                    pwi = dp % 2
                    for hc in range(NHC):
                        OP("tensor", lambda e, b=b, hc=hc, pwi=pwi: e.matmul(
                            out=pw[pwi][:], lhsT=w2b[:, b, hc, :], rhs=eo[:, hc, :],
                            start=(hc == 0), stop=(hc == NHC - 1)),
                           [f"w2b{b}", f"pw{pwi}"] + list(EOALL), [f"pw{pwi}"],
                           tag=f"{nametag}2{ei}{dp}{hc}")
                    if ei < E:
                        OP("vector", lambda e, pwi=pwi, ei=ei: e.tensor_tensor(
                            out=sge[:], in0=pw[pwi][:], in1=cwb[:, ei % 2, :],
                            op=ALU.mult),
                           [f"pw{pwi}", f"cwb{ei % 2}", "sge"], ["sge"],
                           tag=f"{nametag}v{ei}{dp}")
                    else:
                        OP("vector", lambda e, pwi=pwi: e.tensor_scalar(
                            out=sge[:], in0=pw[pwi][:], scalar1=0.5, scalar2=None,
                            op0=ALU.mult),
                           [f"pw{pwi}", "sge"], ["sge"], tag=f"{nametag}v{ei}{dp}")
                    OP("vector", lambda e, dp=dp, s=s: e.tensor_tensor(
                        out=xres[:, s, dp, :], in0=xres[:, s, dp, :], in1=sge[:],
                        op=ALU.add),
                       ["sge", xr], [xr], tag=f"{nametag}x{ei}{dp}")
                    fill()

        def out_proj_chunk(c, nametag):
            s = c % 2
            xr = f"xres{s}"
            for ep in range(8):
                b = ep % 2
                dma("sync", opb[:, b], oprojT_d[:, :, ep * 128:(ep + 1) * 128],
                    [f"opb{b}", "opd"], [f"opb{b}"], f"{nametag}l{ep}")
                pwi = ep % 2
                for dc in range(DCN):
                    OP("tensor", lambda e, b=b, dc=dc, pwi=pwi: e.matmul(
                        out=pw[pwi][:], lhsT=opb[:, b, dc, :], rhs=xnm[:, dc, :],
                        start=(dc == 0), stop=(dc == DCN - 1)),
                       [f"opb{b}", "xnm", f"pw{pwi}"], [f"pw{pwi}"],
                       tag=f"{nametag}m{ep}{dc}")
                OP("vector", lambda e, ep=ep, pwi=pwi, s=s: e.tensor_tensor(
                    out=xres[:, s, ep, :], in0=xres[:, s, ep, :], in1=pw[pwi][:],
                    op=ALU.add),
                   [f"pw{pwi}", xr], [xr], tag=f"{nametag}e{ep}")

        def halo_ln(dst_is_x3n, nametag):
            for dc in range(DCN):
                OP("tensor", lambda e, dc=dc: e.matmul(
                    out=pstat[0:1, 8:11], lhsT=ones_c[:], rhs=x3f[:, dc, 0:3],
                    start=(dc == 0), stop=(dc == DCN - 1)),
                   ["x3f", "ones_c", "pstat"], ["pstat"], tag=f"{nametag}s{dc}")
            for dc in range(DCN):
                OP("vector", lambda e, dc=dc: e.tensor_tensor(
                    out=sg3[:, 0:3], in0=x3f[:, dc, 0:3], in1=x3f[:, dc, 0:3],
                    op=ALU.mult),
                   ["x3f", "sg3"], ["sg3"], tag=f"{nametag}q{dc}")
                OP("tensor", lambda e, dc=dc: e.matmul(
                    out=pstat[0:1, 12:15], lhsT=ones_c[:], rhs=sg3[:, 0:3],
                    start=(dc == 0), stop=(dc == DCN - 1)),
                   ["sg3", "ones_c", "pstat"], ["pstat"], tag=f"{nametag}k{dc}")
            OP("scalar", lambda e: e.activation(
                out=m_row[0:1, 0:3], in_=pstat[0:1, 8:11], func=AF.Copy,
                scale=1.0 / D), ["pstat", "m_row"], ["m_row"], tag=f"{nametag}m")
            OP("scalar", lambda e: e.activation(
                out=q_row[0:1, 0:3], in_=pstat[0:1, 12:15], func=AF.Copy,
                scale=1.0 / D), ["pstat", "q_row"], ["q_row"], tag=f"{nametag}qq")
            OP("vector", lambda e: e.tensor_tensor(
                out=mr_row[0:1, 0:3], in0=m_row[0:1, 0:3], in1=m_row[0:1, 0:3],
                op=ALU.mult), ["m_row", "mr_row"], ["mr_row"], tag=f"{nametag}mm")
            OP("vector", lambda e: e.tensor_tensor(
                out=q_row[0:1, 0:3], in0=q_row[0:1, 0:3], in1=mr_row[0:1, 0:3],
                op=ALU.subtract), ["q_row", "mr_row"], ["q_row"], tag=f"{nametag}v")
            OP("scalar", lambda e: e.activation(out=q_row[0:1, 0:3],
                                                in_=q_row[0:1, 0:3],
                                                func=AF.Ln, bias=cvals[0:1, 0:1]),
               ["q_row", "cvals"], ["q_row"], tag=f"{nametag}l")
            OP("scalar", lambda e: e.activation(out=q_row[0:1, 0:3],
                                                in_=q_row[0:1, 0:3],
                                                func=AF.Exp, scale=-0.5),
               ["q_row"], ["q_row"], tag=f"{nametag}x")
            OP("vector", lambda e: e.tensor_tensor(
                out=mr_row[0:1, 0:3], in0=m_row[0:1, 0:3], in1=q_row[0:1, 0:3],
                op=ALU.mult), ["m_row", "q_row", "mr_row"], ["mr_row"],
               tag=f"{nametag}mr")
            dma("gpsimd", hrows_d[0:1, 0:3], q_row[0:1, 0:3], ["q_row"],
                ["hrows_d"], f"{nametag}w1")
            dma("gpsimd", hrows_d[1:2, 0:3], mr_row[0:1, 0:3], ["mr_row"],
                ["hrows_d"], f"{nametag}w2")
            dma("gpsimd", bc[:, 0, 0:3], hrows_d[0:1, 0:3].partition_broadcast(128),
                ["hrows_d", "bc0"], ["bc0"], f"{nametag}b1")
            dma("gpsimd", bc[:, 1, 0:3], hrows_d[1:2, 0:3].partition_broadcast(128),
                ["hrows_d", "bc1"], ["bc1"], f"{nametag}b2")
            for dc in range(DCN):
                OP("vector", lambda e, dc=dc: e.tensor_tensor(
                    out=sg3[:, 0:3], in0=x3f[:, dc, 0:3], in1=bc[:, 0, 0:3],
                    op=ALU.mult), ["x3f", "bc0", "sg3"], ["sg3"],
                   tag=f"{nametag}a{dc}")
                OP("vector", lambda e, dc=dc: e.tensor_tensor(
                    out=x3n[:, dc, 0:3], in0=sg3[:, 0:3], in1=bc[:, 1, 0:3],
                    op=ALU.subtract), ["sg3", "bc1", "x3n"], ["x3n"],
                   tag=f"{nametag}b{dc}")

        def halo_pre():
            dma("sync", x3, x_halo, ["eoA"], ["eoA"], "hxl")
            OP("vector", lambda e: e.memset(x3f[:], 0.0), ["x3f"], ["x3f"],
               tag="hx0")
            for dc in range(DCN):
                OP("tensor", lambda e, dc=dc: e.transpose(
                    out=pstat[:, 0:3], in_=x3[:, dc * 128:(dc + 1) * 128],
                    identity=eye[0:3, 0:3]),
                   ["eoA", "eye", "pstat"], ["pstat"], tag=f"hT{dc}")
                OP("vector", lambda e, dc=dc: e.tensor_copy(
                    out=x3f[:, dc, 0:3], in_=pstat[:, 0:3]),
                   ["pstat", "x3f"], ["x3f"], tag=f"hE{dc}")
            halo_ln(True, "hl1")
            for dc in range(DCN):
                OP("tensor", lambda e, dc=dc: e.matmul(
                    out=pstat[0:3, 16:24], lhsT=x3n[:, dc, 0:3],
                    rhs=routerT[:, dc, :], start=(dc == 0), stop=False),
                   ["x3n", "routerT", "pstat"], ["pstat"], tag=f"hg{dc}")
            OP("tensor", lambda e: e.matmul(
                out=pstat[0:3, 16:24], lhsT=ones_bf[0:1, 0:3], rhs=ebias_bf[:],
                start=False, stop=True),
               ["ones_bf", "ebias_bf", "pstat"], ["pstat"], tag="hgb")
            OP("vector", lambda e: e.tensor_copy(out=cw3_tm[:],
                                                 in_=pstat[0:3, 16:24]),
               ["pstat", "cw3"], ["cw3"], tag="hcp")
            OP("vector", lambda e: e.tensor_reduce(
                out=m1c[0:3, 0:1], in_=cw3_tm[:], axis=mybir.AxisListType.X,
                op=ALU.max), ["cw3", "m1c"], ["m1c"], tag="hm1")
            OP("vector", lambda e: e.tensor_scalar(
                out=tmp_r[0:3, 0, :], in0=cw3_tm[:], scalar1=m1c[0:3, 0:1],
                scalar2=None, op0=ALU.is_ge),
               ["cw3", "m1c", "tmp_r"], ["tmp_r"], tag="he1")
            OP("vector", lambda e: e.scalar_tensor_tensor(
                out=tmp_r2[0:3, 0, :], in0=tmp_r[0:3, 0, :], scalar=-1e30,
                in1=cw3_tm[:], op0=ALU.mult, op1=ALU.add),
               ["tmp_r", "cw3", "tmp_r2"], ["tmp_r2"], tag="hl2")
            OP("vector", lambda e: e.tensor_reduce(
                out=m2c[0:3, 0:1], in_=tmp_r2[0:3, 0, :],
                axis=mybir.AxisListType.X, op=ALU.max),
               ["tmp_r2", "m2c"], ["m2c"], tag="hm2")
            OP("vector", lambda e: e.tensor_scalar(
                out=tmp_r2[0:3, 0, :], in0=tmp_r2[0:3, 0, :],
                scalar1=m2c[0:3, 0:1], scalar2=None, op0=ALU.is_ge),
               ["tmp_r2", "m2c"], ["tmp_r2"], tag="he2")
            OP("vector", lambda e: e.tensor_tensor(
                out=s1c[0:3, 0:1], in0=m1c[0:3, 0:1], in1=m2c[0:3, 0:1],
                op=ALU.subtract), ["m1c", "m2c", "s1c"], ["s1c"], tag="hd")
            OP("scalar", lambda e: e.activation(out=s1c[0:3, 0:1],
                                                in_=s1c[0:3, 0:1],
                                                func=AF.Sigmoid),
               ["s1c"], ["s1c"], tag="hsg")
            OP("scalar", lambda e: e.activation(out=s2c[0:3, 0:1],
                                                in_=s1c[0:3, 0:1], func=AF.Copy,
                                                scale=-0.5, bias=0.5),
               ["s1c", "s2c"], ["s2c"], tag="hs2")
            OP("scalar", lambda e: e.activation(out=s1c[0:3, 0:1],
                                                in_=s1c[0:3, 0:1], func=AF.Copy,
                                                scale=0.5),
               ["s1c"], ["s1c"], tag="hs1")
            OP("vector", lambda e: e.tensor_scalar(
                out=tmp_r[0:3, 0, :], in0=tmp_r[0:3, 0, :], scalar1=s1c[0:3, 0:1],
                scalar2=None, op0=ALU.mult), ["tmp_r", "s1c"], ["tmp_r"],
               tag="hh1")
            OP("vector", lambda e: e.tensor_scalar(
                out=tmp_r2[0:3, 0, :], in0=tmp_r2[0:3, 0, :],
                scalar1=s2c[0:3, 0:1], scalar2=None, op0=ALU.mult),
               ["tmp_r2", "s2c"], ["tmp_r2"], tag="hh2")
            OP("vector", lambda e: e.tensor_tensor(
                out=cw3_tm[:], in0=tmp_r[0:3, 0, :], in1=tmp_r2[0:3, 0, :],
                op=ALU.add), ["tmp_r", "tmp_r2", "cw3"], ["cw3"], tag="hcw")
            OP("tensor", lambda e: e.transpose(
                out=pstat[0:8, 32:35], in_=cw3_tm[:], identity=eye[0:3, 0:3]),
               ["cw3", "eye", "pstat"], ["pstat"], tag="hcT")
            OP("vector", lambda e: e.tensor_copy(out=cw3T[:, 0:3],
                                                 in_=pstat[0:8, 32:35]),
               ["pstat", "cw3T"], ["cw3T"], tag="hcTe")
            dma("gpsimd", cw3_d[:, 0:3], cw3T[:, 0:3], ["cw3T"], ["cw3_d"], "hcTs")
        def halo_expert(ei):
            for blk in range(8):
                b = blk % 2
                cols = slice(blk * 256, (blk + 1) * 256)
                dma("sync", w1b[:, b], w1bf_d[ei][:, :, cols],
                    [f"w1b{b}", f"w1d{ei}"], [f"w1b{b}"], f"ha{ei}{blk}")
                dma("sync", w3b[:, b], w3bf_d[ei][:, :, cols],
                    [f"w3b{b}", f"w3d{ei}"], [f"w3b{b}"], f"hb{ei}{blk}")
                for sub in range(2):
                    hc = blk * 2 + sub
                    for dc in range(DCN):
                        OP("tensor", lambda e, b=b, sub=sub, dc=dc: e.matmul(
                            out=pg[0][:, 0:3],
                            lhsT=w1b[:, b, dc, sub * 128:(sub + 1) * 128],
                            rhs=x3n[:, dc, 0:3], start=(dc == 0),
                            stop=(dc == DCN - 1)),
                           [f"w1b{b}", "x3n", "pg0"], ["pg0"],
                           tag=f"hg{ei}{hc}{dc}")
                    for dc in range(DCN):
                        OP("tensor", lambda e, b=b, sub=sub, dc=dc: e.matmul(
                            out=pg[2][:, 0:3],
                            lhsT=w3b[:, b, dc, sub * 128:(sub + 1) * 128],
                            rhs=x3n[:, dc, 0:3], start=(dc == 0),
                            stop=(dc == DCN - 1)),
                           [f"w3b{b}", "x3n", "pg2"], ["pg2"],
                           tag=f"hu{ei}{hc}{dc}")
                    OP("scalar", lambda e: e.activation(
                        out=sg3[:, 0:3], in_=pg[0][:, 0:3], func=AF.Silu),
                       ["pg0", "sg3"], ["sg3"], tag=f"hs{ei}{hc}")
                    OP("vector", lambda e, hc=hc: e.tensor_tensor(
                        out=eo3[:, hc, 0:3], in0=sg3[:, 0:3],
                        in1=pg[2][:, 0:3], op=ALU.mult),
                       ["sg3", "pg2", "eo3"], ["eo3"], tag=f"he{ei}{hc}")
            if ei < E:
                dma("gpsimd", cwb3[:, 0:3],
                    cw3_d[ei:ei + 1, 0:3].partition_broadcast(128),
                    ["cw3_d", "cwb3"], ["cwb3"], f"hc{ei}")
            for dp in range(8):
                b = dp % 2
                dma("sync", w2b[:, b], w2bf_d[ei, dp],
                    [f"w2b{b}", f"w2d{ei}"], [f"w2b{b}"], f"hd{ei}{dp}")
                for hc in range(NHC):
                    OP("tensor", lambda e, b=b, hc=hc: e.matmul(
                        out=pw[0][:, 0:3], lhsT=w2b[:, b, hc, :],
                        rhs=eo3[:, hc, 0:3], start=(hc == 0),
                        stop=(hc == NHC - 1)),
                       [f"w2b{b}", "eo3", "pw0"], ["pw0"],
                       tag=f"h2{ei}{dp}{hc}")
                if ei < E:
                    OP("vector", lambda e: e.tensor_tensor(
                        out=sg3[:, 0:3], in0=pw[0][:, 0:3], in1=cwb3[:, 0:3],
                        op=ALU.mult), ["pw0", "cwb3", "sg3"], ["sg3"],
                       tag=f"hv{ei}{dp}")
                else:
                    OP("vector", lambda e: e.tensor_scalar(
                        out=sg3[:, 0:3], in0=pw[0][:, 0:3], scalar1=0.5,
                        scalar2=None, op0=ALU.mult),
                       ["pw0", "sg3"], ["sg3"], tag=f"hv{ei}{dp}")
                OP("vector", lambda e, dp=dp: e.tensor_tensor(
                    out=x3f[:, dp, 0:3], in0=x3f[:, dp, 0:3],
                    in1=sg3[:, 0:3], op=ALU.add),
                   ["sg3", "x3f"], ["x3f"], tag=f"hx{ei}{dp}")

        def halo_post():
                halo_ln(True, "hl2")
                for et in range(8):
                    b = et % 2
                    dma("sync", ipb[:, b], iprojT_d[:, :, et * 128:(et + 1) * 128],
                        [f"ipb{b}", "ipd"], [f"ipb{b}"], f"hil{et}")
                    for dc in range(DCN):
                        OP("tensor", lambda e, b=b, dc=dc, et=et: e.matmul(
                            out=pg[et % 2][:, 0:3], lhsT=ipb[:, b, dc, :],
                            rhs=x3n[:, dc, 0:3], start=(dc == 0),
                            stop=(dc == DCN - 1)),
                           [f"ipb{b}", "x3n", f"pg{et % 2}"], [f"pg{et % 2}"],
                           tag=f"hi{et}{dc}")
                    OP("vector", lambda e, et=et: e.tensor_copy(
                        out=y3[:, et, 0:3], in_=pg[et % 2][:, 0:3]),
                       [f"pg{et % 2}", "y3"], ["y3"], tag=f"hie{et}")
                for dc in range(DCN):
                    OP("vector", lambda e, dc=dc: e.tensor_scalar(
                        out=tail3[:, dc, :], in0=y3[:, dc, 0:3],
                        scalar1=masks[:, 1:2], scalar2=None, op0=ALU.mult),
                       ["y3", "masks", "tail3"], ["tail3"], tag=f"hms{dc}")

        halo_pre()

        def scan_p1_pre(c, nametag):
            dma("gpsimd", bc[:, 2, :], rows_d[c, 32:33, :].partition_broadcast(128),
                [f"rows{c}", "bc2"], ["bc2"], f"{nametag}bd")

        def scan_p1_step(c, nametag, n, dg):
            bs = n % 2
            if dg == 0:
                dma("gpsimd", bc[:, bs, :],
                    rows_d[c, n:n + 1, :].partition_broadcast(128),
                    [f"rows{c}", f"bc{bs}"], [f"bc{bs}"], f"{nametag}bB{n}")
                dma("gpsimd", scb[:, bs, :],
                    rows_d[c, 16 + n:17 + n, :].partition_broadcast(128),
                    [f"rows{c}", f"scb{bs}"], [f"scb{bs}"], f"{nametag}bC{n}")
            if True:
                ds = (n * DCN + dg) % 2
                OP("scalar", lambda e, dg=dg, n=n, ds=ds: e.activation(
                    out=dAt[:, ds, :], in_=bc[:, 2, :], func=AF.Exp,
                    scale=A_sb[:, dg, n:n + 1]),
                   ["bc2", "A", f"dAt{ds}"], [f"dAt{ds}"],
                   tag=f"{nametag}A{n}{dg}")
                OP("vector", lambda e, dg=dg, bs=bs: e.tensor_tensor(
                    out=ut[:], in0=xconv[:, dg, :], in1=bc[:, bs, :],
                    op=ALU.mult),
                   ["xconv", f"bc{bs}", "ut"], ["ut"], tag=f"{nametag}u{n}{dg}")
                OP("vector", lambda e, dg=dg, n=n, ds=ds: e.tensor_tensor_scan(
                    out=hst[:], data0=dAt[:, ds, :], data1=ut[:],
                    initial=hstate[:, dg, n:n + 1], op0=ALU.mult, op1=ALU.add),
                   [f"dAt{ds}", "ut", "hstate", "hst"], ["hst"],
                   tag=f"{nametag}S{n}{dg}")
                OP("vector", lambda e, dg=dg, n=n: e.tensor_copy(
                    out=hstate[:, dg, n:n + 1], in_=hst[:, TC - 1:TC]),
                   ["hst", "hstate"], ["hstate"], tag=f"{nametag}h{n}{dg}")
                if n == 0:
                    OP("gpsimd", lambda e, dg=dg, bs=bs: e.tensor_tensor(
                        out=yacc[:, dg, :], in0=hst[:], in1=scb[:, bs, :],
                        op=ALU.mult),
                       ["hst", f"scb{bs}", "yacc"], ["yacc"],
                       tag=f"{nametag}y{n}{dg}")
                else:
                    OP("gpsimd", lambda e, dg=dg, bs=bs: e.tensor_tensor(
                        out=ut2[:], in0=hst[:], in1=scb[:, bs, :], op=ALU.mult),
                       ["hst", f"scb{bs}", "ut2"], ["ut2"],
                       tag=f"{nametag}p{n}{dg}")
                    OP("gpsimd", lambda e, dg=dg: e.tensor_tensor(
                        out=yacc[:, dg, :], in0=yacc[:, dg, :], in1=ut2[:],
                        op=ALU.add),
                       ["ut2", "yacc"], ["yacc"], tag=f"{nametag}a{n}{dg}")

        def scan_p1_fin(c, nametag):
            for dg in range(DCN):
                OP("vector", lambda e, dg=dg: e.scalar_tensor_tensor(
                    out=yacc[:, dg, :], in0=xconv[:, dg, :],
                    scalar=D_sb[:, dg:dg + 1], in1=yacc[:, dg, :],
                    op0=ALU.mult, op1=ALU.add),
                   ["xconv", "Dsb", "yacc"], ["yacc"], tag=f"{nametag}D{dg}")
                OP("vector", lambda e, dg=dg: e.tensor_tensor(
                    out=xnm[:, dg, :], in0=yacc[:, dg, :], in1=szbf_w[:, dg, :],
                    op=ALU.mult),
                   ["yacc", "szbf_w", "xnm"], ["xnm"], tag=f"{nametag}G{dg}")

        def front(c):
            s = c % 2
            xr = f"xres{s}"
            ln_chunk(c, f"1B{c}", outbuf=xnm, outres="xnm")
            for et in range(16):
                b = et % 2
                dma("sync", ipb[:, b], iprojT_d[:, :, et * 128:(et + 1) * 128],
                    [f"ipb{b}", "ipd"], [f"ipb{b}"], f"1il{c}{et}")
                pgi = et % 4
                for dc in range(DCN):
                    OP("tensor", lambda e, b=b, dc=dc, pgi=pgi: e.matmul(
                        out=pg[pgi][:], lhsT=ipb[:, b, dc, :], rhs=xnm[:, dc, :],
                        start=(dc == 0), stop=(dc == DCN - 1)),
                       [f"ipb{b}", "xnm", f"pg{pgi}"], [f"pg{pgi}"],
                       tag=f"1i{c}{et}{dc}")
                if et < 8:
                    OP("vector", lambda e, et=et, pgi=pgi: e.tensor_copy(
                        out=xssm[:, et, 3:515], in_=pg[pgi][:]),
                       [f"pg{pgi}", "xssm"], ["xssm"], tag=f"1ie{c}{et}")
                else:
                    OP("scalar", lambda e, et=et, pgi=pgi: e.activation(
                        out=szbf_w[:, et - 8, :], in_=pg[pgi][:], func=AF.Silu),
                       [f"pg{pgi}", "szbf_w"], ["szbf_w"], tag=f"1iz{c}{et}")
            dma("scalar", szbf_d[c], szbf_w[:], ["szbf_w"], [f"szd{c}"],
                f"1zs{c}")
            if c == 0:
                OP("vector", lambda e: e.tensor_copy(out=xssm[:, :, 0:3],
                                                     in_=tail3[:]),
                   ["tail3", "xssm"], ["xssm"], tag="1t0")
            for dg in range(DCN):
                OP("vector", lambda e, dg=dg: e.tensor_scalar(
                    out=xconv[:, dg, :], in0=xssm[:, dg, 0:TC],
                    scalar1=convw[:, dg, 0:1], scalar2=None, op0=ALU.mult),
                   ["xssm", "convw", "xconv"], ["xconv"], tag=f"1c0{c}{dg}")
                for k in range(1, 4):
                    OP("vector", lambda e, dg=dg, k=k: e.scalar_tensor_tensor(
                        out=xconv[:, dg, :], in0=xssm[:, dg, k:k + TC],
                        scalar=convw[:, dg, k:k + 1], in1=xconv[:, dg, :],
                        op0=ALU.mult, op1=ALU.add),
                       ["xssm", "convw", "xconv"], ["xconv"],
                       tag=f"1c{k}{c}{dg}")
                OP("scalar", lambda e, dg=dg: e.activation(
                    out=xconv[:, dg, :], in_=xconv[:, dg, :], func=AF.Silu,
                    bias=convb[:, dg:dg + 1]),
                   ["xconv", "convb"], ["xconv"], tag=f"1cs{c}{dg}")
                OP("scalar", lambda e, dg=dg: e.copy(
                    out=xconv_bf[:, dg, :], in_=xconv[:, dg, :]),
                   ["xconv", "yacc"], ["yacc"], tag=f"1cb{c}{dg}")
            OP("vector", lambda e: e.tensor_copy(
                out=xssm[:, :, 0:3], in_=xssm[:, :, 512:515]),
               ["xssm"], ["xssm"], tag=f"1tl{c}")
            for dc in range(DCN):
                OP("tensor", lambda e, dc=dc: e.matmul(
                    out=pstat[0:80, :], lhsT=xprojT[:, dc, :],
                    rhs=xconv_bf[:, dc, :], start=(dc == 0),
                    stop=(dc == DCN - 1)),
                   ["yacc", "xprojT", "pstat"], ["pstat"], tag=f"1xp{c}{dc}")
            OP("scalar", lambda e: e.activation(out=drow[:], in_=pstat[64:65, :],
                                                func=AF.Abs),
               ["pstat", "drow"], ["drow"], tag=f"1d1{c}")
            OP("scalar", lambda e: e.activation(out=drow[:], in_=drow[:],
                                                func=AF.Exp, scale=-1.0),
               ["drow"], ["drow"], tag=f"1d2{c}")
            OP("scalar", lambda e: e.activation(out=drow[:], in_=drow[:],
                                                func=AF.Ln, bias=1.0),
               ["drow"], ["drow"], tag=f"1d3{c}")
            OP("scalar", lambda e: e.activation(out=srow[:], in_=pstat[64:65, :],
                                                func=AF.Relu),
               ["pstat", "q_row"], ["q_row"], tag=f"1d4{c}")
            OP("vector", lambda e: e.tensor_tensor(out=drow[:], in0=drow[:],
                                                   in1=srow[:], op=ALU.add),
               ["drow", "q_row"], ["drow"], tag=f"1d5{c}")
            dma("gpsimd", rows_d[c, 32:33, :], drow[:], ["drow"], [f"rows{c}"],
                f"1dw{c}")
            dma("gpsimd", scanrows[0:16, :],
                rows_d[c, 32:33, :].partition_broadcast(16),
                [f"rows{c}", "scanrows"], ["scanrows"], f"1db{c}")
            OP("vector", lambda e: e.tensor_tensor(
                out=scanrows[0:16, :], in0=pstat[0:16, :], in1=scanrows[0:16, :],
                op=ALU.mult),
               ["pstat", "scanrows"], ["scanrows"], tag=f"1dB{c}")
            OP("vector", lambda e: e.tensor_copy(out=scanrows[32:48, :],
                                                 in_=pstat[32:48, :]),
               ["pstat", "scanrows"], ["scanrows"], tag=f"1Ce{c}")
            dma("gpsimd", rows_d[c, 0:16, :], scanrows[0:16, :],
                ["scanrows"], [f"rows{c}"], f"1dBs{c}")
            dma("gpsimd", rows_d[c, 16:32, :], scanrows[32:48, :],
                ["scanrows"], [f"rows{c}"], f"1Cs{c}")

        def load_x(c):
            s = c % 2
            xr = f"xres{s}"
            dma("sync", xres[:, s], xres_d[c], [xr, f"xrd{c}"], [xr], f"1xl{c}")

        def chunk_head(c, with_load=True):
            if with_load:
                load_x(c)
            ln_chunk(c, f"1A{c}")
            router_chunk(c, f"1R{c}")

        def p1_tail(c):
            scan_p1_fin(c, f"1s{c}")
            out_proj_chunk(c, f"1o{c}")
            s = c % 2
            dma("sync", xres_d[c], xres[:, s], [f"xres{s}"],
                [f"xres{s}", f"xrd{c}"], f"1xs{c}")

        def make_filler(steps):
            def filler():
                if steps:
                    steps.pop(0)()
            def drain():
                while steps:
                    steps.pop(0)()
            return filler, drain

        def p3loads(c):
            s = c % 2
            xr = f"xres{s}"
            dma("sync", xres[:, s], xres_d[c], [xr, f"xrd{c}"], [xr], f"3xl{c}")
            dma("scalar", szbf_w[:], szbf_d[c], [f"szd{c}", "szbf_w"],
                ["szbf_w"], f"3zl{c}")

        def pass2_pre(c):
            dma("gpsimd", bc[:, 2, :], rows_d[c, 33:34, :].partition_broadcast(128),
                [f"rows{c}", "bc2"], ["bc2"], f"3bS{c}")

        def pass2_step(c, n, dg):
            bs = n % 2
            if dg == 0:
                dma("gpsimd", scb[:, bs, :],
                    rows_d[c, 16 + n:17 + n, :].partition_broadcast(128),
                    [f"rows{c}", f"scb{bs}"], [f"scb{bs}"], f"3bC{c}{n}")
            if True:
                ds = (n * DCN + dg) % 2
                OP("scalar", lambda e, dg=dg, n=n, ds=ds: e.activation(
                    out=dAt[:, ds, :], in_=bc[:, 2, :], func=AF.Exp,
                    scale=A_sb[:, dg, n:n + 1]),
                   ["bc2", "A", f"dAt{ds}"], [f"dAt{ds}"],
                   tag=f"3E{c}{n}{dg}")
                OP("vector", lambda e, dg=dg, n=n, ds=ds, bs=bs:
                   e.scalar_tensor_tensor(
                       out=ut[:], in0=dAt[:, ds, :],
                       scalar=hin[:, dg, n:n + 1], in1=scb[:, bs, :],
                       op0=ALU.mult, op1=ALU.mult),
                   [f"dAt{ds}", "hin", f"scb{bs}", "ut"], ["ut"],
                   tag=f"3t{c}{n}{dg}")
                if n == 0:
                    OP("vector", lambda e, dg=dg: e.tensor_copy(
                        out=yacc[:, dg, :], in_=ut[:]),
                       ["ut", "yacc"], ["yacc"], tag=f"3y{c}{n}{dg}")
                else:
                    OP("vector", lambda e, dg=dg: e.tensor_tensor(
                        out=yacc[:, dg, :], in0=yacc[:, dg, :], in1=ut[:],
                        op=ALU.add),
                       ["ut", "yacc"], ["yacc"], tag=f"3a{c}{n}{dg}")

        def pass2_fin(c):
            for dg in range(DCN):
                OP("vector", lambda e, dg=dg: e.tensor_tensor(
                    out=xnm[:, dg, :], in0=yacc[:, dg, :], in1=szbf_w[:, dg, :],
                    op=ALU.mult),
                   ["yacc", "szbf_w", "xnm"], ["xnm"], tag=f"3g{c}{dg}")
            out_proj_chunk(c, f"3o{c}")
            ln_chunk(c, f"3A{c}")
            router_chunk(c, f"3R{c}")

        def output_chunk(c):
            s = c % 2
            xr = f"xres{s}"
            for tt in range(NTT):
                ob = tt % 2
                for dc in range(DCN):
                    OP("tensor", lambda e, tt=tt, dc=dc, s=s: e.transpose(
                        out=pstat[:, 128:256],
                        in_=xres[:, s, dc, tt * 128:(tt + 1) * 128],
                        identity=eye[:]),
                       [xr, "eye", "pstat"], ["pstat"], tag=f"3T{c}{tt}{dc}")
                    OP("vector", lambda e, ob=ob, dc=dc: e.tensor_copy(
                        out=outstg[:, ob, dc * 128:(dc + 1) * 128],
                        in_=pstat[:, 128:256]),
                       ["pstat", f"os{ob}"], [f"os{ob}", "xssm"],
                       tag=f"3O{c}{tt}{dc}")
                trow = c * TC + tt * 128
                dma("sync", out_d[trow:trow + 128, :], outstg[:, ob],
                    [f"os{ob}"], [f"os{ob}done"], f"3os{c}{tt}")


        chunk_head(0)
        for c in range(NCH):
            if c >= 1:
                steps = [
                    (lambda cc=c - 1, n=n, dg=dg:
                     scan_p1_step(cc, f"1s{cc}", n, dg))
                    for n in range(NST) for dg in range(DCN)]
                filler, drain = make_filler(steps)
            else:
                filler, drain = None, lambda: None
            for ei in range(NEXP):
                ffn_expert(c, f"1F{c}", ei, filler=(filler if ei >= 2 else None))
                if c == 0:
                    halo_expert(ei)
                if c >= 1 and ei == 0:
                    front(c - 1)
                    scan_p1_pre(c - 1, f"1s{c - 1}")
                if c == 0 and ei == 1:
                    load_x(1)
            drain()
            if c == 0:
                halo_post()
                chunk_head(1, with_load=False)
            if c >= 1:
                p1_tail(c - 1)
                if c + 1 < NCH:
                    chunk_head(c + 1)
        front(3)
        sview = eo_flat[0:1, 0:2 * T]
        ones_r2k = yacc.ap().rearrange("p a b -> p (a b)")[0:1, 0:TC]
        OP("vector", lambda e: e.memset(ones_r2k[0:1, :], 1.0),
           ["yacc"], ["ones_r2k", "yacc"], tag="Sones")
        dma("gpsimd", sview[0:1, 0:T].rearrange("r (c t) -> r c t", c=NCH),
            rows_d[:, 32:33, :].transpose([1, 0, 2]),
            list(EOALL) + [f"rows{c}" for c in range(NCH)], list(EOALL), "Sload")
        for sc in range(NCH):
            OP("vector", lambda e, sc=sc: e.tensor_tensor_scan(
                out=sview[0:1, T + sc * TC:T + (sc + 1) * TC],
                data0=ones_r2k[0:1, :],
                data1=sview[0:1, sc * TC:(sc + 1) * TC],
                initial=0.0, op0=ALU.mult, op1=ALU.add),
               list(EOALL) + ["ones_r2k"], list(EOALL), tag=f"Sscan{sc}")
        for sc in range(1, NCH):
            OP("vector", lambda e, sc=sc: e.tensor_scalar(
                out=sview[0:1, T + sc * TC:T + (sc + 1) * TC],
                in0=sview[0:1, T + sc * TC:T + (sc + 1) * TC],
                scalar1=sview[0:1, T + sc * TC - 1:T + sc * TC],
                scalar2=None, op0=ALU.add),
               list(EOALL), list(EOALL), tag=f"Soff{sc}")
        dma("gpsimd", rows_d[:, 33:34, :].transpose([1, 0, 2]),
            sview[0:1, T:2 * T].rearrange("r (c t) -> r c t", c=NCH),
            list(EOALL),
            [f"rows{c}" for c in range(NCH)], "Sstore")
        scan_p1_pre(3, "1s3")
        for n in range(NST):
            for dg in range(DCN):
                scan_p1_step(3, "1s3", n, dg)
        OP("vector", lambda e: e.tensor_scalar(
            out=ccs[:], in0=hstate.ap().rearrange("p dg n -> p (dg n)"),
            scalar1=masks[:, 0:1], scalar2=None, op0=ALU.mult),
           ["hstate", "masks", "ccs"], ["ccs"], tag="ccm")
        dma("gpsimd", cc_in_d[:, :], ccs[:], ["ccs"], ["cc_in"], "ccw")
        OP("gpsimd", lambda e: e.collective_compute(
            "AllReduce", ALU.add,
            replica_groups=[[0, 1], [2, 3], [4, 5], [6, 7]],
            ins=[cc_in_d[:, :]], outs=[cc_out_d[:, :]]),
           ["cc_in", "cc_out"], ["cc_out"], tag="cc")
        dma("gpsimd", ccs[:], cc_out_d[:, :], ["cc_out", "ccs"], ["ccs"], "ccr")
        OP("vector", lambda e: e.tensor_scalar(
            out=hin.ap().rearrange("p dg n -> p (dg n)"), in0=ccs[:],
            scalar1=masks[:, 1:2], scalar2=None, op0=ALU.mult),
           ["ccs", "masks", "hin"], ["hin"], tag="cci")
        p1_tail(3)
        p3loads(0)
        pass2_pre(0)
        for n in range(NST):
            for dg in range(DCN):
                pass2_step(0, n, dg)
        pass2_fin(0)
        p3loads(1)
        for c in range(NCH):
            if c + 1 < NCH:
                pass2_pre(c + 1)
                steps = [
                    (lambda cc=c + 1, n=n, dg=dg: pass2_step(cc, n, dg))
                    for n in range(NST) for dg in range(DCN)]
                filler, drain = make_filler(steps)
            else:
                filler, drain = None, lambda: None
            for ei in range(NEXP):
                ffn_expert(c, f"3F{c}", ei, filler=filler)
            drain()
            if c + 1 < NCH:
                pass2_fin(c + 1)
            output_chunk(c)
            if c + 2 < NCH:
                p3loads(c + 2)

        OP("vector", lambda e: e.engine_nop(),
           reads=[], writes=["os0done", "os1done"], tag="final")

        sched.prepare()
        nw = [0]

        @block.sync
        def _(eng):
            nw[0] += sched.emit_engine("sync", eng, semdict)

        @block.scalar
        def _(eng):
            nw[0] += sched.emit_engine("scalar", eng, semdict)

        @block.vector
        def _(eng):
            nw[0] += sched.emit_engine("vector", eng, semdict)

        @block.tensor
        def _(eng):
            nw[0] += sched.emit_engine("tensor", eng, semdict)

        @block.gpsimd
        def _(eng):
            nw[0] += sched.emit_engine("gpsimd", eng, semdict)

        import os
        if os.environ.get("KB_DEBUG"):
            print(f"[kernel] items={len(sched.items)} waits={nw[0]} "
                  f"ninst={nc.n_instructions()}")

    return nc


_PROG_CACHE = {}


def kernel(**inputs):
    x = np.asarray(inputs["L0_attn_x"])
    if "prog" not in _PROG_CACHE:
        _PROG_CACHE["prog"] = build_program()
    nc = _PROG_CACHE["prog"]

    eye = np.eye(128, dtype=np.float32)
    base = {
        "w_inproj": np.ascontiguousarray(inputs["L0_attn_in_proj_w"], np.float32),
        "w_convw": np.ascontiguousarray(inputs["L0_attn_conv_w"], np.float32),
        "w_convb": np.ascontiguousarray(inputs["L0_attn_conv_b"], np.float32),
        "w_xproj": np.ascontiguousarray(inputs["L0_attn_x_proj_w"], np.float32),
        "w_alog": np.ascontiguousarray(inputs["L0_attn_A_log"], np.float32),
        "w_dpar": np.ascontiguousarray(inputs["L0_attn_D_param"], np.float32),
        "w_outproj": np.ascontiguousarray(inputs["L0_attn_out_proj_w"], np.float32),
        "w_router": np.ascontiguousarray(inputs["L0_ffn_router_w"], np.float32),
        "w_ebias": np.ascontiguousarray(inputs["L0_ffn_experts_bias"], np.float32),
        "w_w1": np.ascontiguousarray(inputs["L0_ffn_w1"], np.float32),
        "w_w3": np.ascontiguousarray(inputs["L0_ffn_w3"], np.float32),
        "w_w2": np.ascontiguousarray(inputs["L0_ffn_w2"], np.float32),
        "w_sw1": np.ascontiguousarray(inputs["L0_ffn_sw1"], np.float32),
        "w_sw3": np.ascontiguousarray(inputs["L0_ffn_sw3"], np.float32),
        "w_sw2": np.ascontiguousarray(inputs["L0_ffn_sw2"], np.float32),
        "eye_in": eye,
    }
    in_maps = []
    for core in range(8):
        b, half = core // 2, core % 2
        xm = np.ascontiguousarray(x[b, half * T:(half + 1) * T, :], np.float32)
        if half == 0:
            xh = np.zeros((3, D), np.float32)
        else:
            xh = np.ascontiguousarray(x[b, half * T - 3:half * T, :], np.float32)
        masks = np.zeros((128, 2), np.float32)
        masks[:, 0] = 1.0 - half
        masks[:, 1] = float(half)
        m = dict(base)
        m.update({"x_main": xm, "x_halo": xh, "masks_in": masks})
        in_maps.append(m)
    res = run_bass_kernel_spmd(nc, in_maps, core_ids=list(range(8)))
    out = np.empty((4, 4096, D), np.float32)
    for core in range(8):
        b, half = core // 2, core % 2
        out[b, half * T:(half + 1) * T, :] = res.results[core]["out"]
    return out


# version 17
# speedup vs baseline: 1.1148x; 1.0044x over previous
import numpy as np
import concourse.bass as bass
import concourse.mybir as mybir
from concourse.bass_utils import run_bass_kernel_spmd
from contextlib import ExitStack

F32 = mybir.dt.float32
BF16 = mybir.dt.bfloat16
AF = mybir.ActivationFunctionType
ALU = mybir.AluOpType

D = 1024
DCN = 8
E = 8
NEXP = 9
H = 2048
NHC = 16
NST = 16
T = 2048
TC = 512
NCH = 4
NTT = 4
EPS = 1e-5

ENGINES = ("sync", "scalar", "vector", "tensor", "gpsimd")
DMA_Q = {"sync": "qs", "scalar": "qa", "gpsimd": "qg", "vector": "qv"}
NLANES = 8
EOALL = ("eoA", "eoX")


class Sched:

    def __init__(self):
        self.items = []
        self.writers = {}
        self.readers = {}

    def op(self, engine, fn, reads=(), writes=(), dma=False, tag=""):
        idx = len(self.items)
        deps = set()
        for r in reads:
            deps.update(self.writers.get(r, ()))
        for r in writes:
            deps.update(self.writers.get(r, ()))
            deps.update(self.readers.get(r, ()))
        self.items.append(dict(engine=engine, fn=fn, deps=deps, dma=dma,
                               tag=tag, needs_inc=False))
        for r in reads:
            self.readers.setdefault(r, []).append(idx)
        for r in writes:
            self.writers[r] = [idx]
            self.readers[r] = []
        return idx

    @staticmethod
    def _queue(it):
        return DMA_Q[it["engine"]] if it["dma"] else None

    @staticmethod
    def _chan(it):
        if it["dma"]:
            return DMA_Q[it["engine"]] + str(it.get("lane", 0))
        return it["engine"]

    MARGIN = 10
    SYNC_CHANS = ("vector", "scalar", "gpsimd")

    def prepare(self):
        qseq = {}
        lane_prev = {}
        for i, it in enumerate(self.items):
            q = self._queue(it)
            if q is not None:
                k = qseq.get(q, 0)
                qseq[q] = k + 1
                it["lane"] = k % NLANES
                key = (q, it["lane"])
                if key in lane_prev:
                    it["deps"].add(lane_prev[key])
                lane_prev[key] = i
        seq = {}
        for it in self.items:
            c = self._chan(it)
            it["cseq"] = seq.get(c, 0)
            seq[c] = it["cseq"] + 1
        for it in self.items:
            for d in it["deps"]:
                dit = self.items[d]
                c = self._chan(dit)
                if c != self._chan(it):
                    dit["needs_inc"] = True
                elif (c in self.SYNC_CHANS and not dit["dma"] and not it["dma"]
                      and it["cseq"] - dit["cseq"] <= self.MARGIN):
                    dit["needs_inc"] = True
                    it.setdefault("close_deps", []).append(d)
        for it in self.items:
            if it["dma"]:
                it["needs_inc"] = True
        counters = {}
        for it in self.items:
            c = self._chan(it)
            if it["needs_inc"]:
                counters[c] = counters.get(c, 0) + 1
                it["inc_val"] = counters[c] * (16 if it["dma"] else 1)
            else:
                it["inc_val"] = None

    def emit_engine(self, ename, eng, sems):
        hiwater = {}
        nwaits = 0
        for it in self.items:
            if it["engine"] != ename:
                continue
            mychan = self._chan(it)
            want = {}
            close = set(it.get("close_deps", ()))
            for d in it["deps"]:
                dit = self.items[d]
                c = self._chan(dit)
                if (c == mychan and not it["dma"] and not dit["dma"]
                        and d not in close):
                    continue
                if dit["inc_val"] is None:
                    raise RuntimeError(f"dep without inc: {dit['tag']}")
                want[c] = max(want.get(c, 0), dit["inc_val"])
            for c, v in sorted(want.items()):
                if hiwater.get(c, 0) >= v:
                    continue
                eng.wait_ge(sems[c], v)
                hiwater[c] = v
                nwaits += 1
            inst = it["fn"](eng)
            if it["inc_val"] is not None:
                if inst is None:
                    raise RuntimeError(f"needs inc, no inst: {it['tag']}")
                inst.then_inc(sems[mychan], 16 if it["dma"] else 1)
        return nwaits


def build_program():
    nc = bass.Bass("TRN2", target_bir_lowering=False)

    def din(name, shape, dt=F32):
        return nc.dram_tensor(name, shape, dt, kind="ExternalInput").ap()

    x_main = din("x_main", [T, D])
    x_halo = din("x_halo", [3, D])
    w_inproj = din("w_inproj", [2 * D, D])
    w_convw = din("w_convw", [D, 4])
    w_convb = din("w_convb", [D])
    w_xproj = din("w_xproj", [33, D])
    w_alog = din("w_alog", [D, NST])
    w_dpar = din("w_dpar", [D])
    w_outproj = din("w_outproj", [D, D])
    w_router = din("w_router", [E, D])
    w_ebias = din("w_ebias", [E])
    w_w1 = din("w_w1", [E, D, H])
    w_w3 = din("w_w3", [E, D, H])
    w_w2 = din("w_w2", [E, H, D])
    w_sw1 = din("w_sw1", [1, D, H])
    w_sw3 = din("w_sw3", [1, D, H])
    w_sw2 = din("w_sw2", [1, H, D])
    eye_in = din("eye_in", [128, 128])
    masks_in = din("masks_in", [128, 2])
    out_d = nc.dram_tensor("out", [T, D], F32, kind="ExternalOutput").ap()

    w1bf_d = nc.dram_tensor("w1bf_d", [NEXP, 128, DCN, H], BF16).ap()
    w3bf_d = nc.dram_tensor("w3bf_d", [NEXP, 128, DCN, H], BF16).ap()
    w2bf_d = nc.dram_tensor("w2bf_d", [NEXP, 8, 128, NHC, 128], BF16).ap()
    iprojT_d = nc.dram_tensor("iprojT_d", [128, DCN, 2 * D], BF16).ap()
    oprojT_d = nc.dram_tensor("oprojT_d", [128, DCN, D], BF16).ap()
    xres_d = nc.dram_tensor("xres_d", [NCH, 128, DCN, TC], F32).ap()
    szbf_d = nc.dram_tensor("szbf_d", [NCH, 128, DCN, TC], BF16).ap()
    rows_d = nc.dram_tensor("rows_d", [NCH, 34, TC], F32).ap()
    lnrows_d = nc.dram_tensor("lnrows_d", [2, TC], F32).ap()
    cwT_d = nc.dram_tensor("cwT_d", [E, T], F32).ap()
    cc_in_d = nc.dram_tensor("cc_in_d", [128, 128], F32).ap()
    cc_out_d = nc.dram_tensor("cc_out_d", [128, 128], F32).ap()
    hrows_d = nc.dram_tensor("hrows_d", [2, TC], F32).ap()
    cw3_d = nc.dram_tensor("cw3_d", [E, 4], F32).ap()

    sched = Sched()

    with ExitStack() as ctx:
        def sb(name, shape, dt=F32):
            return ctx.enter_context(nc.sbuf_tensor(name, shape, dt))

        xres = sb("xres", [128, 2, DCN, TC])
        xssm = sb("xssm", [128, DCN, 515])
        xconv = sb("xconv", [128, DCN, TC])
        yacc = sb("yacc", [128, DCN, TC])
        xnorm = sb("xnorm", [128, DCN, TC], BF16)
        xnm = sb("xnm", [128, DCN, TC], BF16)
        eo = sb("eo", [128, NHC, TC], BF16)
        szbf_w = sb("szbf_w", [128, DCN, TC], BF16)
        sg = sb("sg", [128, 2, TC])
        sge = sb("sge", [128, TC])
        w1b = sb("w1b", [128, 2, DCN, 256], BF16)
        w3b = sb("w3b", [128, 2, DCN, 256], BF16)
        w2b = sb("w2b", [128, 2, NHC, 128], BF16)
        ipb = sb("ipb", [128, 2, DCN, 128], BF16)
        opb = sb("opb", [128, 2, DCN, 128], BF16)
        dAt = sb("dAt", [128, 2, TC])
        ut = sb("ut", [128, TC])
        ut2 = sb("ut2", [128, TC])
        hst = sb("hst", [128, TC])
        bc = sb("bc", [128, 3, TC])
        cwb = sb("cwb", [128, 2, TC])
        scb = sb("scb", [128, 2, TC])
        scanrows = sb("scanrows", [48, TC])
        m_row = sb("m_row", [1, TC])
        q_row = sb("q_row", [1, TC])
        mr_row = sb("mr_row", [1, TC])
        drow = sb("drow", [1, TC])
        srow = q_row
        scar = sb("scar", [1, 2])
        ones_c = sb("ones_c", [128, 1])
        ones_bf = sb("ones_bf", [1, 128], BF16)
        hstate = sb("hstate", [128, DCN, NST])
        hin = sb("hin", [128, DCN, NST])
        ccs = sb("ccs", [128, 128])
        A_sb = sb("A_sb", [128, DCN, NST])
        convw = sb("convw", [128, DCN, 4])
        convb = sb("convb", [128, DCN])
        D_sb = sb("D_sb", [128, DCN])
        masks = sb("masks", [128, 2])
        cvals = sb("cvals", [128, 2])
        eye = sb("eye", [128, 128])
        eye_bf = sb("eye_bf", [128, 128], BF16)
        xprojT = sb("xprojT", [128, DCN, 80], BF16)
        routerT = sb("routerT", [128, DCN, E], BF16)
        ebias_r = sb("ebias_r", [1, E])
        ebias_bf = sb("ebias_bf", [1, E], BF16)
        cw_tm = sb("cw_tm", [128, 4, E])
        cwT_sb = sb("cwT_sb", [8, TC])
        tmp_r = sb("tmp_r", [128, 4, E])
        tmp_r2 = sb("tmp_r2", [128, 4, E])
        m1c = sb("m1c", [128, 4])
        m2c = sb("m2c", [128, 4])
        s1c = sb("s1c", [128, 4])
        s2c = sb("s2c", [128, 4])
        x3f = sb("x3f", [128, DCN, 4])
        x3n = sb("x3n", [128, DCN, 4], BF16)
        eo3 = sb("eo3", [128, NHC, 4], BF16)
        sg3 = sb("sg3", [128, 4])
        y3 = sb("y3", [128, DCN, 4])
        cw3_tm = sb("cw3_tm", [3, E])
        cw3T = sb("cw3T", [8, 4])
        cwb3 = sb("cwb3", [128, 4])
        tail3 = sb("tail3", [128, DCN, 3])

        eo_f = eo.ap().bitcast(F32)
        eo_flat = eo_f.rearrange("p a b -> p (a b)")
        eo_bflat = eo.ap().rearrange("p a b -> p (a b)")
        xs_flat = xssm.ap().rearrange("p a b -> p (a b)")
        xs_bflat = xssm.ap().bitcast(BF16).rearrange("p a b -> p (a b)")
        xc_flat = xconv.ap().rearrange("p a b -> p (a b)")
        xc_bflat = xconv.ap().bitcast(BF16).rearrange("p a b -> p (a b)")
        ya_flat = yacc.ap().rearrange("p a b -> p (a b)")
        ya_bflat = yacc.ap().bitcast(BF16).rearrange("p a b -> p (a b)")
        sz_bflat = szbf_w.ap().rearrange("p a b -> p (a b)")
        xn_bflat = xnorm.ap().rearrange("p a b -> p (a b)")
        band_f32 = [eo_flat[:, 0:2048], xs_flat[:, 0:2048],
                    xc_flat[:, 0:2048], ya_flat[:, 0:2048]]
        band_bf = [eo_bflat[:, 4096:6144], xs_bflat[:, 4096:6144],
                   xc_bflat[:, 4096:6144], ya_bflat[:, 4096:6144]]
        band_tag = [["eoA"], ["xssm"], ["xconv"], ["yacc"]]
        xconv_bf = ya_bflat[:, 0:4096].rearrange("p (dc t) -> p dc t", dc=DCN)
        xtm = eo_flat[:, 3072:4096]
        xproj_sb = eo_flat[0:33, 0:1024]
        router_sb = eo_flat[0:8, 1024:2048]
        x3 = eo_flat[0:3, 0:1024]
        outstg = xs_flat[:, 0:2048].rearrange(
            "p (b x) -> p b x", b=2)

        pg = [ctx.enter_context(nc.psum_tensor(f"pg{i}", [128, TC], F32))
              for i in range(4)]
        pw = [ctx.enter_context(nc.psum_tensor(f"pw{i}", [128, TC], F32))
              for i in range(2)]
        pstat = ctx.enter_context(nc.psum_tensor("pstat", [128, TC], F32))
        pbf = ctx.enter_context(nc.psum_tensor("pbf", [128, 1024], BF16))

        semdict = {}
        for s in ("sync", "scalar", "vector", "tensor", "gpsimd"):
            semdict[s] = ctx.enter_context(nc.semaphore("sem_" + s))
        for q in ("qs", "qa", "qg", "qv"):
            for ln in range(NLANES):
                semdict[q + str(ln)] = ctx.enter_context(
                    nc.semaphore(f"sem_{q}{ln}"))

        block = ctx.enter_context(nc.Block())
        OP = sched.op

        def dma(qeng, out, in_, reads, writes, tag="", slow=False):
            if slow:
                OP(qeng, lambda e, o=out, i=in_: e.dma_start(
                    out=o, in_=i, allow_slow_non_contiguous=True),
                   reads=reads, writes=writes, dma=True, tag=tag)
            else:
                OP(qeng, lambda e, o=out, i=in_: e.dma_start(out=o, in_=i),
                   reads=reads, writes=writes, dma=True, tag=tag)

        dma("sync", eye[:], eye_in, [], ["eye"], "eye")
        dma("sync", masks[:], masks_in, [], ["masks"], "masks")
        dma("sync", A_sb[:], w_alog.rearrange("(dc dp) n -> dp dc n", dp=128),
            [], ["A"], "alog")
        dma("sync", convw[:], w_convw.rearrange("(dc dp) k -> dp dc k", dp=128),
            [], ["convw"], "convw")
        dma("sync", convb[:], w_convb.rearrange("(dc dp) -> dp dc", dp=128),
            [], ["convb"], "convb", slow=True)
        dma("sync", D_sb[:], w_dpar.rearrange("(dc dp) -> dp dc", dp=128),
            [], ["Dsb"], "dpar", slow=True)
        dma("sync", ebias_r[:], w_ebias.rearrange("(o e) -> o e", o=1), [], ["ebias"], "eb")
        OP("scalar", lambda e: e.activation(out=A_sb[:], in_=A_sb[:], func=AF.Exp),
           ["A"], ["A"], tag="a_exp")
        OP("scalar", lambda e: e.mul(out=A_sb[:], in_=A_sb[:], mul=-1.0),
           ["A"], ["A"], tag="a_neg")
        OP("scalar", lambda e: e.copy(out=ebias_bf[:], in_=ebias_r[:]),
           ["ebias"], ["ebias_bf"], tag="ebbf")
        OP("scalar", lambda e: e.copy(out=eye_bf[:], in_=eye[:]),
           ["eye"], ["eye_bf"], tag="eyebf")
        OP("vector", lambda e: e.memset(ones_c[:], 1.0), [], ["ones_c"], tag="o2")
        OP("vector", lambda e: e.memset(ones_bf[:], 1.0), [], ["ones_bf"], tag="o3")
        OP("vector", lambda e: e.memset(scar[:], 0.0), [], ["scar"], tag="o4")
        OP("vector", lambda e: e.memset(hstate[:], 0.0), [], ["hstate"], tag="o5")
        OP("vector", lambda e: e.memset(cvals[:, 0:1], EPS), [], ["cvals"], tag="o6")
        OP("vector", lambda e: e.memset(cvals[:, 1:2], 0.5), [], ["cvals"], tag="o7")

        OP("vector", lambda e: e.memset(xprojT[:], 0.0), [], ["xprojT"], tag="xp0")
        dma("sync", xproj_sb, w_xproj, ["eoA"], ["eoA"], "xpl")
        for dc in range(DCN):
            OP("tensor", lambda e, dc=dc: e.transpose(
                out=pstat[:, 0:33], in_=xproj_sb[:, dc * 128:(dc + 1) * 128],
                identity=eye[0:33, 0:33]),
               ["eoA", "eye", "pstat"], ["pstat"], tag=f"xpT{dc}")
            OP("vector", lambda e, dc=dc: e.tensor_copy(
                out=xprojT[:, dc, 0:16], in_=pstat[:, 0:16]),
               ["pstat", "xprojT"], ["xprojT"], tag=f"xpE1{dc}")
            OP("vector", lambda e, dc=dc: e.tensor_copy(
                out=xprojT[:, dc, 32:48], in_=pstat[:, 16:32]),
               ["pstat", "xprojT"], ["xprojT"], tag=f"xpE2{dc}")
            OP("vector", lambda e, dc=dc: e.tensor_copy(
                out=xprojT[:, dc, 64:65], in_=pstat[:, 32:33]),
               ["pstat", "xprojT"], ["xprojT"], tag=f"xpE3{dc}")
        dma("sync", router_sb, w_router, ["eoA"], ["eoA"], "rtl")
        for dc in range(DCN):
            OP("tensor", lambda e, dc=dc: e.transpose(
                out=pstat[:, 40:48], in_=router_sb[:, dc * 128:(dc + 1) * 128],
                identity=eye[0:8, 0:8]),
               ["eoA", "eye", "pstat"], ["pstat"], tag=f"rtT{dc}")
            OP("vector", lambda e, dc=dc: e.tensor_copy(
                out=routerT[:, dc, :], in_=pstat[:, 40:48]),
               ["pstat", "routerT"], ["routerT"], tag=f"rtE{dc}")

        for c in range(NCH):
            s = c % 2
            for tt in range(NTT):
                trow = c * TC + tt * 128
                dma("sync", xtm, x_main[trow:trow + 128, :],
                    ["eoX"], ["eoX"], f"xl{c}{tt}")
                for dc in range(DCN):
                    OP("tensor", lambda e, dc=dc: e.transpose(
                        out=pstat[:, 128:256], in_=xtm[:, dc * 128:(dc + 1) * 128],
                        identity=eye[:]),
                       ["eoX", "eye", "pstat"], ["pstat"], tag=f"xT{c}{tt}{dc}")
                    OP("vector", lambda e, s=s, tt=tt, dc=dc: e.tensor_copy(
                        out=xres[:, s, dc, tt * 128:(tt + 1) * 128],
                        in_=pstat[:, 128:256]),
                       ["pstat", f"xres{s}"], [f"xres{s}"], tag=f"xE{c}{tt}{dc}")
            dma("gpsimd", xres_d[c], xres[:, s], [f"xres{s}"],
                [f"xres{s}", f"xrd{c}"], f"xs{c}")

        band_i = [0]

        def band_unit(load_src, nrows, ncols_f32, store_fn, dres, ename):
            b = band_i[0] % 4
            band_i[0] += 1
            tags = band_tag[b]
            qload = "sync" if b % 2 == 0 else "gpsimd"
            fsrc = band_f32[b][0:nrows, 0:ncols_f32]
            bsrc = band_bf[b][0:nrows, 0:ncols_f32]
            dma(qload, fsrc, load_src, tags, tags, f"{ename}l")
            OP("scalar", lambda e, fsrc=fsrc, bsrc=bsrc: e.copy(
                out=bsrc, in_=fsrc), tags, tags, tag=f"{ename}c")
            store_fn(bsrc, tags)

        def prep_ffn(wsrc, dst, dres, ename):
            for dc in range(DCN):
                def store(bsrc, tags, dres=dres, dc=dc, ename=ename):
                    dma("scalar", dst[:, dc:dc + 1, :], bsrc.unsqueeze(1),
                        tags, list(tags) + [dres], f"{ename}s{dc}")
                band_unit(wsrc[dc * 128:(dc + 1) * 128, :], 128, H,
                          store, dres, f"{ename}{dc}")

        def prep_w2(eidx, wsrc, dres, ename):
            for hc in range(NHC):
                def store(bsrc, tags, dres=dres, hc=hc, ename=ename):
                    dma("scalar",
                        w2bf_d[:, :, :, hc, :][eidx].transpose([1, 0, 2]),
                        bsrc.rearrange("p (dp ds) -> p dp ds", dp=8),
                        tags, list(tags) + [dres], f"{ename}s{hc}")
                band_unit(wsrc[hc * 128:(hc + 1) * 128, :], 128, D,
                          store, dres, f"{ename}{hc}")

        for ei in range(E):
            prep_ffn(w_w1[ei], w1bf_d[ei], f"w1d{ei}", f"pw1e{ei}")
            prep_ffn(w_w3[ei], w3bf_d[ei], f"w3d{ei}", f"pw3e{ei}")
            prep_w2(ei, w_w2[ei], f"w2d{ei}", f"pw2e{ei}")
        prep_ffn(w_sw1[0], w1bf_d[E], f"w1d{E}", "psw1")
        prep_ffn(w_sw3[0], w3bf_d[E], f"w3d{E}", "psw3")
        prep_w2(E, w_sw2[0], f"w2d{E}", "psw2")

        def prep_transposed(wsrc, n_rows, dst, dres, ename):
            for ob in range(n_rows // 128):
                b = band_i[0] % 4
                band_i[0] += 1
                tags = band_tag[b]
                qload = "sync" if b % 2 == 0 else "gpsimd"
                fsrc = band_f32[b][:, 0:1024]
                bsrc = band_bf[b][:, 0:1024]
                btmp = band_bf[b][:, 1024:2048]
                dma(qload, fsrc, wsrc[ob * 128:(ob + 1) * 128, :],
                    tags, tags, f"{ename}l{ob}")
                OP("scalar", lambda e, fsrc=fsrc, bsrc=bsrc: e.copy(
                    out=bsrc, in_=fsrc), tags, tags, tag=f"{ename}c{ob}")
                for dc in range(DCN):
                    OP("tensor", lambda e, dc=dc, bsrc=bsrc: e.transpose(
                        out=pbf[:, (dc % 8) * 128:(dc % 8) * 128 + 128],
                        in_=bsrc[:, dc * 128:(dc + 1) * 128],
                        identity=eye_bf[:]),
                       list(tags) + ["eye_bf", "pbf"], ["pbf"],
                       tag=f"{ename}t{ob}{dc}")
                OP("vector", lambda e, btmp=btmp: e.tensor_copy(
                    out=btmp, in_=pbf[:, 0:1024]),
                   ["pbf"] + list(tags), tags, tag=f"{ename}e{ob}")
                dma("scalar", dst[:, :, ob * 128:(ob + 1) * 128],
                    btmp.rearrange("p (dc x) -> p dc x", dc=DCN),
                    tags, list(tags) + [dres], f"{ename}s{ob}")

        prep_transposed(w_inproj, 2 * D, iprojT_d, "ipd", "pip")
        prep_transposed(w_outproj, D, oprojT_d, "opd", "pop")

        def ln_chunk(c, nametag, outbuf=None, outres="xnorm"):
            if outbuf is None:
                outbuf = xnorm
            s = c % 2
            xr = f"xres{s}"
            for dc in range(DCN):
                OP("tensor", lambda e, dc=dc, s=s: e.matmul(
                    out=pstat[0:1, :], lhsT=ones_c[:], rhs=xres[:, s, dc, :],
                    start=(dc == 0), stop=(dc == DCN - 1)),
                   [xr, "ones_c", "pstat"], ["pstat"], tag=f"{nametag}sm{dc}")
            for dc in range(DCN):
                sb_i = dc % 2
                OP("vector", lambda e, dc=dc, s=s, sb_i=sb_i: e.tensor_tensor(
                    out=sg[:, sb_i, :], in0=xres[:, s, dc, :],
                    in1=xres[:, s, dc, :], op=ALU.mult),
                   [xr, f"sg{sb_i}"], [f"sg{sb_i}"], tag=f"{nametag}sq{dc}")
                OP("tensor", lambda e, dc=dc, sb_i=sb_i: e.matmul(
                    out=pw[0][0:1, :], lhsT=ones_c[:], rhs=sg[:, sb_i, :],
                    start=(dc == 0), stop=(dc == DCN - 1)),
                   [f"sg{sb_i}", "ones_c", "pw0"], ["pw0"], tag=f"{nametag}sk{dc}")
            OP("scalar", lambda e: e.activation(out=m_row[:], in_=pstat[0:1, :],
                                                func=AF.Copy, scale=1.0 / D),
               ["pstat", "m_row"], ["m_row"], tag=f"{nametag}m")
            OP("scalar", lambda e: e.activation(out=q_row[:], in_=pw[0][0:1, :],
                                                func=AF.Copy, scale=1.0 / D),
               ["pw0", "q_row"], ["q_row"], tag=f"{nametag}q")
            OP("vector", lambda e: e.tensor_tensor(
                out=mr_row[:], in0=m_row[:], in1=m_row[:], op=ALU.mult),
               ["m_row", "mr_row"], ["mr_row"], tag=f"{nametag}mm")
            OP("vector", lambda e: e.tensor_tensor(
                out=q_row[:], in0=q_row[:], in1=mr_row[:], op=ALU.subtract),
               ["q_row", "mr_row"], ["q_row"], tag=f"{nametag}v")
            OP("scalar", lambda e: e.activation(out=q_row[:], in_=q_row[:],
                                                func=AF.Ln, bias=cvals[0:1, 0:1]),
               ["q_row", "cvals"], ["q_row"], tag=f"{nametag}ln")
            OP("scalar", lambda e: e.activation(out=q_row[:], in_=q_row[:],
                                                func=AF.Exp, scale=-0.5),
               ["q_row"], ["q_row"], tag=f"{nametag}ex")
            OP("vector", lambda e: e.tensor_tensor(
                out=mr_row[:], in0=m_row[:], in1=q_row[:], op=ALU.mult),
               ["m_row", "q_row", "mr_row"], ["mr_row"], tag=f"{nametag}mr")
            dma("gpsimd", lnrows_d[0:1, :], q_row[:], ["q_row"], ["lnr_d"],
                f"{nametag}wr")
            dma("gpsimd", lnrows_d[1:2, :], mr_row[:], ["mr_row"], ["lnr_d"],
                f"{nametag}wm")
            dma("gpsimd", bc[:, 0, :], lnrows_d[0:1, :].partition_broadcast(128),
                ["lnr_d", "bc0"], ["bc0"], f"{nametag}br")
            dma("gpsimd", bc[:, 1, :], lnrows_d[1:2, :].partition_broadcast(128),
                ["lnr_d", "bc1"], ["bc1"], f"{nametag}bm")
            for dc in range(DCN):
                OP("vector", lambda e, dc=dc, s=s: e.tensor_tensor(
                    out=sge[:], in0=xres[:, s, dc, :], in1=bc[:, 0, :],
                    op=ALU.mult),
                   [xr, "bc0", "sge"], ["sge"], tag=f"{nametag}n1{dc}")
                OP("vector", lambda e, dc=dc, outbuf=outbuf: e.tensor_tensor(
                    out=outbuf[:, dc, :], in0=sge[:], in1=bc[:, 1, :],
                    op=ALU.subtract),
                   ["sge", "bc1", outres], [outres], tag=f"{nametag}n2{dc}")

        def router_chunk(c, nametag):
            for tt in range(NTT):
                for dc in range(DCN):
                    OP("tensor", lambda e, tt=tt, dc=dc: e.matmul(
                        out=pstat[:, tt * 8:tt * 8 + 8],
                        lhsT=xnorm[:, dc, tt * 128:(tt + 1) * 128],
                        rhs=routerT[:, dc, :], start=(dc == 0), stop=False),
                       ["xnorm", "routerT", "pstat"], ["pstat"],
                       tag=f"{nametag}lg{tt}{dc}")
                OP("tensor", lambda e, tt=tt: e.matmul(
                    out=pstat[:, tt * 8:tt * 8 + 8], lhsT=ones_bf[:],
                    rhs=ebias_bf[:], start=False, stop=True),
                   ["ones_bf", "ebias_bf", "pstat"], ["pstat"],
                   tag=f"{nametag}lb{tt}")
            OP("vector", lambda e: e.tensor_copy(
                out=cw_tm.ap().rearrange("p a e -> p (a e)"), in_=pstat[:, 0:32]),
               ["pstat", "cw_tm"], ["cw_tm"], tag=f"{nametag}cp")
            OP("vector", lambda e: e.tensor_reduce(
                out=m1c[:], in_=cw_tm[:], axis=mybir.AxisListType.X, op=ALU.max),
               ["cw_tm", "m1c"], ["m1c"], tag=f"{nametag}m1")
            for a in range(4):
                OP("vector", lambda e, a=a: e.tensor_scalar(
                    out=tmp_r[:, a, :], in0=cw_tm[:, a, :], scalar1=m1c[:, a:a + 1],
                    scalar2=None, op0=ALU.is_ge),
                   ["cw_tm", "m1c", "tmp_r"], ["tmp_r"], tag=f"{nametag}e1{a}")
            OP("vector", lambda e: e.scalar_tensor_tensor(
                out=tmp_r2.ap().rearrange("p a e -> p (a e)"),
                in0=tmp_r.ap().rearrange("p a e -> p (a e)"), scalar=-1e30,
                in1=cw_tm.ap().rearrange("p a e -> p (a e)"),
                op0=ALU.mult, op1=ALU.add),
               ["tmp_r", "cw_tm", "tmp_r2"], ["tmp_r2"], tag=f"{nametag}l2")
            OP("vector", lambda e: e.tensor_reduce(
                out=m2c[:], in_=tmp_r2[:], axis=mybir.AxisListType.X, op=ALU.max),
               ["tmp_r2", "m2c"], ["m2c"], tag=f"{nametag}m2")
            for a in range(4):
                OP("vector", lambda e, a=a: e.tensor_scalar(
                    out=tmp_r2[:, a, :], in0=tmp_r2[:, a, :],
                    scalar1=m2c[:, a:a + 1], scalar2=None, op0=ALU.is_ge),
                   ["tmp_r2", "m2c"], ["tmp_r2"], tag=f"{nametag}e2{a}")
            OP("vector", lambda e: e.tensor_tensor(
                out=s1c[:], in0=m1c[:], in1=m2c[:], op=ALU.subtract),
               ["m1c", "m2c", "s1c"], ["s1c"], tag=f"{nametag}d")
            OP("scalar", lambda e: e.activation(out=s1c[:], in_=s1c[:],
                                                func=AF.Sigmoid),
               ["s1c"], ["s1c"], tag=f"{nametag}sg")
            OP("scalar", lambda e: e.activation(out=s2c[:], in_=s1c[:],
                                                func=AF.Copy, scale=-0.5,
                                                bias=0.5),
               ["s1c", "s2c"], ["s2c"], tag=f"{nametag}s2")
            OP("scalar", lambda e: e.activation(out=s1c[:], in_=s1c[:],
                                                func=AF.Copy, scale=0.5),
               ["s1c"], ["s1c"], tag=f"{nametag}s1")
            for a in range(4):
                OP("vector", lambda e, a=a: e.tensor_scalar(
                    out=tmp_r[:, a, :], in0=tmp_r[:, a, :], scalar1=s1c[:, a:a + 1],
                    scalar2=None, op0=ALU.mult),
                   ["tmp_r", "s1c"], ["tmp_r"], tag=f"{nametag}h1{a}")
                OP("vector", lambda e, a=a: e.tensor_scalar(
                    out=tmp_r2[:, a, :], in0=tmp_r2[:, a, :],
                    scalar1=s2c[:, a:a + 1], scalar2=None, op0=ALU.mult),
                   ["tmp_r2", "s2c"], ["tmp_r2"], tag=f"{nametag}h2{a}")
            OP("vector", lambda e: e.tensor_tensor(
                out=cw_tm.ap().rearrange("p a e -> p (a e)"),
                in0=tmp_r.ap().rearrange("p a e -> p (a e)"),
                in1=tmp_r2.ap().rearrange("p a e -> p (a e)"), op=ALU.add),
               ["tmp_r", "tmp_r2", "cw_tm"], ["cw_tm"], tag=f"{nametag}cw")
            for tt in range(NTT):
                OP("tensor", lambda e, tt=tt: e.transpose(
                    out=pw[1][0:8, tt * 128:(tt + 1) * 128],
                    in_=cw_tm[:, tt, :], identity=eye[:]),
                   ["cw_tm", "eye", "pw1"], ["pw1"], tag=f"{nametag}T{tt}")
            OP("vector", lambda e: e.tensor_copy(out=cwT_sb[:],
                                                 in_=pw[1][0:8, :]),
               ["pw1", "cwT_sb"], ["cwT_sb"], tag=f"{nametag}Te")
            dma("gpsimd", cwT_d[:, c * TC:(c + 1) * TC], cwT_sb[:],
                ["cwT_sb"], ["cwT_d"], f"{nametag}Ts")

        def ffn_expert(c, nametag, ei, filler=None):
            def fill():
                if filler:
                    filler()
            s = c % 2
            xr = f"xres{s}"
            if True:
                for blk in range(8):
                    b = blk % 2
                    cols = slice(blk * 256, (blk + 1) * 256)
                    dma("sync", w1b[:, b], w1bf_d[ei][:, :, cols],
                        [f"w1b{b}", f"w1d{ei}"], [f"w1b{b}"], f"{nametag}a{ei}{blk}")
                    dma("sync", w3b[:, b], w3bf_d[ei][:, :, cols],
                        [f"w3b{b}", f"w3d{ei}"], [f"w3b{b}"], f"{nametag}b{ei}{blk}")
                    for sub in range(2):
                        hc = blk * 2 + sub
                        pgi = hc % 2
                        pui = 2 + hc % 2
                        for dc in range(DCN):
                            OP("tensor", lambda e, b=b, sub=sub, dc=dc, pgi=pgi:
                               e.matmul(
                                   out=pg[pgi][:],
                                   lhsT=w1b[:, b, dc, sub * 128:(sub + 1) * 128],
                                   rhs=xnorm[:, dc, :], start=(dc == 0),
                                   stop=(dc == DCN - 1)),
                               [f"w1b{b}", "xnorm", f"pg{pgi}"], [f"pg{pgi}"],
                               tag=f"{nametag}g{ei}{hc}{dc}")
                        for dc in range(DCN):
                            OP("tensor", lambda e, b=b, sub=sub, dc=dc, pui=pui:
                               e.matmul(
                                   out=pg[pui][:],
                                   lhsT=w3b[:, b, dc, sub * 128:(sub + 1) * 128],
                                   rhs=xnorm[:, dc, :], start=(dc == 0),
                                   stop=(dc == DCN - 1)),
                               [f"w3b{b}", "xnorm", f"pg{pui}"], [f"pg{pui}"],
                               tag=f"{nametag}w{ei}{hc}{dc}")
                        OP("scalar", lambda e, pgi=pgi, hc=hc: e.activation(
                            out=sg[:, hc % 2, :], in_=pg[pgi][:], func=AF.Silu),
                           [f"pg{pgi}", f"sg{hc % 2}"], [f"sg{hc % 2}"],
                           tag=f"{nametag}s{ei}{hc}")
                        OP("vector", lambda e, pui=pui, hc=hc: e.tensor_tensor(
                            out=eo[:, hc, :], in0=sg[:, hc % 2, :], in1=pg[pui][:],
                            op=ALU.mult),
                           [f"sg{hc % 2}", f"pg{pui}"] + list(EOALL), list(EOALL),
                           tag=f"{nametag}e{ei}{hc}")
                        fill()
                if ei < E:
                    dma("gpsimd", cwb[:, ei % 2, :],
                        cwT_d[ei:ei + 1, c * TC:(c + 1) * TC].partition_broadcast(128),
                        ["cwT_d", f"cwb{ei % 2}"], [f"cwb{ei % 2}"],
                        f"{nametag}c{ei}")
                for dp in range(8):
                    b = dp % 2
                    dma("sync", w2b[:, b], w2bf_d[ei, dp],
                        [f"w2b{b}", f"w2d{ei}"], [f"w2b{b}"], f"{nametag}d{ei}{dp}")
                    pwi = dp % 2
                    for hc in range(NHC):
                        OP("tensor", lambda e, b=b, hc=hc, pwi=pwi: e.matmul(
                            out=pw[pwi][:], lhsT=w2b[:, b, hc, :], rhs=eo[:, hc, :],
                            start=(hc == 0), stop=(hc == NHC - 1)),
                           [f"w2b{b}", f"pw{pwi}"] + list(EOALL), [f"pw{pwi}"],
                           tag=f"{nametag}2{ei}{dp}{hc}")
                    if ei < E:
                        OP("vector", lambda e, pwi=pwi, ei=ei: e.tensor_tensor(
                            out=sge[:], in0=pw[pwi][:], in1=cwb[:, ei % 2, :],
                            op=ALU.mult),
                           [f"pw{pwi}", f"cwb{ei % 2}", "sge"], ["sge"],
                           tag=f"{nametag}v{ei}{dp}")
                    else:
                        OP("vector", lambda e, pwi=pwi: e.tensor_scalar(
                            out=sge[:], in0=pw[pwi][:], scalar1=0.5, scalar2=None,
                            op0=ALU.mult),
                           [f"pw{pwi}", "sge"], ["sge"], tag=f"{nametag}v{ei}{dp}")
                    OP("vector", lambda e, dp=dp, s=s: e.tensor_tensor(
                        out=xres[:, s, dp, :], in0=xres[:, s, dp, :], in1=sge[:],
                        op=ALU.add),
                       ["sge", xr], [xr], tag=f"{nametag}x{ei}{dp}")
                    fill()

        def out_proj_chunk(c, nametag):
            s = c % 2
            xr = f"xres{s}"
            for ep in range(8):
                b = ep % 2
                dma("sync", opb[:, b], oprojT_d[:, :, ep * 128:(ep + 1) * 128],
                    [f"opb{b}", "opd"], [f"opb{b}"], f"{nametag}l{ep}")
                pwi = ep % 2
                for dc in range(DCN):
                    OP("tensor", lambda e, b=b, dc=dc, pwi=pwi: e.matmul(
                        out=pw[pwi][:], lhsT=opb[:, b, dc, :], rhs=xnm[:, dc, :],
                        start=(dc == 0), stop=(dc == DCN - 1)),
                       [f"opb{b}", "xnm", f"pw{pwi}"], [f"pw{pwi}"],
                       tag=f"{nametag}m{ep}{dc}")
                OP("vector", lambda e, ep=ep, pwi=pwi, s=s: e.tensor_tensor(
                    out=xres[:, s, ep, :], in0=xres[:, s, ep, :], in1=pw[pwi][:],
                    op=ALU.add),
                   [f"pw{pwi}", xr], [xr], tag=f"{nametag}e{ep}")

        def halo_ln(dst_is_x3n, nametag):
            for dc in range(DCN):
                OP("tensor", lambda e, dc=dc: e.matmul(
                    out=pstat[0:1, 8:11], lhsT=ones_c[:], rhs=x3f[:, dc, 0:3],
                    start=(dc == 0), stop=(dc == DCN - 1)),
                   ["x3f", "ones_c", "pstat"], ["pstat"], tag=f"{nametag}s{dc}")
            for dc in range(DCN):
                OP("vector", lambda e, dc=dc: e.tensor_tensor(
                    out=sg3[:, 0:3], in0=x3f[:, dc, 0:3], in1=x3f[:, dc, 0:3],
                    op=ALU.mult),
                   ["x3f", "sg3"], ["sg3"], tag=f"{nametag}q{dc}")
                OP("tensor", lambda e, dc=dc: e.matmul(
                    out=pstat[0:1, 12:15], lhsT=ones_c[:], rhs=sg3[:, 0:3],
                    start=(dc == 0), stop=(dc == DCN - 1)),
                   ["sg3", "ones_c", "pstat"], ["pstat"], tag=f"{nametag}k{dc}")
            OP("scalar", lambda e: e.activation(
                out=m_row[0:1, 0:3], in_=pstat[0:1, 8:11], func=AF.Copy,
                scale=1.0 / D), ["pstat", "m_row"], ["m_row"], tag=f"{nametag}m")
            OP("scalar", lambda e: e.activation(
                out=q_row[0:1, 0:3], in_=pstat[0:1, 12:15], func=AF.Copy,
                scale=1.0 / D), ["pstat", "q_row"], ["q_row"], tag=f"{nametag}qq")
            OP("vector", lambda e: e.tensor_tensor(
                out=mr_row[0:1, 0:3], in0=m_row[0:1, 0:3], in1=m_row[0:1, 0:3],
                op=ALU.mult), ["m_row", "mr_row"], ["mr_row"], tag=f"{nametag}mm")
            OP("vector", lambda e: e.tensor_tensor(
                out=q_row[0:1, 0:3], in0=q_row[0:1, 0:3], in1=mr_row[0:1, 0:3],
                op=ALU.subtract), ["q_row", "mr_row"], ["q_row"], tag=f"{nametag}v")
            OP("scalar", lambda e: e.activation(out=q_row[0:1, 0:3],
                                                in_=q_row[0:1, 0:3],
                                                func=AF.Ln, bias=cvals[0:1, 0:1]),
               ["q_row", "cvals"], ["q_row"], tag=f"{nametag}l")
            OP("scalar", lambda e: e.activation(out=q_row[0:1, 0:3],
                                                in_=q_row[0:1, 0:3],
                                                func=AF.Exp, scale=-0.5),
               ["q_row"], ["q_row"], tag=f"{nametag}x")
            OP("vector", lambda e: e.tensor_tensor(
                out=mr_row[0:1, 0:3], in0=m_row[0:1, 0:3], in1=q_row[0:1, 0:3],
                op=ALU.mult), ["m_row", "q_row", "mr_row"], ["mr_row"],
               tag=f"{nametag}mr")
            dma("gpsimd", hrows_d[0:1, 0:3], q_row[0:1, 0:3], ["q_row"],
                ["hrows_d"], f"{nametag}w1")
            dma("gpsimd", hrows_d[1:2, 0:3], mr_row[0:1, 0:3], ["mr_row"],
                ["hrows_d"], f"{nametag}w2")
            dma("gpsimd", bc[:, 0, 0:3], hrows_d[0:1, 0:3].partition_broadcast(128),
                ["hrows_d", "bc0"], ["bc0"], f"{nametag}b1")
            dma("gpsimd", bc[:, 1, 0:3], hrows_d[1:2, 0:3].partition_broadcast(128),
                ["hrows_d", "bc1"], ["bc1"], f"{nametag}b2")
            for dc in range(DCN):
                OP("vector", lambda e, dc=dc: e.tensor_tensor(
                    out=sg3[:, 0:3], in0=x3f[:, dc, 0:3], in1=bc[:, 0, 0:3],
                    op=ALU.mult), ["x3f", "bc0", "sg3"], ["sg3"],
                   tag=f"{nametag}a{dc}")
                OP("vector", lambda e, dc=dc: e.tensor_tensor(
                    out=x3n[:, dc, 0:3], in0=sg3[:, 0:3], in1=bc[:, 1, 0:3],
                    op=ALU.subtract), ["sg3", "bc1", "x3n"], ["x3n"],
                   tag=f"{nametag}b{dc}")

        def halo_path():
            dma("sync", x3, x_halo, ["eoA"], ["eoA"], "hxl")
            OP("vector", lambda e: e.memset(x3f[:], 0.0), ["x3f"], ["x3f"],
               tag="hx0")
            for dc in range(DCN):
                OP("tensor", lambda e, dc=dc: e.transpose(
                    out=pstat[:, 0:3], in_=x3[:, dc * 128:(dc + 1) * 128],
                    identity=eye[0:3, 0:3]),
                   ["eoA", "eye", "pstat"], ["pstat"], tag=f"hT{dc}")
                OP("vector", lambda e, dc=dc: e.tensor_copy(
                    out=x3f[:, dc, 0:3], in_=pstat[:, 0:3]),
                   ["pstat", "x3f"], ["x3f"], tag=f"hE{dc}")
            halo_ln(True, "hl1")
            for dc in range(DCN):
                OP("tensor", lambda e, dc=dc: e.matmul(
                    out=pstat[0:3, 16:24], lhsT=x3n[:, dc, 0:3],
                    rhs=routerT[:, dc, :], start=(dc == 0), stop=False),
                   ["x3n", "routerT", "pstat"], ["pstat"], tag=f"hg{dc}")
            OP("tensor", lambda e: e.matmul(
                out=pstat[0:3, 16:24], lhsT=ones_bf[0:1, 0:3], rhs=ebias_bf[:],
                start=False, stop=True),
               ["ones_bf", "ebias_bf", "pstat"], ["pstat"], tag="hgb")
            OP("vector", lambda e: e.tensor_copy(out=cw3_tm[:],
                                                 in_=pstat[0:3, 16:24]),
               ["pstat", "cw3"], ["cw3"], tag="hcp")
            OP("vector", lambda e: e.tensor_reduce(
                out=m1c[0:3, 0:1], in_=cw3_tm[:], axis=mybir.AxisListType.X,
                op=ALU.max), ["cw3", "m1c"], ["m1c"], tag="hm1")
            OP("vector", lambda e: e.tensor_scalar(
                out=tmp_r[0:3, 0, :], in0=cw3_tm[:], scalar1=m1c[0:3, 0:1],
                scalar2=None, op0=ALU.is_ge),
               ["cw3", "m1c", "tmp_r"], ["tmp_r"], tag="he1")
            OP("vector", lambda e: e.scalar_tensor_tensor(
                out=tmp_r2[0:3, 0, :], in0=tmp_r[0:3, 0, :], scalar=-1e30,
                in1=cw3_tm[:], op0=ALU.mult, op1=ALU.add),
               ["tmp_r", "cw3", "tmp_r2"], ["tmp_r2"], tag="hl2")
            OP("vector", lambda e: e.tensor_reduce(
                out=m2c[0:3, 0:1], in_=tmp_r2[0:3, 0, :],
                axis=mybir.AxisListType.X, op=ALU.max),
               ["tmp_r2", "m2c"], ["m2c"], tag="hm2")
            OP("vector", lambda e: e.tensor_scalar(
                out=tmp_r2[0:3, 0, :], in0=tmp_r2[0:3, 0, :],
                scalar1=m2c[0:3, 0:1], scalar2=None, op0=ALU.is_ge),
               ["tmp_r2", "m2c"], ["tmp_r2"], tag="he2")
            OP("vector", lambda e: e.tensor_tensor(
                out=s1c[0:3, 0:1], in0=m1c[0:3, 0:1], in1=m2c[0:3, 0:1],
                op=ALU.subtract), ["m1c", "m2c", "s1c"], ["s1c"], tag="hd")
            OP("scalar", lambda e: e.activation(out=s1c[0:3, 0:1],
                                                in_=s1c[0:3, 0:1],
                                                func=AF.Sigmoid),
               ["s1c"], ["s1c"], tag="hsg")
            OP("scalar", lambda e: e.activation(out=s2c[0:3, 0:1],
                                                in_=s1c[0:3, 0:1], func=AF.Copy,
                                                scale=-0.5, bias=0.5),
               ["s1c", "s2c"], ["s2c"], tag="hs2")
            OP("scalar", lambda e: e.activation(out=s1c[0:3, 0:1],
                                                in_=s1c[0:3, 0:1], func=AF.Copy,
                                                scale=0.5),
               ["s1c"], ["s1c"], tag="hs1")
            OP("vector", lambda e: e.tensor_scalar(
                out=tmp_r[0:3, 0, :], in0=tmp_r[0:3, 0, :], scalar1=s1c[0:3, 0:1],
                scalar2=None, op0=ALU.mult), ["tmp_r", "s1c"], ["tmp_r"],
               tag="hh1")
            OP("vector", lambda e: e.tensor_scalar(
                out=tmp_r2[0:3, 0, :], in0=tmp_r2[0:3, 0, :],
                scalar1=s2c[0:3, 0:1], scalar2=None, op0=ALU.mult),
               ["tmp_r2", "s2c"], ["tmp_r2"], tag="hh2")
            OP("vector", lambda e: e.tensor_tensor(
                out=cw3_tm[:], in0=tmp_r[0:3, 0, :], in1=tmp_r2[0:3, 0, :],
                op=ALU.add), ["tmp_r", "tmp_r2", "cw3"], ["cw3"], tag="hcw")
            OP("tensor", lambda e: e.transpose(
                out=pstat[0:8, 32:35], in_=cw3_tm[:], identity=eye[0:3, 0:3]),
               ["cw3", "eye", "pstat"], ["pstat"], tag="hcT")
            OP("vector", lambda e: e.tensor_copy(out=cw3T[:, 0:3],
                                                 in_=pstat[0:8, 32:35]),
               ["pstat", "cw3T"], ["cw3T"], tag="hcTe")
            dma("gpsimd", cw3_d[:, 0:3], cw3T[:, 0:3], ["cw3T"], ["cw3_d"], "hcTs")
            for ei in range(NEXP):
                for blk in range(8):
                    b = blk % 2
                    cols = slice(blk * 256, (blk + 1) * 256)
                    dma("sync", w1b[:, b], w1bf_d[ei][:, :, cols],
                        [f"w1b{b}", f"w1d{ei}"], [f"w1b{b}"], f"ha{ei}{blk}")
                    dma("sync", w3b[:, b], w3bf_d[ei][:, :, cols],
                        [f"w3b{b}", f"w3d{ei}"], [f"w3b{b}"], f"hb{ei}{blk}")
                    for sub in range(2):
                        hc = blk * 2 + sub
                        for dc in range(DCN):
                            OP("tensor", lambda e, b=b, sub=sub, dc=dc: e.matmul(
                                out=pg[0][:, 0:3],
                                lhsT=w1b[:, b, dc, sub * 128:(sub + 1) * 128],
                                rhs=x3n[:, dc, 0:3], start=(dc == 0),
                                stop=(dc == DCN - 1)),
                               [f"w1b{b}", "x3n", "pg0"], ["pg0"],
                               tag=f"hg{ei}{hc}{dc}")
                        for dc in range(DCN):
                            OP("tensor", lambda e, b=b, sub=sub, dc=dc: e.matmul(
                                out=pg[2][:, 0:3],
                                lhsT=w3b[:, b, dc, sub * 128:(sub + 1) * 128],
                                rhs=x3n[:, dc, 0:3], start=(dc == 0),
                                stop=(dc == DCN - 1)),
                               [f"w3b{b}", "x3n", "pg2"], ["pg2"],
                               tag=f"hu{ei}{hc}{dc}")
                        OP("scalar", lambda e: e.activation(
                            out=sg3[:, 0:3], in_=pg[0][:, 0:3], func=AF.Silu),
                           ["pg0", "sg3"], ["sg3"], tag=f"hs{ei}{hc}")
                        OP("vector", lambda e, hc=hc: e.tensor_tensor(
                            out=eo3[:, hc, 0:3], in0=sg3[:, 0:3],
                            in1=pg[2][:, 0:3], op=ALU.mult),
                           ["sg3", "pg2", "eo3"], ["eo3"], tag=f"he{ei}{hc}")
                if ei < E:
                    dma("gpsimd", cwb3[:, 0:3],
                        cw3_d[ei:ei + 1, 0:3].partition_broadcast(128),
                        ["cw3_d", "cwb3"], ["cwb3"], f"hc{ei}")
                for dp in range(8):
                    b = dp % 2
                    dma("sync", w2b[:, b], w2bf_d[ei, dp],
                        [f"w2b{b}", f"w2d{ei}"], [f"w2b{b}"], f"hd{ei}{dp}")
                    for hc in range(NHC):
                        OP("tensor", lambda e, b=b, hc=hc: e.matmul(
                            out=pw[0][:, 0:3], lhsT=w2b[:, b, hc, :],
                            rhs=eo3[:, hc, 0:3], start=(hc == 0),
                            stop=(hc == NHC - 1)),
                           [f"w2b{b}", "eo3", "pw0"], ["pw0"],
                           tag=f"h2{ei}{dp}{hc}")
                    if ei < E:
                        OP("vector", lambda e: e.tensor_tensor(
                            out=sg3[:, 0:3], in0=pw[0][:, 0:3], in1=cwb3[:, 0:3],
                            op=ALU.mult), ["pw0", "cwb3", "sg3"], ["sg3"],
                           tag=f"hv{ei}{dp}")
                    else:
                        OP("vector", lambda e: e.tensor_scalar(
                            out=sg3[:, 0:3], in0=pw[0][:, 0:3], scalar1=0.5,
                            scalar2=None, op0=ALU.mult),
                           ["pw0", "sg3"], ["sg3"], tag=f"hv{ei}{dp}")
                    OP("vector", lambda e, dp=dp: e.tensor_tensor(
                        out=x3f[:, dp, 0:3], in0=x3f[:, dp, 0:3],
                        in1=sg3[:, 0:3], op=ALU.add),
                       ["sg3", "x3f"], ["x3f"], tag=f"hx{ei}{dp}")
            halo_ln(True, "hl2")
            for et in range(8):
                b = et % 2
                dma("sync", ipb[:, b], iprojT_d[:, :, et * 128:(et + 1) * 128],
                    [f"ipb{b}", "ipd"], [f"ipb{b}"], f"hil{et}")
                for dc in range(DCN):
                    OP("tensor", lambda e, b=b, dc=dc, et=et: e.matmul(
                        out=pg[et % 2][:, 0:3], lhsT=ipb[:, b, dc, :],
                        rhs=x3n[:, dc, 0:3], start=(dc == 0),
                        stop=(dc == DCN - 1)),
                       [f"ipb{b}", "x3n", f"pg{et % 2}"], [f"pg{et % 2}"],
                       tag=f"hi{et}{dc}")
                OP("vector", lambda e, et=et: e.tensor_copy(
                    out=y3[:, et, 0:3], in_=pg[et % 2][:, 0:3]),
                   [f"pg{et % 2}", "y3"], ["y3"], tag=f"hie{et}")
            for dc in range(DCN):
                OP("vector", lambda e, dc=dc: e.tensor_scalar(
                    out=tail3[:, dc, :], in0=y3[:, dc, 0:3],
                    scalar1=masks[:, 1:2], scalar2=None, op0=ALU.mult),
                   ["y3", "masks", "tail3"], ["tail3"], tag=f"hms{dc}")

        halo_path()

        def scan_p1_pre(c, nametag):
            dma("gpsimd", bc[:, 2, :], rows_d[c, 32:33, :].partition_broadcast(128),
                [f"rows{c}", "bc2"], ["bc2"], f"{nametag}bd")

        def scan_p1_step(c, nametag, n, dg):
            bs = n % 2
            if dg == 0:
                dma("gpsimd", bc[:, bs, :],
                    rows_d[c, n:n + 1, :].partition_broadcast(128),
                    [f"rows{c}", f"bc{bs}"], [f"bc{bs}"], f"{nametag}bB{n}")
                dma("gpsimd", scb[:, bs, :],
                    rows_d[c, 16 + n:17 + n, :].partition_broadcast(128),
                    [f"rows{c}", f"scb{bs}"], [f"scb{bs}"], f"{nametag}bC{n}")
            if True:
                ds = (n * DCN + dg) % 2
                OP("scalar", lambda e, dg=dg, n=n, ds=ds: e.activation(
                    out=dAt[:, ds, :], in_=bc[:, 2, :], func=AF.Exp,
                    scale=A_sb[:, dg, n:n + 1]),
                   ["bc2", "A", f"dAt{ds}"], [f"dAt{ds}"],
                   tag=f"{nametag}A{n}{dg}")
                OP("vector", lambda e, dg=dg, bs=bs: e.tensor_tensor(
                    out=ut[:], in0=xconv[:, dg, :], in1=bc[:, bs, :],
                    op=ALU.mult),
                   ["xconv", f"bc{bs}", "ut"], ["ut"], tag=f"{nametag}u{n}{dg}")
                OP("vector", lambda e, dg=dg, n=n, ds=ds: e.tensor_tensor_scan(
                    out=hst[:], data0=dAt[:, ds, :], data1=ut[:],
                    initial=hstate[:, dg, n:n + 1], op0=ALU.mult, op1=ALU.add),
                   [f"dAt{ds}", "ut", "hstate", "hst"], ["hst"],
                   tag=f"{nametag}S{n}{dg}")
                OP("vector", lambda e, dg=dg, n=n: e.tensor_copy(
                    out=hstate[:, dg, n:n + 1], in_=hst[:, TC - 1:TC]),
                   ["hst", "hstate"], ["hstate"], tag=f"{nametag}h{n}{dg}")
                if n == 0:
                    OP("gpsimd", lambda e, dg=dg, bs=bs: e.tensor_tensor(
                        out=yacc[:, dg, :], in0=hst[:], in1=scb[:, bs, :],
                        op=ALU.mult),
                       ["hst", f"scb{bs}", "yacc"], ["yacc"],
                       tag=f"{nametag}y{n}{dg}")
                else:
                    OP("gpsimd", lambda e, dg=dg, bs=bs: e.tensor_tensor(
                        out=ut2[:], in0=hst[:], in1=scb[:, bs, :], op=ALU.mult),
                       ["hst", f"scb{bs}", "ut2"], ["ut2"],
                       tag=f"{nametag}p{n}{dg}")
                    OP("gpsimd", lambda e, dg=dg: e.tensor_tensor(
                        out=yacc[:, dg, :], in0=yacc[:, dg, :], in1=ut2[:],
                        op=ALU.add),
                       ["ut2", "yacc"], ["yacc"], tag=f"{nametag}a{n}{dg}")

        def scan_p1_fin(c, nametag):
            for dg in range(DCN):
                OP("vector", lambda e, dg=dg: e.scalar_tensor_tensor(
                    out=yacc[:, dg, :], in0=xconv[:, dg, :],
                    scalar=D_sb[:, dg:dg + 1], in1=yacc[:, dg, :],
                    op0=ALU.mult, op1=ALU.add),
                   ["xconv", "Dsb", "yacc"], ["yacc"], tag=f"{nametag}D{dg}")
                OP("vector", lambda e, dg=dg: e.tensor_tensor(
                    out=xnm[:, dg, :], in0=yacc[:, dg, :], in1=szbf_w[:, dg, :],
                    op=ALU.mult),
                   ["yacc", "szbf_w", "xnm"], ["xnm"], tag=f"{nametag}G{dg}")

        def front(c):
            s = c % 2
            xr = f"xres{s}"
            ln_chunk(c, f"1B{c}", outbuf=xnm, outres="xnm")
            for et in range(16):
                b = et % 2
                dma("sync", ipb[:, b], iprojT_d[:, :, et * 128:(et + 1) * 128],
                    [f"ipb{b}", "ipd"], [f"ipb{b}"], f"1il{c}{et}")
                pgi = et % 4
                for dc in range(DCN):
                    OP("tensor", lambda e, b=b, dc=dc, pgi=pgi: e.matmul(
                        out=pg[pgi][:], lhsT=ipb[:, b, dc, :], rhs=xnm[:, dc, :],
                        start=(dc == 0), stop=(dc == DCN - 1)),
                       [f"ipb{b}", "xnm", f"pg{pgi}"], [f"pg{pgi}"],
                       tag=f"1i{c}{et}{dc}")
                if et < 8:
                    OP("vector", lambda e, et=et, pgi=pgi: e.tensor_copy(
                        out=xssm[:, et, 3:515], in_=pg[pgi][:]),
                       [f"pg{pgi}", "xssm"], ["xssm"], tag=f"1ie{c}{et}")
                else:
                    OP("scalar", lambda e, et=et, pgi=pgi: e.activation(
                        out=szbf_w[:, et - 8, :], in_=pg[pgi][:], func=AF.Silu),
                       [f"pg{pgi}", "szbf_w"], ["szbf_w"], tag=f"1iz{c}{et}")
            dma("scalar", szbf_d[c], szbf_w[:], ["szbf_w"], [f"szd{c}"],
                f"1zs{c}")
            if c == 0:
                OP("vector", lambda e: e.tensor_copy(out=xssm[:, :, 0:3],
                                                     in_=tail3[:]),
                   ["tail3", "xssm"], ["xssm"], tag="1t0")
            for dg in range(DCN):
                OP("vector", lambda e, dg=dg: e.tensor_scalar(
                    out=xconv[:, dg, :], in0=xssm[:, dg, 0:TC],
                    scalar1=convw[:, dg, 0:1], scalar2=None, op0=ALU.mult),
                   ["xssm", "convw", "xconv"], ["xconv"], tag=f"1c0{c}{dg}")
                for k in range(1, 4):
                    OP("vector", lambda e, dg=dg, k=k: e.scalar_tensor_tensor(
                        out=xconv[:, dg, :], in0=xssm[:, dg, k:k + TC],
                        scalar=convw[:, dg, k:k + 1], in1=xconv[:, dg, :],
                        op0=ALU.mult, op1=ALU.add),
                       ["xssm", "convw", "xconv"], ["xconv"],
                       tag=f"1c{k}{c}{dg}")
                OP("scalar", lambda e, dg=dg: e.activation(
                    out=xconv[:, dg, :], in_=xconv[:, dg, :], func=AF.Silu,
                    bias=convb[:, dg:dg + 1]),
                   ["xconv", "convb"], ["xconv"], tag=f"1cs{c}{dg}")
                OP("scalar", lambda e, dg=dg: e.copy(
                    out=xconv_bf[:, dg, :], in_=xconv[:, dg, :]),
                   ["xconv", "yacc"], ["yacc"], tag=f"1cb{c}{dg}")
            OP("vector", lambda e: e.tensor_copy(
                out=xssm[:, :, 0:3], in_=xssm[:, :, 512:515]),
               ["xssm"], ["xssm"], tag=f"1tl{c}")
            for dc in range(DCN):
                OP("tensor", lambda e, dc=dc: e.matmul(
                    out=pstat[0:80, :], lhsT=xprojT[:, dc, :],
                    rhs=xconv_bf[:, dc, :], start=(dc == 0),
                    stop=(dc == DCN - 1)),
                   ["yacc", "xprojT", "pstat"], ["pstat"], tag=f"1xp{c}{dc}")
            OP("scalar", lambda e: e.activation(out=drow[:], in_=pstat[64:65, :],
                                                func=AF.Abs),
               ["pstat", "drow"], ["drow"], tag=f"1d1{c}")
            OP("scalar", lambda e: e.activation(out=drow[:], in_=drow[:],
                                                func=AF.Exp, scale=-1.0),
               ["drow"], ["drow"], tag=f"1d2{c}")
            OP("scalar", lambda e: e.activation(out=drow[:], in_=drow[:],
                                                func=AF.Ln, bias=1.0),
               ["drow"], ["drow"], tag=f"1d3{c}")
            OP("scalar", lambda e: e.activation(out=srow[:], in_=pstat[64:65, :],
                                                func=AF.Relu),
               ["pstat", "q_row"], ["q_row"], tag=f"1d4{c}")
            OP("vector", lambda e: e.tensor_tensor(out=drow[:], in0=drow[:],
                                                   in1=srow[:], op=ALU.add),
               ["drow", "q_row"], ["drow"], tag=f"1d5{c}")
            dma("gpsimd", rows_d[c, 32:33, :], drow[:], ["drow"], [f"rows{c}"],
                f"1dw{c}")
            dma("gpsimd", scanrows[0:16, :],
                rows_d[c, 32:33, :].partition_broadcast(16),
                [f"rows{c}", "scanrows"], ["scanrows"], f"1db{c}")
            OP("vector", lambda e: e.tensor_tensor(
                out=scanrows[0:16, :], in0=pstat[0:16, :], in1=scanrows[0:16, :],
                op=ALU.mult),
               ["pstat", "scanrows"], ["scanrows"], tag=f"1dB{c}")
            OP("vector", lambda e: e.tensor_copy(out=scanrows[32:48, :],
                                                 in_=pstat[32:48, :]),
               ["pstat", "scanrows"], ["scanrows"], tag=f"1Ce{c}")
            dma("gpsimd", rows_d[c, 0:16, :], scanrows[0:16, :],
                ["scanrows"], [f"rows{c}"], f"1dBs{c}")
            dma("gpsimd", rows_d[c, 16:32, :], scanrows[32:48, :],
                ["scanrows"], [f"rows{c}"], f"1Cs{c}")

        def load_x(c):
            s = c % 2
            xr = f"xres{s}"
            dma("sync", xres[:, s], xres_d[c], [xr, f"xrd{c}"], [xr], f"1xl{c}")

        def chunk_head(c, with_load=True):
            if with_load:
                load_x(c)
            ln_chunk(c, f"1A{c}")
            router_chunk(c, f"1R{c}")

        def p1_tail(c):
            scan_p1_fin(c, f"1s{c}")
            out_proj_chunk(c, f"1o{c}")
            s = c % 2
            dma("sync", xres_d[c], xres[:, s], [f"xres{s}"],
                [f"xres{s}", f"xrd{c}"], f"1xs{c}")

        def make_filler(steps):
            def filler():
                if steps:
                    steps.pop(0)()
            def drain():
                while steps:
                    steps.pop(0)()
            return filler, drain

        def p3loads(c):
            s = c % 2
            xr = f"xres{s}"
            dma("sync", xres[:, s], xres_d[c], [xr, f"xrd{c}"], [xr], f"3xl{c}")
            dma("scalar", szbf_w[:], szbf_d[c], [f"szd{c}", "szbf_w"],
                ["szbf_w"], f"3zl{c}")

        def pass2_pre(c):
            dma("gpsimd", bc[:, 2, :], rows_d[c, 33:34, :].partition_broadcast(128),
                [f"rows{c}", "bc2"], ["bc2"], f"3bS{c}")

        def pass2_step(c, n, dg):
            bs = n % 2
            if dg == 0:
                dma("gpsimd", scb[:, bs, :],
                    rows_d[c, 16 + n:17 + n, :].partition_broadcast(128),
                    [f"rows{c}", f"scb{bs}"], [f"scb{bs}"], f"3bC{c}{n}")
            if True:
                ds = (n * DCN + dg) % 2
                OP("scalar", lambda e, dg=dg, n=n, ds=ds: e.activation(
                    out=dAt[:, ds, :], in_=bc[:, 2, :], func=AF.Exp,
                    scale=A_sb[:, dg, n:n + 1]),
                   ["bc2", "A", f"dAt{ds}"], [f"dAt{ds}"],
                   tag=f"3E{c}{n}{dg}")
                OP("vector", lambda e, dg=dg, n=n, ds=ds, bs=bs:
                   e.scalar_tensor_tensor(
                       out=ut[:], in0=dAt[:, ds, :],
                       scalar=hin[:, dg, n:n + 1], in1=scb[:, bs, :],
                       op0=ALU.mult, op1=ALU.mult),
                   [f"dAt{ds}", "hin", f"scb{bs}", "ut"], ["ut"],
                   tag=f"3t{c}{n}{dg}")
                if n == 0:
                    OP("vector", lambda e, dg=dg: e.tensor_copy(
                        out=yacc[:, dg, :], in_=ut[:]),
                       ["ut", "yacc"], ["yacc"], tag=f"3y{c}{n}{dg}")
                else:
                    OP("vector", lambda e, dg=dg: e.tensor_tensor(
                        out=yacc[:, dg, :], in0=yacc[:, dg, :], in1=ut[:],
                        op=ALU.add),
                       ["ut", "yacc"], ["yacc"], tag=f"3a{c}{n}{dg}")

        def pass2_fin(c):
            for dg in range(DCN):
                OP("vector", lambda e, dg=dg: e.tensor_tensor(
                    out=xnm[:, dg, :], in0=yacc[:, dg, :], in1=szbf_w[:, dg, :],
                    op=ALU.mult),
                   ["yacc", "szbf_w", "xnm"], ["xnm"], tag=f"3g{c}{dg}")
            out_proj_chunk(c, f"3o{c}")
            ln_chunk(c, f"3A{c}")
            router_chunk(c, f"3R{c}")

        def output_chunk(c):
            s = c % 2
            xr = f"xres{s}"
            for tt in range(NTT):
                ob = tt % 2
                for dc in range(DCN):
                    OP("tensor", lambda e, tt=tt, dc=dc, s=s: e.transpose(
                        out=pstat[:, 128:256],
                        in_=xres[:, s, dc, tt * 128:(tt + 1) * 128],
                        identity=eye[:]),
                       [xr, "eye", "pstat"], ["pstat"], tag=f"3T{c}{tt}{dc}")
                    OP("vector", lambda e, ob=ob, dc=dc: e.tensor_copy(
                        out=outstg[:, ob, dc * 128:(dc + 1) * 128],
                        in_=pstat[:, 128:256]),
                       ["pstat", f"os{ob}"], [f"os{ob}", "xssm"],
                       tag=f"3O{c}{tt}{dc}")
                trow = c * TC + tt * 128
                dma("sync", out_d[trow:trow + 128, :], outstg[:, ob],
                    [f"os{ob}"], [f"os{ob}done"], f"3os{c}{tt}")


        chunk_head(0)
        for c in range(NCH):
            if c >= 1:
                steps = [
                    (lambda cc=c - 1, n=n, dg=dg:
                     scan_p1_step(cc, f"1s{cc}", n, dg))
                    for n in range(NST) for dg in range(DCN)]
                filler, drain = make_filler(steps)
            else:
                filler, drain = None, lambda: None
            for ei in range(NEXP):
                ffn_expert(c, f"1F{c}", ei, filler=(filler if ei >= 2 else None))
                if c >= 1 and ei == 0:
                    front(c - 1)
                    scan_p1_pre(c - 1, f"1s{c - 1}")
                if c == 0 and ei == 1:
                    load_x(1)
            drain()
            if c == 0:
                chunk_head(1, with_load=False)
            if c >= 1:
                p1_tail(c - 1)
                if c + 1 < NCH:
                    chunk_head(c + 1)
        front(3)
        sview = eo_flat[0:1, 0:2 * T]
        ones_r2k = yacc.ap().rearrange("p a b -> p (a b)")[0:1, 0:TC]
        OP("vector", lambda e: e.memset(ones_r2k[0:1, :], 1.0),
           ["yacc"], ["ones_r2k", "yacc"], tag="Sones")
        dma("gpsimd", sview[0:1, 0:T].rearrange("r (c t) -> r c t", c=NCH),
            rows_d[:, 32:33, :].transpose([1, 0, 2]),
            list(EOALL) + [f"rows{c}" for c in range(NCH)], list(EOALL), "Sload")
        for sc in range(NCH):
            OP("vector", lambda e, sc=sc: e.tensor_tensor_scan(
                out=sview[0:1, T + sc * TC:T + (sc + 1) * TC],
                data0=ones_r2k[0:1, :],
                data1=sview[0:1, sc * TC:(sc + 1) * TC],
                initial=0.0, op0=ALU.mult, op1=ALU.add),
               list(EOALL) + ["ones_r2k"], list(EOALL), tag=f"Sscan{sc}")
        for sc in range(1, NCH):
            OP("vector", lambda e, sc=sc: e.tensor_scalar(
                out=sview[0:1, T + sc * TC:T + (sc + 1) * TC],
                in0=sview[0:1, T + sc * TC:T + (sc + 1) * TC],
                scalar1=sview[0:1, T + sc * TC - 1:T + sc * TC],
                scalar2=None, op0=ALU.add),
               list(EOALL), list(EOALL), tag=f"Soff{sc}")
        dma("gpsimd", rows_d[:, 33:34, :].transpose([1, 0, 2]),
            sview[0:1, T:2 * T].rearrange("r (c t) -> r c t", c=NCH),
            list(EOALL),
            [f"rows{c}" for c in range(NCH)], "Sstore")
        scan_p1_pre(3, "1s3")
        for n in range(NST):
            for dg in range(DCN):
                scan_p1_step(3, "1s3", n, dg)
        OP("vector", lambda e: e.tensor_scalar(
            out=ccs[:], in0=hstate.ap().rearrange("p dg n -> p (dg n)"),
            scalar1=masks[:, 0:1], scalar2=None, op0=ALU.mult),
           ["hstate", "masks", "ccs"], ["ccs"], tag="ccm")
        dma("gpsimd", cc_in_d[:, :], ccs[:], ["ccs"], ["cc_in"], "ccw")
        OP("gpsimd", lambda e: e.collective_compute(
            "AllReduce", ALU.add,
            replica_groups=[[0, 1], [2, 3], [4, 5], [6, 7]],
            ins=[cc_in_d[:, :]], outs=[cc_out_d[:, :]]),
           ["cc_in", "cc_out"], ["cc_out"], tag="cc")
        dma("gpsimd", ccs[:], cc_out_d[:, :], ["cc_out", "ccs"], ["ccs"], "ccr")
        OP("vector", lambda e: e.tensor_scalar(
            out=hin.ap().rearrange("p dg n -> p (dg n)"), in0=ccs[:],
            scalar1=masks[:, 1:2], scalar2=None, op0=ALU.mult),
           ["ccs", "masks", "hin"], ["hin"], tag="cci")
        p1_tail(3)
        p3loads(0)
        pass2_pre(0)
        for n in range(NST):
            for dg in range(DCN):
                pass2_step(0, n, dg)
        pass2_fin(0)
        p3loads(1)
        for c in range(NCH):
            if c + 1 < NCH:
                pass2_pre(c + 1)
                steps = [
                    (lambda cc=c + 1, n=n, dg=dg: pass2_step(cc, n, dg))
                    for n in range(NST) for dg in range(DCN)]
                filler, drain = make_filler(steps)
            else:
                filler, drain = None, lambda: None
            for ei in range(NEXP):
                ffn_expert(c, f"3F{c}", ei, filler=filler)
            drain()
            if c + 1 < NCH:
                pass2_fin(c + 1)
            output_chunk(c)
            if c + 2 < NCH:
                p3loads(c + 2)

        OP("vector", lambda e: e.engine_nop(),
           reads=[], writes=["os0done", "os1done"], tag="final")

        sched.prepare()
        nw = [0]

        @block.sync
        def _(eng):
            nw[0] += sched.emit_engine("sync", eng, semdict)

        @block.scalar
        def _(eng):
            nw[0] += sched.emit_engine("scalar", eng, semdict)

        @block.vector
        def _(eng):
            nw[0] += sched.emit_engine("vector", eng, semdict)

        @block.tensor
        def _(eng):
            nw[0] += sched.emit_engine("tensor", eng, semdict)

        @block.gpsimd
        def _(eng):
            nw[0] += sched.emit_engine("gpsimd", eng, semdict)

        import os
        if os.environ.get("KB_DEBUG"):
            print(f"[kernel] items={len(sched.items)} waits={nw[0]} "
                  f"ninst={nc.n_instructions()}")

    return nc


_PROG_CACHE = {}


def kernel(**inputs):
    x = np.asarray(inputs["L0_attn_x"])
    if "prog" not in _PROG_CACHE:
        _PROG_CACHE["prog"] = build_program()
    nc = _PROG_CACHE["prog"]

    eye = np.eye(128, dtype=np.float32)
    base = {
        "w_inproj": np.ascontiguousarray(inputs["L0_attn_in_proj_w"], np.float32),
        "w_convw": np.ascontiguousarray(inputs["L0_attn_conv_w"], np.float32),
        "w_convb": np.ascontiguousarray(inputs["L0_attn_conv_b"], np.float32),
        "w_xproj": np.ascontiguousarray(inputs["L0_attn_x_proj_w"], np.float32),
        "w_alog": np.ascontiguousarray(inputs["L0_attn_A_log"], np.float32),
        "w_dpar": np.ascontiguousarray(inputs["L0_attn_D_param"], np.float32),
        "w_outproj": np.ascontiguousarray(inputs["L0_attn_out_proj_w"], np.float32),
        "w_router": np.ascontiguousarray(inputs["L0_ffn_router_w"], np.float32),
        "w_ebias": np.ascontiguousarray(inputs["L0_ffn_experts_bias"], np.float32),
        "w_w1": np.ascontiguousarray(inputs["L0_ffn_w1"], np.float32),
        "w_w3": np.ascontiguousarray(inputs["L0_ffn_w3"], np.float32),
        "w_w2": np.ascontiguousarray(inputs["L0_ffn_w2"], np.float32),
        "w_sw1": np.ascontiguousarray(inputs["L0_ffn_sw1"], np.float32),
        "w_sw3": np.ascontiguousarray(inputs["L0_ffn_sw3"], np.float32),
        "w_sw2": np.ascontiguousarray(inputs["L0_ffn_sw2"], np.float32),
        "eye_in": eye,
    }
    in_maps = []
    for core in range(8):
        b, half = core // 2, core % 2
        xm = np.ascontiguousarray(x[b, half * T:(half + 1) * T, :], np.float32)
        if half == 0:
            xh = np.zeros((3, D), np.float32)
        else:
            xh = np.ascontiguousarray(x[b, half * T - 3:half * T, :], np.float32)
        masks = np.zeros((128, 2), np.float32)
        masks[:, 0] = 1.0 - half
        masks[:, 1] = float(half)
        m = dict(base)
        m.update({"x_main": xm, "x_halo": xh, "masks_in": masks})
        in_maps.append(m)
    res = run_bass_kernel_spmd(nc, in_maps, core_ids=list(range(8)))
    out = np.empty((4, 4096, D), np.float32)
    for core in range(8):
        b, half = core // 2, core % 2
        out[b, half * T:(half + 1) * T, :] = res.results[core]["out"]
    return out
